# version 07
# baseline (speedup 1.0000x reference)
import contextlib
import numpy as np

import concourse.bass as bass
import concourse.mybir as mybir
from concourse.bass_utils import run_bass_kernel_spmd

F32 = mybir.dt.float32
BF16 = mybir.dt.bfloat16
AF = mybir.ActivationFunctionType
ALU = mybir.AluOpType
AXX = mybir.AxisListType.X

P = 128
T = 4096
D = 1024
DI = 2048
DS = 16
E = 8
CH = 8
TT = 512
KD = D // P
KI = DI // P
SBLK = 514
SG = 2
NSG = DS // SG
SCF = SG * SBLK

ENGS = ("tensor", "vector", "scalar", "gpsimd", "sync")
SEMS = ENGS + ("dmaS", "dmaA")

BLK = [
    ("eye", 128),
    ("r16", 2048),
    ("r8", 1024),
    ("be", 1024),
    ("win", 64 * 512),
    ("conv", 64 * 128),
    ("wx", 16 * 16),
    ("wdt", 256 * 128),
    ("wout", 64 * 256),
    ("wg", 8 * 8),
    ("we", 64 * 1024),
]
BOFF = {}
_o = 0
for _n, _c in BLK:
    BOFF[_n] = _o
    _o += _c
FT = _o


class Buf:
    __slots__ = ("w", "rs")

    def __init__(self):
        self.w = None
        self.rs = []

    def r(self):
        return [self.w] if self.w is not None else []

    def wkill(self):
        out = list(self.rs)
        if self.w is not None:
            out.append(self.w)
        return out


class Prog:
    def __init__(self):
        self.ops = []
        self.cnt = {s: 0 for s in SEMS}
        self.maxw = {}

    def emit(self, eng, fn, waits=(), inc_sem=None, inc=1, tag="", hard=()):
        if inc_sem is None:
            inc_sem = eng
        seen = {}
        for (s, v) in waits:
            if s == eng and s in ENGS:
                continue
            if v > seen.get(s, -1):
                seen[s] = v
        for (s, v) in hard:
            if v > seen.get(s, -1):
                seen[s] = v
        ws = []
        for s, v in seen.items():
            k = (eng, s)
            if self.maxw.get(k, -1) >= v:
                continue
            self.maxw[k] = v
            ws.append((s, v))
        tok = None
        if inc:
            self.cnt[inc_sem] += inc
            tok = (inc_sem, self.cnt[inc_sem])
        self.ops.append((eng, fn, ws, inc_sem, inc, tag))
        return tok


class KB:

    def __init__(self, nc, stack, taps=(), n_chunks=CH):
        self.nc = nc
        self.n_chunks = n_chunks
        self.st = stack
        self.pr = Prog()
        self.taps_req = set(taps)
        self.taps = {}
        self.d_x = nc.dram_tensor("x_in", [P, KD, T], F32, kind="ExternalInput").ap()
        self.d_wf = nc.dram_tensor("wf32", [P, FT], F32, kind="ExternalInput").ap()
        self.d_w16 = nc.dram_tensor("w16", [P, FT], BF16, kind="Internal").ap()
        self.d_out = nc.dram_tensor("out", [P, KD, T], F32, kind="ExternalOutput").ap()
        bias_specs = [
            ("binr", [P, 32]), ("bdtr", [P, 16]), ("boutr", [P, 8]),
            ("dr", [P, 16]), ("convbr", [P, 16]), ("alogr", [P, 256]),
            ("bx", [P, 1]), ("bgrow", [1, 8]),
        ]
        self.d_bias = {n: nc.dram_tensor("bias_" + n, s, F32, kind="ExternalInput").ap()
                       for n, s in bias_specs}
        self.sems = {}
        self.alloc()

    def sb(self, name, shape, dt):
        t = self.st.enter_context(self.nc.sbuf_tensor(name, shape, dt))
        return t

    def alloc(self):
        sb = self.sb
        st = self.st
        self.c_binr = sb("c_binr", [P, 32], F32)
        self.c_bdtr = sb("c_bdtr", [P, 16], F32)
        self.c_boutr = sb("c_boutr", [P, 8], F32)
        self.c_dr = sb("c_dr", [P, 16], F32)
        self.c_convbr = sb("c_convbr", [P, 16], F32)
        self.c_A = sb("c_A", [P, 256], F32)
        self.c_bx = sb("c_bx", [P, 1], F32)
        self.c_bgrow = sb("c_bgrow", [1, 8], F32)
        self.c_bgrow16 = sb("c_bgrow16", [1, 8], BF16)
        self.c_bgb = sb("c_bgb", [P, 8], F32)
        self.c_eye16 = sb("c_eye16", [P, P], BF16)
        self.c_r16 = sb("c_r16", [16, 2048], BF16)
        self.c_be16 = sb("c_be16", [8, 1024], BF16)
        self.c_ones1 = sb("c_ones1", [1, P], BF16)
        self.c_onesc = sb("c_onesc", [P, 1], BF16)
        self.xch = sb("xch", [P, KD, TT], F32)
        self.scr16 = sb("scr16", [P, KD, TT], BF16)
        self.xn1 = sb("xn1", [P, KD, TT], BF16)
        self.xn2 = sb("xn2", [P, KD, TT], BF16)
        self.u16 = [sb(f"u16_{i}", [P, KI, TT], BF16) for i in range(2)]
        self.sz16 = sb("sz16", [P, KI, TT], BF16)
        self.ball = sb("ball", [P, DS, TT], BF16)
        self.yrhs = sb("yrhs", [P, KI, TT], BF16)
        self.xcv = [sb(f"xcv{i}", [P, 516], BF16) for i in range(2)]
        self.tails = sb("tails", [P, 16, 4], BF16)
        self.dA = [sb(f"dA{i}", [P, SCF], F32) for i in range(2)]
        self.dBu = [sb(f"dBu{i}", [P, SCF], BF16) for i in range(2)]
        self.hfull = sb("hfull", [P, DS, SBLK], BF16)
        self.carry = sb("carry", [P, 256], BF16)
        self.sp1 = sb("sp1", [P, TT], F32)
        self.sp2 = sb("sp2", [P, TT], F32)
        self.sp3 = sb("sp3", [P, TT], F32)
        self.dful = sb("dful", [P, 8, TT], BF16)
        self.v16 = [sb(f"v16_{i}", [P, TT], BF16) for i in range(2)]
        self.ytmp = sb("ytmp", [P, TT], BF16)
        self.st_a = sb("st_a", [1, TT], F32)[:]
        self.st_b = sb("st_b", [1, TT], F32)[:]
        self.st_c = sb("st_c", [1, TT], F32)[:]
        self.mrow16 = sb("mrow16", [1, TT], BF16)
        self.rrow16 = sb("rrow16", [1, TT], BF16)
        self.mb16 = sb("mb16", [P, TT], BF16)
        self.rb16 = sb("rb16", [P, TT], BF16)
        self.gtm = sb("gtm", [P, 4, 8], F32)
        self.ge = sb("ge", [P, 8], F32)
        self.gm = sb("gm", [P, 8], F32)
        self.gs = sb("gs", [P, 8], F32)
        self.wtm16 = sb("wtm16", [P, 4, 8], BF16)
        self.sbW8 = sb("sbW8", [8, TT], BF16)
        self.sbB16 = sb("sbB16", [16, TT], BF16)
        self.streams = {}
        for name, ppc, pc, bp, rb in [
            ("win", 64, 512, 2, 2),
            ("conv", 64, 128, 4, 2),
            ("wx", 16, 16, 16, 1),
            ("wdt", 256, 128, 8, 2),
            ("wout", 64, 256, 4, 2),
            ("wg", 8, 8, 8, 1),
            ("we", 64, 1024, 1, 2),
        ]:
            arena = sb("ar_" + name, [P, rb * bp * pc], BF16)
            self.streams[name] = dict(
                name=name, ppc=ppc, pc=pc, bp=bp, rb=rb, arena=arena,
                slots=[Buf() for _ in range(rb)], gb=0, base=0,
                cast_tok=None)
        self.pbank = [self.st.enter_context(
            self.nc.psum_tensor(f"pb{i}", [P, 512], F32)) for i in range(8)]
        self.bankbuf = [Buf() for _ in range(8)]
        for s in SEMS:
            self.sems[s] = st.enter_context(self.nc.semaphore("sem_" + s))
        self.b_xch = Buf()
        self.b_scr16 = Buf()
        self.b_xn1 = Buf()
        self.b_xn2 = Buf()
        self.b_u = [Buf(), Buf()]
        self.b_sz = [Buf() for _ in range(KI)]
        self.b_ball = Buf()
        self.b_yrhs = Buf()
        self.b_xcv = [Buf(), Buf()]
        self.b_tails = Buf()
        self.b_dA = [Buf(), Buf()]
        self.b_dBu = [Buf(), Buf()]
        self.b_hfull = Buf()
        self.b_carry = Buf()
        self.b_sp = Buf()
        self.b_dful = [Buf() for _ in range(8)]
        self.b_v = [Buf(), Buf()]
        self.b_ytmp = Buf()
        self.b_ln = Buf()
        self.b_bcast = Buf()
        self.b_gate = Buf()
        self.b_sbB = Buf()
        self.b_sbW8 = Buf()
        self.b_wb = Buf()

    def op(self, eng, fn, reads=(), writes=(), waits=(), inc_sem=None, inc=1,
           tag="", hard=()):
        ws = list(waits)
        bc = getattr(self, "b_consts", None)
        if bc is not None and bc.w is not None:
            ws += bc.r()
        for b in reads:
            ws += b.r()
        for b in writes:
            ws += b.wkill()
        t = self.pr.emit(eng, fn, ws, inc_sem, inc, tag, hard=hard)
        if t is not None:
            for b in reads:
                b.rs.append(t)
            for b in writes:
                b.w = t
                b.rs = []
        return t

    def dma(self, queue_eng, sem, out_ap, in_ap, reads=(), writes=(),
            waits=(), tag=""):
        fn = (lambda e, o=out_ap, i=in_ap: e.dma_start(out=o, in_=i))
        return self.op(queue_eng, fn, reads=reads, writes=writes, waits=waits,
                       inc_sem=sem, inc=16, tag=tag)

    def mm_seq(self, items, bank_bufs, tag=""):
        n = len(items)
        banks = sorted({b for _, _, b in items})
        tok = None
        first_by_bank = {b: True for b in banks}
        bc = getattr(self, "b_consts", None)
        for i, (fn, rbs, bi) in enumerate(items):
            ws = []
            if i == 0 and bc is not None and bc.w is not None:
                ws += bc.r()
            for b in rbs:
                ws += b.r()
            if first_by_bank.get(bi):
                ws += bank_bufs[bi].wkill()
                first_by_bank[bi] = False
            last = i == n - 1
            t = self.pr.emit("tensor", fn, ws, None, 1 if last else 0, tag)
            if last:
                tok = t
        for _, rbs, _bi in items:
            for b in rbs:
                if tok not in b.rs:
                    b.rs.append(tok)
        for b in banks:
            bank_bufs[b].w = tok
            bank_bufs[b].rs = []
        return tok

    def tap(self, name, ap):
        if name not in self.taps_req:
            return
        shape = list(ap.shape)
        d = self.nc.dram_tensor("tap_" + name, shape, ap.dtype,
                                kind="ExternalOutput").ap()
        self.taps[name] = name
        waits = [(s, self.pr.cnt[s]) for s in ("tensor", "vector", "scalar",
                                               "gpsimd")]
        self.dma("scalar", "dmaA", d, ap, waits=waits, tag="tap_" + name)

    def set_stream_chunk(self, name, c):
        st = self.streams[name]
        st["chunk_base"] = c * (st["ppc"] // st["bp"])

    def _ensure(self, name, upto):
        s = self.streams[name]
        nb = s["ppc"] // s["bp"]
        cols = s["bp"] * s["pc"]
        while s["gb"] < upto:
            b = s["gb"]
            slot = b % s["rb"]
            src = BOFF[name] + (b % nb) * cols
            waits = []
            if b < nb and s.get("cast_tok"):
                waits.append(s["cast_tok"])
            self.dma("sync", "dmaS",
                     s["arena"][:, slot * cols:(slot + 1) * cols],
                     self.d_w16[:, src:src + cols],
                     writes=[s["slots"][slot]], waits=waits,
                     tag=f"w_{name}_{b}")
            s["gb"] += 1

    def piece(self, name, j):
        s = self.streams[name]
        gb = s["chunk_base"] + j // s["bp"]
        self._ensure(name, gb + 1)
        slot = gb % s["rb"]
        cols = s["bp"] * s["pc"]
        off = slot * cols + (j % s["bp"]) * s["pc"]
        return s["arena"][:, off:off + s["pc"]], s["slots"][slot]

    def prologue(self):
        nc = self.nc
        self.b_consts = Buf()
        BC = self.b_consts

        for n, dst in [("binr", self.c_binr), ("bdtr", self.c_bdtr),
                       ("boutr", self.c_boutr), ("dr", self.c_dr),
                       ("convbr", self.c_convbr), ("alogr", self.c_A),
                       ("bx", self.c_bx), ("bgrow", self.c_bgrow)]:
            self.dma("scalar", "dmaA", dst[:], self.d_bias[n],
                     writes=[BC], tag="bias_" + n)
        self.op("scalar", lambda e: e.activation(
            out=self.c_A[:], in_=self.c_A[:], func=AF.Exp),
            writes=[BC], tag="A_exp")
        self.op("vector", lambda e: e.tensor_scalar(
            out=self.c_A[:], in0=self.c_A[:], scalar1=-1.0, scalar2=None,
            op0=ALU.mult), writes=[BC], tag="A_neg")
        self.op("vector", lambda e: e.memset(self.c_ones1[:], 1.0),
                writes=[BC])
        self.op("vector", lambda e: e.memset(self.c_onesc[:], 1.0),
                writes=[BC])
        self.op("vector", lambda e: e.tensor_copy(
            out=self.c_bgrow16[:], in_=self.c_bgrow[:]), writes=[BC])
        self.mm_seq([(lambda e: e.matmul(
            out=self.pbank[7][:, 0:8], lhsT=self.c_ones1[:],
            rhs=self.c_bgrow16[:], start=True, stop=True), [BC], 7)],
            self.bankbuf, tag="bgb_mm")
        self.op("vector", lambda e: e.tensor_copy(
            out=self.c_bgb[:], in_=self.pbank[7][:, 0:8]),
            reads=[self.bankbuf[7]], writes=[BC], tag="bgb_ev")
        for par in range(2):
            da3 = self.dA[par][:].rearrange("p (s c) -> p s c", s=SG)
            db3 = self.dBu[par][:].rearrange("p (s c) -> p s c", s=SG)
            self.op("vector", lambda e, a=da3: e.memset(a[:, :, 0:1], 0.0),
                    writes=[self.b_dA[par]])
            self.op("vector", lambda e, a=da3: e.memset(a[:, :, 1:2], 1.0),
                    writes=[self.b_dA[par]])
            self.op("vector", lambda e, a=db3: e.memset(a[:, :, 1:2], 0.0),
                    writes=[self.b_dBu[par]])
        self.op("vector", lambda e: e.memset(self.carry[:], 0.0),
                writes=[self.b_carry])
        self.op("gpsimd", lambda e: e.memset(self.tails[:], 0.0),
                writes=[self.b_tails])
        fin_t = [self.streams["win"]["arena"][:].bitcast(F32),
                 self.streams["wdt"]["arena"][:].bitcast(F32)]
        fout = self.streams["we"]["arena"][:]
        b_in = [Buf(), Buf()]
        b_out = [Buf(), Buf()]
        pcw = 1024
        n_pieces = (FT + pcw - 1) // pcw
        blk_tok = {}
        blk_end = {n: BOFF[n] + c for n, c in BLK}
        for i in range(n_pieces):
            lo = i * pcw
            hi = min(FT, lo + pcw)
            w = hi - lo
            s_in = fin_t[i % 2][:, 0:w]
            s_out = fout[:, (i % 2) * pcw:(i % 2) * pcw + w]
            self.dma("sync", "dmaS", s_in, self.d_wf[:, lo:hi],
                     writes=[b_in[i % 2]], tag=f"cast_in{i}")
            self.op("vector", lambda e, o=s_out, s=s_in: e.tensor_copy(
                out=o, in_=s), reads=[b_in[i % 2]], writes=[b_out[i % 2]],
                tag=f"cast{i}")
            t = self.dma("sync", "dmaS", self.d_w16[:, lo:hi], s_out,
                         reads=[b_out[i % 2]], tag=f"cast_out{i}")
            for n, endc in blk_end.items():
                if lo < endc <= hi:
                    blk_tok[n] = t
        self.blk_tok = blk_tok
        for name in self.streams:
            self.streams[name]["cast_tok"] = blk_tok[name]
        last_in = b_in[0].wkill() + b_in[1].wkill()
        last_out = b_out[0].wkill() + b_out[1].wkill()
        for s, toks in (("win", last_in), ("wdt", last_in),
                        ("we", last_out)):
            for sl in self.streams[s]["slots"]:
                sl.rs = list(toks)
        self.dma("sync", "dmaS", self.c_eye16[:],
                 self.d_w16[:, BOFF["eye"]:BOFF["eye"] + 128],
                 waits=[blk_tok["eye"]], writes=[self.b_consts],
                 tag="ld_eye")
        self.dma("sync", "dmaS", self.c_r16[:],
                 self.d_w16[0:16, BOFF["r16"]:BOFF["r16"] + 2048],
                 waits=[blk_tok["r16"]], writes=[self.b_consts],
                 tag="ld_r16")
        self.dma("sync", "dmaS", self.c_be16[:],
                 self.d_w16[0:8, BOFF["be"]:BOFF["be"] + 1024],
                 waits=[blk_tok["be"]], writes=[self.b_consts],
                 tag="ld_be")

    def gen_ln(self, src_f32_3d, src_buf, dst16, b_dst, xsq, b_xsq,
               bk0, bk1, tag):
        op = self.op
        yield 0
        op("scalar", lambda e: e.activation(
            out=self.scr16[:], in_=src_f32_3d, func=AF.Identity),
            reads=[src_buf], writes=[self.b_scr16], tag=tag + "_cast")
        yield 0
        op("vector", lambda e, o=xsq: e.tensor_tensor(
            out=o, in0=self.scr16[:], in1=self.scr16[:], op=ALU.mult),
            reads=[self.b_scr16], writes=[b_xsq], tag=tag + "_sq")
        yield 0
        items = []
        for k in range(KD):
            items.append((lambda e, k=k: e.matmul(
                out=self.pbank[bk0][0:1, :], lhsT=self.c_onesc[:],
                rhs=self.scr16[:, k, :], start=(k == 0), stop=(k == KD - 1)),
                [self.b_scr16], bk0))
        for k in range(KD):
            items.append((lambda e, k=k, x=xsq: e.matmul(
                out=self.pbank[bk1][0:1, :], lhsT=self.c_onesc[:],
                rhs=x[:, k, :], start=(k == 0), stop=(k == KD - 1)),
                [b_xsq], bk1))
        self.mm_seq(items, self.bankbuf, tag=tag + "_stats")
        yield 0
        op("vector", lambda e: e.tensor_scalar(
            out=self.st_a, in0=self.pbank[bk0][0:1, :], scalar1=1.0 / D,
            scalar2=None, op0=ALU.mult), reads=[self.bankbuf[bk0]],
            writes=[self.b_ln], tag=tag + "_mean")
        op("vector", lambda e: e.tensor_scalar(
            out=self.st_b, in0=self.pbank[bk1][0:1, :], scalar1=1.0 / D,
            scalar2=None, op0=ALU.mult), reads=[self.bankbuf[bk1]],
            writes=[self.b_ln], tag=tag + "_msq")
        op("vector", lambda e: e.tensor_tensor(
            out=self.st_c, in0=self.st_a, in1=self.st_a,
            op=ALU.mult), writes=[self.b_ln], tag=tag + "_m2")
        op("vector", lambda e: e.tensor_tensor(
            out=self.st_b, in0=self.st_b, in1=self.st_c,
            op=ALU.subtract), writes=[self.b_ln], tag=tag + "_var")
        yield 0
        op("scalar", lambda e: e.activation(
            out=self.st_c, in_=self.st_b, func=AF.Ln, bias=np.float32(1e-5)),
            writes=[self.b_ln], tag=tag + "_lnv")
        op("scalar", lambda e: e.activation(
            out=self.st_b, in_=self.st_c, func=AF.Exp, scale=np.float32(-0.5)),
            writes=[self.b_ln], tag=tag + "_rstd")
        op("vector", lambda e: e.tensor_copy(out=self.mrow16[:],
                                             in_=self.st_a),
           writes=[self.b_ln], tag=tag + "_m16")
        op("vector", lambda e: e.tensor_copy(out=self.rrow16[:],
                                             in_=self.st_b),
           writes=[self.b_ln], tag=tag + "_r16c")
        yield 0
        items = [
            (lambda e: e.matmul(out=self.pbank[bk0][:, :],
                                lhsT=self.c_ones1[:],
                                rhs=self.mrow16[:], start=True, stop=True),
             [self.b_ln], bk0),
            (lambda e: e.matmul(out=self.pbank[bk1][:, :],
                                lhsT=self.c_ones1[:],
                                rhs=self.rrow16[:], start=True, stop=True),
             [self.b_ln], bk1),
        ]
        self.mm_seq(items, self.bankbuf, tag=tag + "_bc")
        op("scalar", lambda e: e.activation(
            out=self.mb16[:], in_=self.pbank[bk0][:, :], func=AF.Identity),
            reads=[self.bankbuf[bk0]], writes=[self.b_bcast],
            tag=tag + "_mbev")
        op("scalar", lambda e: e.activation(
            out=self.rb16[:], in_=self.pbank[bk1][:, :], func=AF.Identity),
            reads=[self.bankbuf[bk1]], writes=[self.b_bcast],
            tag=tag + "_rbev")
        yield 0
        for k in range(KD):
            op("vector", lambda e, k=k: e.tensor_tensor(
                out=dst16[:, k, :],
                in0=self.scr16[:, k, :], in1=self.mb16[:], op=ALU.subtract),
                reads=[self.b_scr16, self.b_bcast], writes=[b_dst],
                tag=tag + f"_sub{k}")
            op("vector", lambda e, k=k: e.tensor_tensor(
                out=dst16[:, k, :], in0=dst16[:, k, :], in1=self.rb16[:],
                op=ALU.mult), reads=[self.b_bcast], writes=[b_dst],
                tag=tag + f"_mul{k}")
            if k % 3 == 2:
                yield 0

    def gen_front(self, c):
        if c >= self.n_chunks:
            return
        op = self.op
        for nm in ("win", "conv", "wx"):
            self.set_stream_chunk(nm, c)
        upar = c % 2
        u16 = self.u16[upar]
        b_u = self.b_u[upar]
        yield 40
        self.dma("scalar", "dmaA", self.xch[:],
                 self.d_x[:, :, c * TT:(c + 1) * TT],
                 writes=[self.b_xch], tag=f"x{c}")
        for ms in self.gen_ln(self.xch[:], self.b_xch, self.xn1, self.b_xn1,
                              self.xn2[:], self.b_xn2, 6, 7, f"ln1_{c}"):
            yield max(ms, 42)
        if c == 0:
            self.tap("xn1", self.xn1[:])
        WSEQ = [0, 4, 1, 5, 2, 6, 3, 7]
        for oi, mg in enumerate(WSEQ):
            is_z = mg >= 4
            yield 46
            for kb in range(4):
                items = []
                for k in (2 * kb, 2 * kb + 1):
                    pap, pslot = self.piece("win", oi * 8 + k)
                    for m in range(4):
                        items.append((lambda e, pap=pap, m=m, k=k: e.matmul(
                            out=self.pbank[m][:, :],
                            lhsT=pap[:, m * P:(m + 1) * P],
                            rhs=self.xn1[:, k, :],
                            start=(k == 0), stop=(k == KD - 1)),
                            [pslot, self.b_xn1], m))
                self.mm_seq(items, self.bankbuf, tag=f"win{c}_{mg}_{kb}")
            for m in range(4):
                j = mg * 4 + m
                if j < KI:
                    g = j
                    par = g % 2
                    yield 46
                    op("vector", lambda e, g=g, par=par: e.tensor_copy(
                        out=self.xcv[par][:, 0:3], in_=self.tails[:, g, 0:3]),
                        reads=[self.b_tails], writes=[self.b_xcv[par]],
                        tag=f"tin{c}_{g}")
                    op("scalar", lambda e, m=m, g=g, par=par: e.activation(
                        out=self.xcv[par][:, 3:515], in_=self.pbank[m][:, :],
                        func=AF.Identity, bias=self.c_binr[:, g:g + 1]),
                        reads=[self.bankbuf[m]], writes=[self.b_xcv[par]],
                        tag=f"xhev{c}_{g}")
                    citems = []
                    for kk in range(4):
                        pap, pslot = self.piece("conv", g * 4 + kk)
                        citems.append((lambda e, pap=pap, kk=kk, par=par, m=m:
                                       e.matmul(
                                           out=self.pbank[m][:, :], lhsT=pap,
                                           rhs=self.xcv[par][:, kk:kk + 512],
                                           start=(kk == 0), stop=(kk == 3)),
                                       [pslot, self.b_xcv[par]], m))
                    self.mm_seq(citems, self.bankbuf, tag=f"conv{c}_{g}")
                    op("vector", lambda e, g=g, par=par: e.tensor_copy(
                        out=self.tails[:, g, 0:3],
                        in_=self.xcv[par][:, 512:515]),
                        reads=[self.b_xcv[par]], writes=[self.b_tails],
                        tag=f"tout{c}_{g}")
                    op("scalar", lambda e, g=g, m=m: e.activation(
                        out=u16[:, g, :], in_=self.pbank[m][:, :],
                        func=AF.Silu, bias=self.c_convbr[:, g:g + 1]),
                        reads=[self.bankbuf[m]], writes=[b_u],
                        tag=f"uev{c}_{g}")
                else:
                    zi = j - KI
                    yield 8 * zi + 10
                    op("scalar", lambda e, m=m, zi=zi, j=j: e.activation(
                        out=self.sz16[:, zi, :], in_=self.pbank[m][:, :],
                        func=AF.Silu, bias=self.c_binr[:, j:j + 1]),
                        reads=[self.bankbuf[m]], writes=[self.b_sz[zi]],
                        tag=f"szev{c}_{zi}")
        if c == 0:
            self.tap("u", u16[:])
            self.tap("sz", self.sz16[:])
        yield 80
        items = []
        for k in range(KI):
            pap, pslot = self.piece("wx", k)
            items.append((lambda e, pap=pap, k=k: e.matmul(
                out=self.pbank[7][0:16, :], lhsT=pap, rhs=u16[:, k, :],
                start=(k == 0), stop=(k == KI - 1)), [pslot, b_u], 7))
        self.mm_seq(items, self.bankbuf, tag=f"wxB{c}")
        op("vector", lambda e: e.tensor_scalar(
            out=self.sbB16[:], in0=self.pbank[7][0:16, :],
            scalar1=self.c_bx[0:16, :], scalar2=None, op0=ALU.add),
            reads=[self.bankbuf[7]], writes=[self.b_sbB], tag=f"Bev{c}")
        if c == 0:
            self.tap("B", self.sbB16[:])

    def emit_ball(self, c):
        if c >= self.n_chunks:
            return
        op = self.op
        for si in range(DS):
            bb = 6 + (si % 2)
            self.mm_seq([(lambda e, si=si, bb=bb: e.matmul(
                out=self.pbank[bb][:, :],
                lhsT=self.c_r16[:, si * P:(si + 1) * P], rhs=self.sbB16[:],
                start=True, stop=True),
                [self.b_sbB], bb)],
                self.bankbuf, tag=f"bb{c}_{si}")
            op("scalar", lambda e, si=si, bb=bb: e.activation(
                out=self.ball[:, si, :], in_=self.pbank[bb][:, :],
                func=AF.Identity), reads=[self.bankbuf[bb]],
                writes=[self.b_ball], tag=f"bbev{c}_{si}")

    def gen_wdt(self, c):
        op = self.op
        self.set_stream_chunk("wdt", c)
        upar = c % 2
        u16 = self.u16[upar]
        b_u = self.b_u[upar]
        wsl = [0, 0, 1, 1, 2, 2, 41, 44, 47, 50, 53, 56, 59, 62, 65, 68]
        for g in range(KI):
            yield wsl[g]
            bank = 6 if g == 0 else (7 if g == 1 else 4 + (g % 2))
            items = []
            for k in range(KI):
                pap, pslot = self.piece("wdt", g * 16 + k)
                items.append((lambda e, pap=pap, k=k, bank=bank: e.matmul(
                    out=self.pbank[bank][:, :], lhsT=pap,
                    rhs=u16[:, k, :], start=(k == 0),
                    stop=(k == KI - 1)), [pslot, b_u], bank))
            self.mm_seq(items, self.bankbuf, tag=f"wdt{c}_{g}")
            op("scalar", lambda e, g=g, bank=bank: e.activation(
                out=self.sp1[:], in_=self.pbank[bank][:, :], func=AF.Abs,
                bias=self.c_bdtr[:, g:g + 1]), reads=[self.bankbuf[bank]],
                writes=[self.b_sp], tag=f"spabs{c}_{g}")
            op("scalar", lambda e, g=g, bank=bank: e.activation(
                out=self.sp2[:], in_=self.pbank[bank][:, :], func=AF.Relu,
                bias=self.c_bdtr[:, g:g + 1]), reads=[self.bankbuf[bank]],
                writes=[self.b_sp], tag=f"sprelu{c}_{g}")
            op("scalar", lambda e: e.activation(
                out=self.sp3[:], in_=self.sp1[:], func=AF.Exp,
                scale=np.float32(-1.0)),
                writes=[self.b_sp], tag=f"spexp{c}_{g}")
            op("scalar", lambda e: e.activation(
                out=self.sp1[:], in_=self.sp3[:], func=AF.Ln,
                bias=np.float32(1.0)),
                writes=[self.b_sp], tag=f"spln{c}_{g}")
            ds = g % 8
            op("vector", lambda e, ds=ds: e.tensor_tensor(
                out=self.dful[:, ds, :], in0=self.sp2[:], in1=self.sp1[:],
                op=ALU.add), reads=[self.b_sp], writes=[self.b_dful[ds]],
                tag=f"delta{c}_{g}")
            if c == 0 and g == 0:
                self.tap("delta_g0", self.dful[:, 0, :])

    def gen_tail(self, c):
        if c < 0:
            return
        op = self.op
        for nm in ("wout", "wg", "we"):
            self.set_stream_chunk(nm, c)
        wbu = self.xn1
        b_wb = self.b_xn1
        yield 0
        self.dma("scalar", "dmaA", self.xch[:],
                 self.d_x[:, :, c * TT:(c + 1) * TT],
                 writes=[self.b_xch], tag=f"xr{c}")
        for mp in range(4):
            b0 = (mp % 2) * 2
            yield 0
            for kq in range(4):
                items = []
                for k in range(4 * kq, 4 * kq + 4):
                    pap, pslot = self.piece("wout", mp * 16 + k)
                    for mm_i in range(2):
                        items.append((lambda e, pap=pap, mm_i=mm_i, k=k,
                                      b0=b0: e.matmul(
                            out=self.pbank[b0 + mm_i][:, :],
                            lhsT=pap[:, mm_i * P:(mm_i + 1) * P],
                            rhs=self.yrhs[:, k, :], start=(k == 0),
                            stop=(k == KI - 1)), [pslot, self.b_yrhs],
                            b0 + mm_i))
                self.mm_seq(items, self.bankbuf, tag=f"wout{c}_{mp}_{kq}")
            for mm_i in range(2):
                m = mp * 2 + mm_i
                bb_ = b0 + mm_i
                op("vector", lambda e, m=m, bb_=bb_:
                   e.scalar_tensor_tensor(
                    out=self.xch[:, m, :], in0=self.pbank[bb_][:, :],
                    scalar=self.c_boutr[:, m:m + 1], in1=self.xch[:, m, :],
                    op0=ALU.add, op1=ALU.add),
                    reads=[self.bankbuf[bb_]], writes=[self.b_xch],
                    tag=f"res1{c}_{m}")
        if c == 0:
            self.tap("resid1", self.xch[:])
        for ms in self.gen_ln(self.xch[:], self.b_xch, self.xn2, self.b_xn2,
                              self.xn1[:], self.b_xn1, 0, 1, f"ln2_{c}"):
            yield ms
        if c == 0:
            self.tap("xn2", self.xn2[:])
        yield 0
        items = []
        for j in range(4):
            for k in range(KD):
                pap, pslot = self.piece("wg", k)
                items.append((lambda e, pap=pap, j=j, k=k: e.matmul(
                    out=self.pbank[2][:, j * 8:(j + 1) * 8],
                    lhsT=self.xn2[:, k, j * P:(j + 1) * P], rhs=pap,
                    start=(k == 0), stop=(k == KD - 1)),
                    [pslot, self.b_xn2], 2))
        self.mm_seq(items, self.bankbuf, tag=f"glog{c}")
        for j in range(4):
            op("vector", lambda e, j=j: e.tensor_tensor(
                out=self.gtm[:, j, :],
                in0=self.pbank[2][:, j * 8:(j + 1) * 8],
                in1=self.c_bgb[:], op=ALU.add),
                reads=[self.bankbuf[2]], writes=[self.b_gate],
                tag=f"gev{c}_{j}")
        if c == 0:
            self.tap("glog", self.gtm[:])
        yield 0
        for j in range(4):
            gj = self.gtm[:, j, :]
            pv = op("vector", lambda e, gj=gj: e.reduce_max(
                out=self.gs[:, 0:1], in_=gj, axis=AXX),
                reads=[self.b_gate], writes=[self.b_gate], tag=f"m1_{c}{j}")
            pv = op("vector", lambda e: e.tensor_scalar(
                out=self.gs[:, 1:2], in0=self.gs[:, 0:1], scalar1=-1.0,
                scalar2=None, op0=ALU.mult), writes=[self.b_gate], hard=[pv])
            pv = op("scalar", lambda e, gj=gj: e.activation(
                out=self.ge[:], in_=gj, func=AF.Exp,
                bias=self.gs[:, 1:2]),
                writes=[self.b_gate], hard=[pv], tag=f"gex{c}{j}")
            pv = op("vector", lambda e: e.reduce_sum(
                out=self.gs[:, 2:3], in_=self.ge[:], axis=AXX),
                reads=[self.b_gate], writes=[self.b_gate], hard=[pv])
            t_rd = pv = op("vector", lambda e: e.reciprocal(
                out=self.gs[:, 3:4], in_=self.gs[:, 2:3]),
                writes=[self.b_gate], hard=[pv])
            pv = op("vector", lambda e, gj=gj: e.tensor_scalar(
                out=self.gm[:], in0=gj, scalar1=self.gs[:, 0:1], scalar2=None,
                op0=ALU.is_ge), writes=[self.b_gate], hard=[pv])
            pv = op("vector", lambda e, gj=gj: e.scalar_tensor_tensor(
                out=self.gm[:], in0=self.gm[:], scalar=-1e30, in1=gj,
                op0=ALU.mult, op1=ALU.add), writes=[self.b_gate], hard=[pv])
            pv = op("vector", lambda e: e.reduce_max(
                out=self.gs[:, 4:5], in_=self.gm[:], axis=AXX),
                writes=[self.b_gate], hard=[pv])
            pv = op("vector", lambda e, gj=gj: e.tensor_scalar(
                out=self.gm[:], in0=gj, scalar1=self.gs[:, 4:5], scalar2=None,
                op0=ALU.is_ge), writes=[self.b_gate], hard=[pv])
            pv = op("vector", lambda e: e.tensor_tensor(
                out=self.ge[:], in0=self.ge[:], in1=self.gm[:], op=ALU.mult),
                writes=[self.b_gate], hard=[pv])
            op("vector", lambda e, j=j: e.tensor_scalar(
                out=self.wtm16[:, j, :], in0=self.ge[:],
                scalar1=self.gs[:, 3:4], scalar2=None, op0=ALU.mult),
                reads=[self.b_gate], writes=[self.b_gate], hard=[pv, t_rd],
                tag=f"wtm{c}{j}")
            yield 0
        if c == 0:
            self.tap("wtm", self.wtm16[:])
        items = []
        for j in range(4):
            items.append((lambda e, j=j: e.transpose(
                out=self.pbank[3][:].bitcast(BF16)[0:8, j * P:(j + 1) * P],
                in_=self.wtm16[:, j, :], identity=self.c_eye16[:]),
                [self.b_gate], 3))
        self.mm_seq(items, self.bankbuf, tag=f"wtr{c}")
        op("vector", lambda e: e.tensor_copy(
            out=self.sbW8[:], in_=self.pbank[3][:].bitcast(BF16)[0:8, 0:512]),
            reads=[self.bankbuf[3]], writes=[self.b_sbW8], tag=f"w8{c}")
        yield 0
        for e_i in range(E):
            bb = (e_i % 2)
            self.mm_seq([(lambda e, e_i=e_i, bb=bb: e.matmul(
                out=self.pbank[bb][:, :],
                lhsT=self.c_r16[0:8, e_i * P:(e_i + 1) * P],
                rhs=self.sbW8[:], start=True, stop=True),
                [self.b_sbW8], bb)], self.bankbuf, tag=f"wb{c}_{e_i}")
            op("scalar", lambda e, e_i=e_i, bb=bb: e.activation(
                out=wbu[:, e_i, :], in_=self.pbank[bb][:, :],
                func=AF.Identity), reads=[self.bankbuf[bb]],
                writes=[b_wb], tag=f"wbev{c}_{e_i}")
        for e_i in range(E):
            yield 4
            op("vector", lambda e, e_i=e_i: e.tensor_tensor(
                out=self.scr16[:],
                in0=self.xn2[:],
                in1=wbu[:, e_i, :].unsqueeze(1)
                .broadcast_to([P, KD, TT]),
                op=ALU.mult), reads=[self.b_xn2, b_wb],
                writes=[self.b_scr16], tag=f"xe{c}_{e_i}")
            for k in range(KD):
                pap, pslot = self.piece("we", e_i * 8 + k)
                items = []
                for o in range(KD):
                    items.append((lambda e, pap=pap, o=o, k=k, e_i=e_i:
                                  e.matmul(
                                      out=self.pbank[o][:, :],
                                      lhsT=pap[:, o * P:(o + 1) * P],
                                      rhs=self.scr16[:, k, :],
                                      start=(e_i == 0 and k == 0),
                                      stop=False),
                                  [pslot, self.b_scr16], o))
                self.mm_seq(items, self.bankbuf, tag=f"we{c}_{e_i}_{k}")
        items = []
        for o in range(KD):
            items.append((lambda e, o=o: e.matmul(
                out=self.pbank[o][:, :],
                lhsT=self.c_be16[:, o * P:(o + 1) * P], rhs=self.sbW8[:],
                start=False, stop=True),
                [self.b_sbW8], o))
        self.mm_seq(items, self.bankbuf, tag=f"webias{c}")
        yield 0
        for o in range(KD):
            op("vector", lambda e, o=o: e.tensor_tensor(
                out=self.xch[:, o, :], in0=self.pbank[o][:, :],
                in1=self.xch[:, o, :], op=ALU.add),
                reads=[self.bankbuf[o]], writes=[self.b_xch],
                tag=f"res2{c}_{o}")
        if c == 0:
            self.tap("resid2", self.xch[:])
        self.dma("scalar", "dmaA", self.d_out[:, :, c * TT:(c + 1) * TT],
                 self.xch[:], reads=[self.b_xch], tag=f"out{c}")

    def drive_gens(self, gens, slot):
        for gst in gens:
            n = 0
            while (gst["ms"] is not None and gst["ms"] <= slot
                   and n < gst["rate"]):
                try:
                    gst["ms"] = next(gst["it"])
                except StopIteration:
                    gst["ms"] = None
                n += 1

    def mkgen(self, it, rate):
        try:
            ms = next(it)
        except StopIteration:
            ms = None
        return {"it": it, "ms": ms, "rate": rate}

    def emit_span(self, c):
        op = self.op
        upar = c % 2
        u16 = self.u16[upar]
        b_u = self.b_u[upar]
        gens = [self.mkgen(self.gen_wdt(c), 4),
                self.mkgen(self.gen_tail(c - 1), 4),
                self.mkgen(self.gen_front(c + 1), 2)]
        slot = 0
        for g in range(KI):
            self.drive_gens(gens, slot)
            ds = g % 8
            vp = g % 2
            op("vector", lambda e, g=g, ds=ds, vp=vp: e.tensor_tensor(
                out=self.v16[vp][:], in0=self.dful[:, ds, :],
                in1=u16[:, g, :],
                op=ALU.mult), reads=[self.b_dful[ds], b_u],
                writes=[self.b_v[vp]], tag=f"v{c}_{g}")
            for sg in range(NSG):
                sp = sg % 2
                da3 = self.dA[sp][:].rearrange("p (s c) -> p s c", s=SG)
                db3 = self.dBu[sp][:].rearrange("p (s c) -> p s c", s=SG)
                for jj in range(SG):
                    s_idx = sg * SG + jj
                    col = g * 16 + s_idx
                    op("scalar", lambda e, jj=jj, col=col, sp=sp, da3=da3,
                       ds=ds:
                       e.activation(
                           out=da3[:, jj, 2:514], in_=self.dful[:, ds, :],
                           func=AF.Exp, scale=self.c_A[:, col:col + 1]),
                       reads=[self.b_dful[ds]],
                       writes=[self.b_dA[sp]],
                       tag=f"dA{c}_{g}_{sg}_{jj}")
                op("vector", lambda e, sp=sp, sg=sg, db3=db3, vp=vp:
                    e.tensor_tensor(
                    out=db3[:, :, 2:514],
                    in0=self.v16[vp][:].unsqueeze(1)
                    .broadcast_to([P, SG, 512]),
                    in1=self.ball[:, sg * SG:(sg + 1) * SG, :],
                    op=ALU.mult), reads=[self.b_v[vp], self.b_ball],
                    writes=[self.b_dBu[sp]], tag=f"dbu{c}_{g}_{sg}")
                cc = g * 16 + sg * SG
                t_inj = op("vector", lambda e, sp=sp, cc=cc, db3=db3:
                    e.tensor_copy(
                    out=db3[:, :, 0:1].squeeze(2),
                    in_=self.carry[:, cc:cc + SG]),
                    reads=[self.b_carry], writes=[self.b_dBu[sp]],
                    tag=f"cin{c}_{g}_{sg}")
                hsl = self.hfull[:].rearrange("p s c -> p (s c)")[
                    :, sg * SCF:(sg + 1) * SCF]
                op("vector", lambda e, sp=sp, hsl=hsl: e.tensor_tensor_scan(
                    out=hsl, data0=self.dA[sp][:], data1=self.dBu[sp][:],
                    initial=0.0, op0=ALU.mult, op1=ALU.add),
                    reads=[self.b_dA[sp], self.b_dBu[sp]],
                    writes=[self.b_hfull], hard=[t_inj],
                    tag=f"scan{c}_{g}_{sg}")
                slot += 1
                self.drive_gens(gens, slot)
            t_ext = op("vector", lambda e, g=g: e.tensor_copy(
                out=self.carry[:, g * 16:(g + 1) * 16],
                in_=self.hfull[:, :, 513:514].squeeze(2)),
                reads=[self.b_hfull], writes=[self.b_carry],
                tag=f"cout{c}_{g}")
            hD = self.hfull[:]
            for span in (8, 4, 2, 1):
                op("vector", lambda e, span=span, hD=hD: e.tensor_tensor(
                    out=hD[:, 0:span, 2:514], in0=hD[:, 0:span, 2:514],
                    in1=hD[:, span:2 * span, 2:514], op=ALU.add),
                    reads=[self.b_hfull], writes=[self.b_hfull],
                    hard=[t_ext] if span == 8 else (),
                    tag=f"tree{c}_{g}_{span}")
            op("vector", lambda e, g=g: e.scalar_tensor_tensor(
                out=self.ytmp[:], in0=u16[:, g, :],
                scalar=self.c_dr[:, g:g + 1],
                in1=self.hfull[:, 0:1, 2:514].squeeze(1),
                op0=ALU.mult, op1=ALU.add),
                reads=[b_u, self.b_hfull], writes=[self.b_ytmp],
                tag=f"yd{c}_{g}")
            op("vector", lambda e, g=g: e.tensor_tensor(
                out=self.yrhs[:, g, :], in0=self.ytmp[:],
                in1=self.sz16[:, g, :], op=ALU.mult),
                reads=[self.b_ytmp, self.b_sz[g]], writes=[self.b_yrhs],
                tag=f"ysz{c}_{g}")
        if c == 0:
            self.tap("yrhs", self.yrhs[:])
        self.drive_gens(gens, 10 ** 9)
        while any(gst["ms"] is not None for gst in gens):
            self.drive_gens(gens, 10 ** 9)
        self.emit_ball(c + 1)

    def emit_all(self):
        self.prologue()
        for _ in self.gen_front(0):
            pass
        self.emit_ball(0)
        for c in range(self.n_chunks):
            self.emit_span(c)
        for _ in self.gen_tail(self.n_chunks - 1):
            pass

    def finish(self):
        v = self.pr.cnt["dmaA"]
        self.pr.emit("scalar", None, [("dmaA", v)], None, 0, "final_wait")
        vs = self.pr.cnt["dmaS"]
        self.pr.emit("sync", None, [("dmaS", vs)], None, 0, "final_waitS")

    def replay(self, block):
        run_ends = {"dmaS": [], "dmaA": []}
        last_dma_idx = {}
        eng_pos = {e: 0 for e in ENGS}
        cur_run = {}
        for (e, fn, ws, inc_sem, inc, tag) in self.pr.ops:
            pos = eng_pos[e]
            eng_pos[e] += 1
            if inc_sem not in ("dmaS", "dmaA") or not inc:
                continue
            adjacent = (last_dma_idx.get((e, inc_sem)) == pos - 1
                        and not ws)
            last_dma_idx[(e, inc_sem)] = pos
            if not adjacent and inc_sem in cur_run:
                run_ends[inc_sem].append(cur_run[inc_sem])
            cur_run[inc_sem] = cur_run.get(inc_sem, 0) + inc
        for s, v in cur_run.items():
            run_ends[s].append(v)

        import bisect

        def roundup(s, v):
            ends = run_ends.get(s)
            if not ends:
                return v
            i = bisect.bisect_left(ends, v)
            return ends[i] if i < len(ends) else v

        def section(engname):
            def go(eng):
                recent = []
                for (e, fn, ws, inc_sem, inc, tag) in self.pr.ops:
                    if e != engname:
                        continue
                    for (s, v) in ws:
                        if s in ("dmaS", "dmaA"):
                            v = roundup(s, v)
                        eng.wait_ge(self.sems[s], v)
                    if fn is None:
                        continue
                    recent.append(tag)
                    try:
                        inst = fn(eng)
                    except Exception:
                        import sys
                        sys.stderr.write(
                            "replay fail at %s; recent=%s\n"
                            % (tag, recent[-30:]))
                        raise
                    if inc:
                        inst.then_inc(self.sems[inc_sem], inc)
            return go

        block.tensor(section("tensor"))
        block.vector(section("vector"))
        block.scalar(section("scalar"))
        block.gpsimd(section("gpsimd"))
        block.sync(section("sync"))


def build(n_chunks=CH, taps=()):
    nc = bass.Bass("TRN2", target_bir_lowering=False)
    with contextlib.ExitStack() as stack:
        kb = KB(nc, stack, taps=taps, n_chunks=n_chunks)
        block = stack.enter_context(nc.Block())
        kb.emit_all()
        kb.finish()
        kb.replay(block)
    return nc


def _host_blocks(ins):
    W_in = np.asarray(ins["L0_attn_W_in"], np.float32)
    conv_w = np.asarray(ins["L0_attn_conv_w"], np.float32)
    W_x = np.asarray(ins["L0_attn_W_x"], np.float32)
    W_dt = np.asarray(ins["L0_attn_W_dt"], np.float32)
    W_out = np.asarray(ins["L0_attn_W_out"], np.float32)
    Wg = np.asarray(ins["L0_ffn_Wg"], np.float32)
    We = np.asarray(ins["L0_ffn_W"], np.float32)
    be = np.asarray(ins["L0_ffn_b"], np.float32)

    wf = np.zeros((P, FT), np.float32)

    def put(name, arr):
        o = BOFF[name]
        wf[:, o:o + arr.shape[1]] = arr

    put("eye", np.eye(P, dtype=np.float32))
    r16 = np.zeros((P, 2048), np.float32)
    r16[0:16] = np.kron(np.eye(16, dtype=np.float32), np.ones((1, P), np.float32))
    put("r16", r16)
    r8 = np.zeros((P, 1024), np.float32)
    r8[0:8] = np.kron(np.eye(8, dtype=np.float32), np.ones((1, P), np.float32))
    put("r8", r8)
    bep = np.zeros((P, 1024), np.float32)
    bep[0:8] = be
    put("be", bep)
    wp = []
    for mg in [0, 4, 1, 5, 2, 6, 3, 7]:
        for k in range(KD):
            wp.append(W_in[k * P:(k + 1) * P, mg * 512:(mg + 1) * 512])
    put("win", np.concatenate(wp, axis=1))
    cp = []
    for g in range(KI):
        for kk in range(4):
            cp.append(np.diag(conv_w[g * P:(g + 1) * P, 0, kk]).astype(np.float32))
    put("conv", np.concatenate(cp, axis=1))
    xp = [W_x[k * P:(k + 1) * P, DS:2 * DS] for k in range(KI)]
    put("wx", np.concatenate(xp, axis=1))
    dp = []
    for g in range(KI):
        for k in range(KI):
            dp.append(W_dt[k * P:(k + 1) * P, g * P:(g + 1) * P])
    put("wdt", np.concatenate(dp, axis=1))
    op_ = []
    for mp in range(4):
        for k in range(KI):
            op_.append(W_out[k * P:(k + 1) * P, mp * 256:(mp + 1) * 256])
    put("wout", np.concatenate(op_, axis=1))
    gp = [np.ascontiguousarray(Wg[:, k * P:(k + 1) * P].T) for k in range(KD)]
    put("wg", np.concatenate(gp, axis=1))
    ep = []
    for e in range(E):
        WeT = We[e].T
        for k in range(KD):
            ep.append(np.ascontiguousarray(WeT[k * P:(k + 1) * P, :]))
    put("we", np.concatenate(ep, axis=1))

    b_in = np.asarray(ins["L0_attn_b_in"], np.float32)
    b_dt = np.asarray(ins["L0_attn_b_dt"], np.float32)
    b_out = np.asarray(ins["L0_attn_b_out"], np.float32)
    Dv = np.asarray(ins["L0_attn_D"], np.float32)
    conv_b = np.asarray(ins["L0_attn_conv_b"], np.float32)
    A_log = np.asarray(ins["L0_attn_A_log"], np.float32)
    b_x = np.asarray(ins["L0_attn_b_x"], np.float32)
    bg = np.asarray(ins["L0_ffn_bg"], np.float32)

    bias = {
        "binr": np.ascontiguousarray(b_in.reshape(32, P).T),
        "bdtr": np.ascontiguousarray(b_dt.reshape(16, P).T),
        "boutr": np.ascontiguousarray(b_out.reshape(8, P).T),
        "dr": np.ascontiguousarray(Dv.reshape(16, P).T),
        "convbr": np.ascontiguousarray(conv_b.reshape(16, P).T),
        "alogr": np.ascontiguousarray(
            A_log.reshape(16, P, 16).transpose(1, 0, 2).reshape(P, 256)),
        "bx": np.zeros((P, 1), np.float32),
        "bgrow": np.ascontiguousarray(bg.reshape(1, 8)),
    }
    bias["bx"][0:16, 0] = b_x[DS:2 * DS]
    return wf, bias


def _host_x(xb):
    return np.ascontiguousarray(
        xb.T.reshape(KD, P, T).transpose(1, 0, 2)).astype(np.float32)


def _host_out(o):
    return np.ascontiguousarray(
        np.asarray(o).transpose(1, 0, 2).reshape(D, T).T)


def kernel(**inputs):
    x = np.asarray(inputs["L0_attn_x"], np.float32)
    wf, bias = _host_blocks(inputs)
    nc = build()
    in_maps = []
    for b in range(8):
        m = {"x_in": _host_x(x[b]), "wf32": wf}
        for k, v in bias.items():
            m["bias_" + k] = v
        in_maps.append(m)
    res = run_bass_kernel_spmd(nc, in_maps, core_ids=list(range(8)))
    out = np.stack([_host_out(r["out"]) for r in res.results], axis=0)
    return out


# version 08
# speedup vs baseline: 1.0014x; 1.0014x over previous
import contextlib
import numpy as np

import concourse.bass as bass
import concourse.mybir as mybir
from concourse.bass_utils import run_bass_kernel_spmd

F32 = mybir.dt.float32
BF16 = mybir.dt.bfloat16
AF = mybir.ActivationFunctionType
ALU = mybir.AluOpType
AXX = mybir.AxisListType.X

P = 128
T = 4096
D = 1024
DI = 2048
DS = 16
E = 8
CH = 8
TT = 512
KD = D // P
KI = DI // P
SBLK = 514
SG = 2
NSG = DS // SG
SCF = SG * SBLK

ENGS = ("tensor", "vector", "scalar", "gpsimd", "sync")
SEMS = ENGS + ("dmaS", "dmaA")

BLK = [
    ("eye", 128),
    ("r16", 2048),
    ("r8", 1024),
    ("be", 1024),
    ("win", 64 * 512),
    ("conv", 64 * 128),
    ("wx", 16 * 16),
    ("wdt", 256 * 128),
    ("wout", 64 * 256),
    ("wg", 8 * 8),
    ("we", 64 * 1024),
]
BOFF = {}
_o = 0
for _n, _c in BLK:
    BOFF[_n] = _o
    _o += _c
FT = _o


class Buf:
    __slots__ = ("w", "rs")

    def __init__(self):
        self.w = None
        self.rs = []

    def r(self):
        return [self.w] if self.w is not None else []

    def wkill(self):
        out = list(self.rs)
        if self.w is not None:
            out.append(self.w)
        return out


class Prog:
    def __init__(self):
        self.ops = []
        self.cnt = {s: 0 for s in SEMS}
        self.maxw = {}

    def emit(self, eng, fn, waits=(), inc_sem=None, inc=1, tag="", hard=()):
        if inc_sem is None:
            inc_sem = eng
        seen = {}
        for (s, v) in waits:
            if s == eng and s in ENGS:
                continue
            if v > seen.get(s, -1):
                seen[s] = v
        for (s, v) in hard:
            if v > seen.get(s, -1):
                seen[s] = v
        ws = []
        for s, v in seen.items():
            k = (eng, s)
            if self.maxw.get(k, -1) >= v:
                continue
            self.maxw[k] = v
            ws.append((s, v))
        tok = None
        if inc:
            self.cnt[inc_sem] += inc
            tok = (inc_sem, self.cnt[inc_sem])
        self.ops.append((eng, fn, ws, inc_sem, inc, tag))
        return tok


class KB:

    def __init__(self, nc, stack, taps=(), n_chunks=CH):
        self.nc = nc
        self.n_chunks = n_chunks
        self.st = stack
        self.pr = Prog()
        self.taps_req = set(taps)
        self.taps = {}
        self.d_x = nc.dram_tensor("x_in", [P, KD, T], F32, kind="ExternalInput").ap()
        self.d_wf = nc.dram_tensor("wf32", [P, FT], F32, kind="ExternalInput").ap()
        self.d_w16 = nc.dram_tensor("w16", [P, FT], BF16, kind="Internal").ap()
        self.d_out = nc.dram_tensor("out", [P, KD, T], F32, kind="ExternalOutput").ap()
        bias_specs = [
            ("binr", [P, 32]), ("bdtr", [P, 16]), ("boutr", [P, 8]),
            ("dr", [P, 16]), ("convbr", [P, 16]), ("alogr", [P, 256]),
            ("bx", [P, 1]), ("bgrow", [1, 8]),
        ]
        self.d_bias = {n: nc.dram_tensor("bias_" + n, s, F32, kind="ExternalInput").ap()
                       for n, s in bias_specs}
        self.sems = {}
        self.alloc()

    def sb(self, name, shape, dt):
        t = self.st.enter_context(self.nc.sbuf_tensor(name, shape, dt))
        return t

    def alloc(self):
        sb = self.sb
        st = self.st
        self.c_binr = sb("c_binr", [P, 32], F32)
        self.c_bdtr = sb("c_bdtr", [P, 16], F32)
        self.c_boutr = sb("c_boutr", [P, 8], F32)
        self.c_dr = sb("c_dr", [P, 16], F32)
        self.c_convbr = sb("c_convbr", [P, 16], F32)
        self.c_A = sb("c_A", [P, 256], F32)
        self.c_bx = sb("c_bx", [P, 1], F32)
        self.c_bgrow = sb("c_bgrow", [1, 8], F32)
        self.c_bgrow16 = sb("c_bgrow16", [1, 8], BF16)
        self.c_bgb = sb("c_bgb", [P, 8], F32)
        self.c_eye16 = sb("c_eye16", [P, P], BF16)
        self.c_r16 = sb("c_r16", [16, 2048], BF16)
        self.c_be16 = sb("c_be16", [8, 1024], BF16)
        self.c_ones1 = sb("c_ones1", [1, P], BF16)
        self.c_onesc = sb("c_onesc", [P, 1], BF16)
        self.xch = sb("xch", [P, KD, TT], F32)
        self.scr16 = sb("scr16", [P, KD, TT], BF16)
        self.xn1 = sb("xn1", [P, KD, TT], BF16)
        self.xn2 = sb("xn2", [P, KD, TT], BF16)
        self.u16 = [sb(f"u16_{i}", [P, KI, TT], BF16) for i in range(2)]
        self.sz16 = sb("sz16", [P, KI, TT], BF16)
        self.ball = sb("ball", [P, DS, TT], BF16)
        self.yrhs = sb("yrhs", [P, KI, TT], BF16)
        self.xcv = [sb(f"xcv{i}", [P, 516], BF16) for i in range(2)]
        self.tails = sb("tails", [P, 16, 4], BF16)
        self.dA = [sb(f"dA{i}", [P, SCF], F32) for i in range(2)]
        self.dBu = [sb(f"dBu{i}", [P, SCF], BF16) for i in range(2)]
        self.hfull = sb("hfull", [P, DS, SBLK], BF16)
        self.carry = sb("carry", [P, 256], BF16)
        self.sp1 = sb("sp1", [P, TT], F32)
        self.sp2 = sb("sp2", [P, TT], F32)
        self.sp3 = sb("sp3", [P, TT], F32)
        self.dful = sb("dful", [P, 8, TT], BF16)
        self.v16 = [sb(f"v16_{i}", [P, TT], BF16) for i in range(2)]
        self.ytmp = sb("ytmp", [P, TT], BF16)
        self.st_a = sb("st_a", [1, TT], F32)[:]
        self.st_b = sb("st_b", [1, TT], F32)[:]
        self.st_c = sb("st_c", [1, TT], F32)[:]
        self.mrow16 = sb("mrow16", [1, TT], BF16)
        self.rrow16 = sb("rrow16", [1, TT], BF16)
        self.mb16 = sb("mb16", [P, TT], BF16)
        self.rb16 = sb("rb16", [P, TT], BF16)
        self.gtm = sb("gtm", [P, 4, 8], F32)
        self.ge = sb("ge", [P, 8], F32)
        self.gm = sb("gm", [P, 8], F32)
        self.gs = sb("gs", [P, 8], F32)
        self.wtm16 = sb("wtm16", [P, 4, 8], BF16)
        self.sbW8 = sb("sbW8", [8, TT], BF16)
        self.sbB16 = sb("sbB16", [16, TT], BF16)
        self.streams = {}
        for name, ppc, pc, bp, rb in [
            ("win", 64, 512, 2, 2),
            ("conv", 64, 128, 4, 2),
            ("wx", 16, 16, 16, 1),
            ("wdt", 256, 128, 8, 2),
            ("wout", 64, 256, 4, 2),
            ("wg", 8, 8, 8, 1),
            ("we", 64, 1024, 1, 2),
        ]:
            arena = sb("ar_" + name, [P, rb * bp * pc], BF16)
            self.streams[name] = dict(
                name=name, ppc=ppc, pc=pc, bp=bp, rb=rb, arena=arena,
                slots=[Buf() for _ in range(rb)], gb=0, base=0,
                cast_tok=None)
        self.pbank = [self.st.enter_context(
            self.nc.psum_tensor(f"pb{i}", [P, 512], F32)) for i in range(8)]
        self.bankbuf = [Buf() for _ in range(8)]
        for s in SEMS:
            self.sems[s] = st.enter_context(self.nc.semaphore("sem_" + s))
        self.b_xch = Buf()
        self.b_scr16 = Buf()
        self.b_xn1 = Buf()
        self.b_xn2 = Buf()
        self.b_u = [Buf(), Buf()]
        self.b_sz = [Buf() for _ in range(KI)]
        self.b_ball = [Buf() for _ in range(NSG)]
        self.b_yrhs = Buf()
        self.b_xcv = [Buf(), Buf()]
        self.b_tails = Buf()
        self.b_dA = [Buf(), Buf()]
        self.b_dBu = [Buf(), Buf()]
        self.b_hfull = Buf()
        self.b_carry = Buf()
        self.b_sp = Buf()
        self.b_dful = [Buf() for _ in range(8)]
        self.b_v = [Buf(), Buf()]
        self.b_ytmp = Buf()
        self.b_ln = Buf()
        self.b_bcast = Buf()
        self.b_gate = Buf()
        self.b_sbB = Buf()
        self.b_sbW8 = Buf()
        self.b_wb = Buf()

    def op(self, eng, fn, reads=(), writes=(), waits=(), inc_sem=None, inc=1,
           tag="", hard=()):
        ws = list(waits)
        bc = getattr(self, "b_consts", None)
        if bc is not None and bc.w is not None:
            ws += bc.r()
        for b in reads:
            ws += b.r()
        for b in writes:
            ws += b.wkill()
        t = self.pr.emit(eng, fn, ws, inc_sem, inc, tag, hard=hard)
        if t is not None:
            for b in reads:
                b.rs.append(t)
            for b in writes:
                b.w = t
                b.rs = []
        return t

    def dma(self, queue_eng, sem, out_ap, in_ap, reads=(), writes=(),
            waits=(), tag=""):
        fn = (lambda e, o=out_ap, i=in_ap: e.dma_start(out=o, in_=i))
        return self.op(queue_eng, fn, reads=reads, writes=writes, waits=waits,
                       inc_sem=sem, inc=16, tag=tag)

    def mm_seq(self, items, bank_bufs, tag=""):
        n = len(items)
        banks = sorted({b for _, _, b in items})
        tok = None
        first_by_bank = {b: True for b in banks}
        bc = getattr(self, "b_consts", None)
        for i, (fn, rbs, bi) in enumerate(items):
            ws = []
            if i == 0 and bc is not None and bc.w is not None:
                ws += bc.r()
            for b in rbs:
                ws += b.r()
            if first_by_bank.get(bi):
                ws += bank_bufs[bi].wkill()
                first_by_bank[bi] = False
            last = i == n - 1
            t = self.pr.emit("tensor", fn, ws, None, 1 if last else 0, tag)
            if last:
                tok = t
        for _, rbs, _bi in items:
            for b in rbs:
                if tok not in b.rs:
                    b.rs.append(tok)
        for b in banks:
            bank_bufs[b].w = tok
            bank_bufs[b].rs = []
        return tok

    def tap(self, name, ap):
        if name not in self.taps_req:
            return
        shape = list(ap.shape)
        d = self.nc.dram_tensor("tap_" + name, shape, ap.dtype,
                                kind="ExternalOutput").ap()
        self.taps[name] = name
        waits = [(s, self.pr.cnt[s]) for s in ("tensor", "vector", "scalar",
                                               "gpsimd")]
        self.dma("scalar", "dmaA", d, ap, waits=waits, tag="tap_" + name)

    def set_stream_chunk(self, name, c):
        st = self.streams[name]
        st["chunk_base"] = c * (st["ppc"] // st["bp"])

    def _ensure(self, name, upto):
        s = self.streams[name]
        nb = s["ppc"] // s["bp"]
        cols = s["bp"] * s["pc"]
        while s["gb"] < upto:
            b = s["gb"]
            slot = b % s["rb"]
            src = BOFF[name] + (b % nb) * cols
            waits = []
            if b < nb and s.get("cast_tok"):
                waits.append(s["cast_tok"])
            self.dma("sync", "dmaS",
                     s["arena"][:, slot * cols:(slot + 1) * cols],
                     self.d_w16[:, src:src + cols],
                     writes=[s["slots"][slot]], waits=waits,
                     tag=f"w_{name}_{b}")
            s["gb"] += 1

    def piece(self, name, j):
        s = self.streams[name]
        gb = s["chunk_base"] + j // s["bp"]
        self._ensure(name, gb + 1)
        slot = gb % s["rb"]
        cols = s["bp"] * s["pc"]
        off = slot * cols + (j % s["bp"]) * s["pc"]
        return s["arena"][:, off:off + s["pc"]], s["slots"][slot]

    def prologue(self):
        nc = self.nc
        self.b_consts = Buf()
        BC = self.b_consts

        for n, dst in [("binr", self.c_binr), ("bdtr", self.c_bdtr),
                       ("boutr", self.c_boutr), ("dr", self.c_dr),
                       ("convbr", self.c_convbr), ("alogr", self.c_A),
                       ("bx", self.c_bx), ("bgrow", self.c_bgrow)]:
            self.dma("scalar", "dmaA", dst[:], self.d_bias[n],
                     writes=[BC], tag="bias_" + n)
        self.op("scalar", lambda e: e.activation(
            out=self.c_A[:], in_=self.c_A[:], func=AF.Exp),
            writes=[BC], tag="A_exp")
        self.op("vector", lambda e: e.tensor_scalar(
            out=self.c_A[:], in0=self.c_A[:], scalar1=-1.0, scalar2=None,
            op0=ALU.mult), writes=[BC], tag="A_neg")
        self.op("vector", lambda e: e.memset(self.c_ones1[:], 1.0),
                writes=[BC])
        self.op("vector", lambda e: e.memset(self.c_onesc[:], 1.0),
                writes=[BC])
        self.op("vector", lambda e: e.tensor_copy(
            out=self.c_bgrow16[:], in_=self.c_bgrow[:]), writes=[BC])
        self.mm_seq([(lambda e: e.matmul(
            out=self.pbank[7][:, 0:8], lhsT=self.c_ones1[:],
            rhs=self.c_bgrow16[:], start=True, stop=True), [BC], 7)],
            self.bankbuf, tag="bgb_mm")
        self.op("vector", lambda e: e.tensor_copy(
            out=self.c_bgb[:], in_=self.pbank[7][:, 0:8]),
            reads=[self.bankbuf[7]], writes=[BC], tag="bgb_ev")
        for par in range(2):
            da3 = self.dA[par][:].rearrange("p (s c) -> p s c", s=SG)
            db3 = self.dBu[par][:].rearrange("p (s c) -> p s c", s=SG)
            self.op("vector", lambda e, a=da3: e.memset(a[:, :, 0:1], 0.0),
                    writes=[self.b_dA[par]])
            self.op("vector", lambda e, a=da3: e.memset(a[:, :, 1:2], 1.0),
                    writes=[self.b_dA[par]])
            self.op("vector", lambda e, a=db3: e.memset(a[:, :, 1:2], 0.0),
                    writes=[self.b_dBu[par]])
        self.op("vector", lambda e: e.memset(self.carry[:], 0.0),
                writes=[self.b_carry])
        self.op("gpsimd", lambda e: e.memset(self.tails[:], 0.0),
                writes=[self.b_tails])
        fin_t = [self.streams["win"]["arena"][:].bitcast(F32),
                 self.streams["wdt"]["arena"][:].bitcast(F32)]
        fout = self.streams["we"]["arena"][:]
        b_in = [Buf(), Buf()]
        b_out = [Buf(), Buf()]
        pcw = 1024
        n_pieces = (FT + pcw - 1) // pcw
        blk_tok = {}
        blk_end = {n: BOFF[n] + c for n, c in BLK}
        for i in range(n_pieces):
            lo = i * pcw
            hi = min(FT, lo + pcw)
            w = hi - lo
            s_in = fin_t[i % 2][:, 0:w]
            s_out = fout[:, (i % 2) * pcw:(i % 2) * pcw + w]
            self.dma("sync", "dmaS", s_in, self.d_wf[:, lo:hi],
                     writes=[b_in[i % 2]], tag=f"cast_in{i}")
            self.op("vector", lambda e, o=s_out, s=s_in: e.tensor_copy(
                out=o, in_=s), reads=[b_in[i % 2]], writes=[b_out[i % 2]],
                tag=f"cast{i}")
            t = self.dma("sync", "dmaS", self.d_w16[:, lo:hi], s_out,
                         reads=[b_out[i % 2]], tag=f"cast_out{i}")
            for n, endc in blk_end.items():
                if lo < endc <= hi:
                    blk_tok[n] = t
        self.blk_tok = blk_tok
        for name in self.streams:
            self.streams[name]["cast_tok"] = blk_tok[name]
        last_in = b_in[0].wkill() + b_in[1].wkill()
        last_out = b_out[0].wkill() + b_out[1].wkill()
        for s, toks in (("win", last_in), ("wdt", last_in),
                        ("we", last_out)):
            for sl in self.streams[s]["slots"]:
                sl.rs = list(toks)
        self.dma("sync", "dmaS", self.c_eye16[:],
                 self.d_w16[:, BOFF["eye"]:BOFF["eye"] + 128],
                 waits=[blk_tok["eye"]], writes=[self.b_consts],
                 tag="ld_eye")
        self.dma("sync", "dmaS", self.c_r16[:],
                 self.d_w16[0:16, BOFF["r16"]:BOFF["r16"] + 2048],
                 waits=[blk_tok["r16"]], writes=[self.b_consts],
                 tag="ld_r16")
        self.dma("sync", "dmaS", self.c_be16[:],
                 self.d_w16[0:8, BOFF["be"]:BOFF["be"] + 1024],
                 waits=[blk_tok["be"]], writes=[self.b_consts],
                 tag="ld_be")

    def gen_ln(self, src_f32_3d, src_buf, dst16, b_dst, xsq, b_xsq,
               bk0, bk1, tag):
        op = self.op
        yield 0
        op("scalar", lambda e: e.activation(
            out=self.scr16[:], in_=src_f32_3d, func=AF.Identity),
            reads=[src_buf], writes=[self.b_scr16], tag=tag + "_cast")
        yield 0
        op("vector", lambda e, o=xsq: e.tensor_tensor(
            out=o, in0=self.scr16[:], in1=self.scr16[:], op=ALU.mult),
            reads=[self.b_scr16], writes=[b_xsq], tag=tag + "_sq")
        yield 0
        items = []
        for k in range(KD):
            items.append((lambda e, k=k: e.matmul(
                out=self.pbank[bk0][0:1, :], lhsT=self.c_onesc[:],
                rhs=self.scr16[:, k, :], start=(k == 0), stop=(k == KD - 1)),
                [self.b_scr16], bk0))
        for k in range(KD):
            items.append((lambda e, k=k, x=xsq: e.matmul(
                out=self.pbank[bk1][0:1, :], lhsT=self.c_onesc[:],
                rhs=x[:, k, :], start=(k == 0), stop=(k == KD - 1)),
                [b_xsq], bk1))
        self.mm_seq(items, self.bankbuf, tag=tag + "_stats")
        yield 0
        op("vector", lambda e: e.tensor_scalar(
            out=self.st_a, in0=self.pbank[bk0][0:1, :], scalar1=1.0 / D,
            scalar2=None, op0=ALU.mult), reads=[self.bankbuf[bk0]],
            writes=[self.b_ln], tag=tag + "_mean")
        op("vector", lambda e: e.tensor_scalar(
            out=self.st_b, in0=self.pbank[bk1][0:1, :], scalar1=1.0 / D,
            scalar2=None, op0=ALU.mult), reads=[self.bankbuf[bk1]],
            writes=[self.b_ln], tag=tag + "_msq")
        op("vector", lambda e: e.tensor_tensor(
            out=self.st_c, in0=self.st_a, in1=self.st_a,
            op=ALU.mult), writes=[self.b_ln], tag=tag + "_m2")
        op("vector", lambda e: e.tensor_tensor(
            out=self.st_b, in0=self.st_b, in1=self.st_c,
            op=ALU.subtract), writes=[self.b_ln], tag=tag + "_var")
        yield 0
        op("scalar", lambda e: e.activation(
            out=self.st_c, in_=self.st_b, func=AF.Ln, bias=np.float32(1e-5)),
            writes=[self.b_ln], tag=tag + "_lnv")
        op("scalar", lambda e: e.activation(
            out=self.st_b, in_=self.st_c, func=AF.Exp, scale=np.float32(-0.5)),
            writes=[self.b_ln], tag=tag + "_rstd")
        op("vector", lambda e: e.tensor_copy(out=self.mrow16[:],
                                             in_=self.st_a),
           writes=[self.b_ln], tag=tag + "_m16")
        op("vector", lambda e: e.tensor_copy(out=self.rrow16[:],
                                             in_=self.st_b),
           writes=[self.b_ln], tag=tag + "_r16c")
        yield 0
        items = [
            (lambda e: e.matmul(out=self.pbank[bk0][:, :],
                                lhsT=self.c_ones1[:],
                                rhs=self.mrow16[:], start=True, stop=True),
             [self.b_ln], bk0),
            (lambda e: e.matmul(out=self.pbank[bk1][:, :],
                                lhsT=self.c_ones1[:],
                                rhs=self.rrow16[:], start=True, stop=True),
             [self.b_ln], bk1),
        ]
        self.mm_seq(items, self.bankbuf, tag=tag + "_bc")
        op("scalar", lambda e: e.activation(
            out=self.mb16[:], in_=self.pbank[bk0][:, :], func=AF.Identity),
            reads=[self.bankbuf[bk0]], writes=[self.b_bcast],
            tag=tag + "_mbev")
        op("scalar", lambda e: e.activation(
            out=self.rb16[:], in_=self.pbank[bk1][:, :], func=AF.Identity),
            reads=[self.bankbuf[bk1]], writes=[self.b_bcast],
            tag=tag + "_rbev")
        yield 0
        for k in range(KD):
            op("vector", lambda e, k=k: e.tensor_tensor(
                out=dst16[:, k, :],
                in0=self.scr16[:, k, :], in1=self.mb16[:], op=ALU.subtract),
                reads=[self.b_scr16, self.b_bcast], writes=[b_dst],
                tag=tag + f"_sub{k}")
            op("vector", lambda e, k=k: e.tensor_tensor(
                out=dst16[:, k, :], in0=dst16[:, k, :], in1=self.rb16[:],
                op=ALU.mult), reads=[self.b_bcast], writes=[b_dst],
                tag=tag + f"_mul{k}")
            if k % 3 == 2:
                yield 0

    def gen_front(self, c):
        if c >= self.n_chunks:
            return
        op = self.op
        for nm in ("win", "conv", "wx"):
            self.set_stream_chunk(nm, c)
        upar = c % 2
        u16 = self.u16[upar]
        b_u = self.b_u[upar]
        yield 40
        self.dma("scalar", "dmaA", self.xch[:],
                 self.d_x[:, :, c * TT:(c + 1) * TT],
                 writes=[self.b_xch], tag=f"x{c}")
        for ms in self.gen_ln(self.xch[:], self.b_xch, self.xn1, self.b_xn1,
                              self.xn2[:], self.b_xn2, 6, 7, f"ln1_{c}"):
            yield max(ms, 42)
        if c == 0:
            self.tap("xn1", self.xn1[:])
        WSEQ = [0, 4, 1, 5, 2, 6, 3, 7]
        for oi, mg in enumerate(WSEQ):
            is_z = mg >= 4
            yield 46
            for kb in range(4):
                items = []
                for k in (2 * kb, 2 * kb + 1):
                    pap, pslot = self.piece("win", oi * 8 + k)
                    for m in range(4):
                        items.append((lambda e, pap=pap, m=m, k=k: e.matmul(
                            out=self.pbank[m][:, :],
                            lhsT=pap[:, m * P:(m + 1) * P],
                            rhs=self.xn1[:, k, :],
                            start=(k == 0), stop=(k == KD - 1)),
                            [pslot, self.b_xn1], m))
                self.mm_seq(items, self.bankbuf, tag=f"win{c}_{mg}_{kb}")
            for m in range(4):
                j = mg * 4 + m
                if j < KI:
                    g = j
                    par = g % 2
                    yield 46
                    op("vector", lambda e, g=g, par=par: e.tensor_copy(
                        out=self.xcv[par][:, 0:3], in_=self.tails[:, g, 0:3]),
                        reads=[self.b_tails], writes=[self.b_xcv[par]],
                        tag=f"tin{c}_{g}")
                    op("scalar", lambda e, m=m, g=g, par=par: e.activation(
                        out=self.xcv[par][:, 3:515], in_=self.pbank[m][:, :],
                        func=AF.Identity, bias=self.c_binr[:, g:g + 1]),
                        reads=[self.bankbuf[m]], writes=[self.b_xcv[par]],
                        tag=f"xhev{c}_{g}")
                    citems = []
                    for kk in range(4):
                        pap, pslot = self.piece("conv", g * 4 + kk)
                        citems.append((lambda e, pap=pap, kk=kk, par=par, m=m:
                                       e.matmul(
                                           out=self.pbank[m][:, :], lhsT=pap,
                                           rhs=self.xcv[par][:, kk:kk + 512],
                                           start=(kk == 0), stop=(kk == 3)),
                                       [pslot, self.b_xcv[par]], m))
                    self.mm_seq(citems, self.bankbuf, tag=f"conv{c}_{g}")
                    op("vector", lambda e, g=g, par=par: e.tensor_copy(
                        out=self.tails[:, g, 0:3],
                        in_=self.xcv[par][:, 512:515]),
                        reads=[self.b_xcv[par]], writes=[self.b_tails],
                        tag=f"tout{c}_{g}")
                    op("scalar", lambda e, g=g, m=m: e.activation(
                        out=u16[:, g, :], in_=self.pbank[m][:, :],
                        func=AF.Silu, bias=self.c_convbr[:, g:g + 1]),
                        reads=[self.bankbuf[m]], writes=[b_u],
                        tag=f"uev{c}_{g}")
                else:
                    zi = j - KI
                    yield 8 * zi + 10
                    op("scalar", lambda e, m=m, zi=zi, j=j: e.activation(
                        out=self.sz16[:, zi, :], in_=self.pbank[m][:, :],
                        func=AF.Silu, bias=self.c_binr[:, j:j + 1]),
                        reads=[self.bankbuf[m]], writes=[self.b_sz[zi]],
                        tag=f"szev{c}_{zi}")
        if c == 0:
            self.tap("u", u16[:])
            self.tap("sz", self.sz16[:])
        yield 80
        items = []
        for k in range(KI):
            pap, pslot = self.piece("wx", k)
            items.append((lambda e, pap=pap, k=k: e.matmul(
                out=self.pbank[7][0:16, :], lhsT=pap, rhs=u16[:, k, :],
                start=(k == 0), stop=(k == KI - 1)), [pslot, b_u], 7))
        self.mm_seq(items, self.bankbuf, tag=f"wxB{c}")
        op("vector", lambda e: e.tensor_scalar(
            out=self.sbB16[:], in0=self.pbank[7][0:16, :],
            scalar1=self.c_bx[0:16, :], scalar2=None, op0=ALU.add),
            reads=[self.bankbuf[7]], writes=[self.b_sbB], tag=f"Bev{c}")
        if c == 0:
            self.tap("B", self.sbB16[:])

    def gen_ball(self, c):
        if c >= self.n_chunks:
            return
        op = self.op
        for sb in range(NSG):
            yield max(0, sb - 2)
            for si in (2 * sb, 2 * sb + 1):
                bb = 6 + (si % 2)
                self.mm_seq([(lambda e, si=si, bb=bb: e.matmul(
                    out=self.pbank[bb][:, :],
                    lhsT=self.c_r16[:, si * P:(si + 1) * P],
                    rhs=self.sbB16[:],
                    start=True, stop=True),
                    [self.b_sbB], bb)],
                    self.bankbuf, tag=f"bb{c}_{si}")
                op("scalar", lambda e, si=si, bb=bb, sb=sb: e.activation(
                    out=self.ball[:, si, :], in_=self.pbank[bb][:, :],
                    func=AF.Identity), reads=[self.bankbuf[bb]],
                    writes=[self.b_ball[sb]], tag=f"bbev{c}_{si}")

    def gen_wdt(self, c):
        op = self.op
        self.set_stream_chunk("wdt", c)
        upar = c % 2
        u16 = self.u16[upar]
        b_u = self.b_u[upar]
        wsl = [0, 0, 1, 1, 2, 2, 41, 44, 47, 50, 53, 56, 59, 62, 65, 68]
        for g in range(KI):
            yield wsl[g]
            bank = 6 if g == 0 else (7 if g == 1 else 4 + (g % 2))
            items = []
            for k in range(KI):
                pap, pslot = self.piece("wdt", g * 16 + k)
                items.append((lambda e, pap=pap, k=k, bank=bank: e.matmul(
                    out=self.pbank[bank][:, :], lhsT=pap,
                    rhs=u16[:, k, :], start=(k == 0),
                    stop=(k == KI - 1)), [pslot, b_u], bank))
            self.mm_seq(items, self.bankbuf, tag=f"wdt{c}_{g}")
            op("scalar", lambda e, g=g, bank=bank: e.activation(
                out=self.sp1[:], in_=self.pbank[bank][:, :], func=AF.Abs,
                bias=self.c_bdtr[:, g:g + 1]), reads=[self.bankbuf[bank]],
                writes=[self.b_sp], tag=f"spabs{c}_{g}")
            op("scalar", lambda e, g=g, bank=bank: e.activation(
                out=self.sp2[:], in_=self.pbank[bank][:, :], func=AF.Relu,
                bias=self.c_bdtr[:, g:g + 1]), reads=[self.bankbuf[bank]],
                writes=[self.b_sp], tag=f"sprelu{c}_{g}")
            op("scalar", lambda e: e.activation(
                out=self.sp3[:], in_=self.sp1[:], func=AF.Exp,
                scale=np.float32(-1.0)),
                writes=[self.b_sp], tag=f"spexp{c}_{g}")
            op("scalar", lambda e: e.activation(
                out=self.sp1[:], in_=self.sp3[:], func=AF.Ln,
                bias=np.float32(1.0)),
                writes=[self.b_sp], tag=f"spln{c}_{g}")
            ds = g % 8
            op("vector", lambda e, ds=ds: e.tensor_tensor(
                out=self.dful[:, ds, :], in0=self.sp2[:], in1=self.sp1[:],
                op=ALU.add), reads=[self.b_sp], writes=[self.b_dful[ds]],
                tag=f"delta{c}_{g}")
            if c == 0 and g == 0:
                self.tap("delta_g0", self.dful[:, 0, :])

    def gen_tail(self, c):
        if c < 0:
            return
        op = self.op
        for nm in ("wout", "wg", "we"):
            self.set_stream_chunk(nm, c)
        wbu = self.xn1
        b_wb = self.b_xn1
        yield 0
        self.dma("scalar", "dmaA", self.xch[:],
                 self.d_x[:, :, c * TT:(c + 1) * TT],
                 writes=[self.b_xch], tag=f"xr{c}")
        for mp in range(4):
            b0 = (mp % 2) * 2
            yield 0
            for kq in range(4):
                items = []
                for k in range(4 * kq, 4 * kq + 4):
                    pap, pslot = self.piece("wout", mp * 16 + k)
                    for mm_i in range(2):
                        items.append((lambda e, pap=pap, mm_i=mm_i, k=k,
                                      b0=b0: e.matmul(
                            out=self.pbank[b0 + mm_i][:, :],
                            lhsT=pap[:, mm_i * P:(mm_i + 1) * P],
                            rhs=self.yrhs[:, k, :], start=(k == 0),
                            stop=(k == KI - 1)), [pslot, self.b_yrhs],
                            b0 + mm_i))
                self.mm_seq(items, self.bankbuf, tag=f"wout{c}_{mp}_{kq}")
            for mm_i in range(2):
                m = mp * 2 + mm_i
                bb_ = b0 + mm_i
                op("vector", lambda e, m=m, bb_=bb_:
                   e.scalar_tensor_tensor(
                    out=self.xch[:, m, :], in0=self.pbank[bb_][:, :],
                    scalar=self.c_boutr[:, m:m + 1], in1=self.xch[:, m, :],
                    op0=ALU.add, op1=ALU.add),
                    reads=[self.bankbuf[bb_]], writes=[self.b_xch],
                    tag=f"res1{c}_{m}")
        if c == 0:
            self.tap("resid1", self.xch[:])
        for ms in self.gen_ln(self.xch[:], self.b_xch, self.xn2, self.b_xn2,
                              self.xn1[:], self.b_xn1, 0, 1, f"ln2_{c}"):
            yield ms
        if c == 0:
            self.tap("xn2", self.xn2[:])
        yield 0
        items = []
        for j in range(4):
            for k in range(KD):
                pap, pslot = self.piece("wg", k)
                items.append((lambda e, pap=pap, j=j, k=k: e.matmul(
                    out=self.pbank[2][:, j * 8:(j + 1) * 8],
                    lhsT=self.xn2[:, k, j * P:(j + 1) * P], rhs=pap,
                    start=(k == 0), stop=(k == KD - 1)),
                    [pslot, self.b_xn2], 2))
        self.mm_seq(items, self.bankbuf, tag=f"glog{c}")
        for j in range(4):
            op("vector", lambda e, j=j: e.tensor_tensor(
                out=self.gtm[:, j, :],
                in0=self.pbank[2][:, j * 8:(j + 1) * 8],
                in1=self.c_bgb[:], op=ALU.add),
                reads=[self.bankbuf[2]], writes=[self.b_gate],
                tag=f"gev{c}_{j}")
        if c == 0:
            self.tap("glog", self.gtm[:])
        yield 0
        for j in range(4):
            gj = self.gtm[:, j, :]
            pv = op("vector", lambda e, gj=gj: e.reduce_max(
                out=self.gs[:, 0:1], in_=gj, axis=AXX),
                reads=[self.b_gate], writes=[self.b_gate], tag=f"m1_{c}{j}")
            pv = op("vector", lambda e: e.tensor_scalar(
                out=self.gs[:, 1:2], in0=self.gs[:, 0:1], scalar1=-1.0,
                scalar2=None, op0=ALU.mult), writes=[self.b_gate], hard=[pv])
            pv = op("scalar", lambda e, gj=gj: e.activation(
                out=self.ge[:], in_=gj, func=AF.Exp,
                bias=self.gs[:, 1:2]),
                writes=[self.b_gate], hard=[pv], tag=f"gex{c}{j}")
            pv = op("vector", lambda e: e.reduce_sum(
                out=self.gs[:, 2:3], in_=self.ge[:], axis=AXX),
                reads=[self.b_gate], writes=[self.b_gate], hard=[pv])
            t_rd = pv = op("vector", lambda e: e.reciprocal(
                out=self.gs[:, 3:4], in_=self.gs[:, 2:3]),
                writes=[self.b_gate], hard=[pv])
            pv = op("vector", lambda e, gj=gj: e.tensor_scalar(
                out=self.gm[:], in0=gj, scalar1=self.gs[:, 0:1], scalar2=None,
                op0=ALU.is_ge), writes=[self.b_gate], hard=[pv])
            pv = op("vector", lambda e, gj=gj: e.scalar_tensor_tensor(
                out=self.gm[:], in0=self.gm[:], scalar=-1e30, in1=gj,
                op0=ALU.mult, op1=ALU.add), writes=[self.b_gate], hard=[pv])
            pv = op("vector", lambda e: e.reduce_max(
                out=self.gs[:, 4:5], in_=self.gm[:], axis=AXX),
                writes=[self.b_gate], hard=[pv])
            pv = op("vector", lambda e, gj=gj: e.tensor_scalar(
                out=self.gm[:], in0=gj, scalar1=self.gs[:, 4:5], scalar2=None,
                op0=ALU.is_ge), writes=[self.b_gate], hard=[pv])
            pv = op("vector", lambda e: e.tensor_tensor(
                out=self.ge[:], in0=self.ge[:], in1=self.gm[:], op=ALU.mult),
                writes=[self.b_gate], hard=[pv])
            op("vector", lambda e, j=j: e.tensor_scalar(
                out=self.wtm16[:, j, :], in0=self.ge[:],
                scalar1=self.gs[:, 3:4], scalar2=None, op0=ALU.mult),
                reads=[self.b_gate], writes=[self.b_gate], hard=[pv, t_rd],
                tag=f"wtm{c}{j}")
            yield 0
        if c == 0:
            self.tap("wtm", self.wtm16[:])
        items = []
        for j in range(4):
            items.append((lambda e, j=j: e.transpose(
                out=self.pbank[3][:].bitcast(BF16)[0:8, j * P:(j + 1) * P],
                in_=self.wtm16[:, j, :], identity=self.c_eye16[:]),
                [self.b_gate], 3))
        self.mm_seq(items, self.bankbuf, tag=f"wtr{c}")
        op("vector", lambda e: e.tensor_copy(
            out=self.sbW8[:], in_=self.pbank[3][:].bitcast(BF16)[0:8, 0:512]),
            reads=[self.bankbuf[3]], writes=[self.b_sbW8], tag=f"w8{c}")
        yield 0
        for e_i in range(E):
            bb = (e_i % 2)
            self.mm_seq([(lambda e, e_i=e_i, bb=bb: e.matmul(
                out=self.pbank[bb][:, :],
                lhsT=self.c_r16[0:8, e_i * P:(e_i + 1) * P],
                rhs=self.sbW8[:], start=True, stop=True),
                [self.b_sbW8], bb)], self.bankbuf, tag=f"wb{c}_{e_i}")
            op("scalar", lambda e, e_i=e_i, bb=bb: e.activation(
                out=wbu[:, e_i, :], in_=self.pbank[bb][:, :],
                func=AF.Identity), reads=[self.bankbuf[bb]],
                writes=[b_wb], tag=f"wbev{c}_{e_i}")
        for e_i in range(E):
            yield 4
            op("vector", lambda e, e_i=e_i: e.tensor_tensor(
                out=self.scr16[:],
                in0=self.xn2[:],
                in1=wbu[:, e_i, :].unsqueeze(1)
                .broadcast_to([P, KD, TT]),
                op=ALU.mult), reads=[self.b_xn2, b_wb],
                writes=[self.b_scr16], tag=f"xe{c}_{e_i}")
            for k in range(KD):
                pap, pslot = self.piece("we", e_i * 8 + k)
                items = []
                for o in range(KD):
                    items.append((lambda e, pap=pap, o=o, k=k, e_i=e_i:
                                  e.matmul(
                                      out=self.pbank[o][:, :],
                                      lhsT=pap[:, o * P:(o + 1) * P],
                                      rhs=self.scr16[:, k, :],
                                      start=(e_i == 0 and k == 0),
                                      stop=False),
                                  [pslot, self.b_scr16], o))
                self.mm_seq(items, self.bankbuf, tag=f"we{c}_{e_i}_{k}")
        items = []
        for o in range(KD):
            items.append((lambda e, o=o: e.matmul(
                out=self.pbank[o][:, :],
                lhsT=self.c_be16[:, o * P:(o + 1) * P], rhs=self.sbW8[:],
                start=False, stop=True),
                [self.b_sbW8], o))
        self.mm_seq(items, self.bankbuf, tag=f"webias{c}")
        yield 0
        for o in range(KD):
            op("vector", lambda e, o=o: e.tensor_tensor(
                out=self.xch[:, o, :], in0=self.pbank[o][:, :],
                in1=self.xch[:, o, :], op=ALU.add),
                reads=[self.bankbuf[o]], writes=[self.b_xch],
                tag=f"res2{c}_{o}")
        if c == 0:
            self.tap("resid2", self.xch[:])
        self.dma("scalar", "dmaA", self.d_out[:, :, c * TT:(c + 1) * TT],
                 self.xch[:], reads=[self.b_xch], tag=f"out{c}")

    def drive_gens(self, gens, slot):
        for gst in gens:
            n = 0
            while (gst["ms"] is not None and gst["ms"] <= slot
                   and n < gst["rate"]):
                try:
                    gst["ms"] = next(gst["it"])
                except StopIteration:
                    gst["ms"] = None
                n += 1

    def mkgen(self, it, rate):
        try:
            ms = next(it)
        except StopIteration:
            ms = None
        return {"it": it, "ms": ms, "rate": rate}

    def emit_span(self, c):
        op = self.op
        upar = c % 2
        u16 = self.u16[upar]
        b_u = self.b_u[upar]
        gens = [self.mkgen(self.gen_wdt(c), 4),
                self.mkgen(self.gen_ball(c), 2),
                self.mkgen(self.gen_tail(c - 1), 4),
                self.mkgen(self.gen_front(c + 1), 2)]
        slot = 0
        for g in range(KI):
            self.drive_gens(gens, slot)
            ds = g % 8
            vp = g % 2
            op("vector", lambda e, g=g, ds=ds, vp=vp: e.tensor_tensor(
                out=self.v16[vp][:], in0=self.dful[:, ds, :],
                in1=u16[:, g, :],
                op=ALU.mult), reads=[self.b_dful[ds], b_u],
                writes=[self.b_v[vp]], tag=f"v{c}_{g}")
            for sg in range(NSG):
                sp = sg % 2
                da3 = self.dA[sp][:].rearrange("p (s c) -> p s c", s=SG)
                db3 = self.dBu[sp][:].rearrange("p (s c) -> p s c", s=SG)
                for jj in range(SG):
                    s_idx = sg * SG + jj
                    col = g * 16 + s_idx
                    op("scalar", lambda e, jj=jj, col=col, sp=sp, da3=da3,
                       ds=ds:
                       e.activation(
                           out=da3[:, jj, 2:514], in_=self.dful[:, ds, :],
                           func=AF.Exp, scale=self.c_A[:, col:col + 1]),
                       reads=[self.b_dful[ds]],
                       writes=[self.b_dA[sp]],
                       tag=f"dA{c}_{g}_{sg}_{jj}")
                op("vector", lambda e, sp=sp, sg=sg, db3=db3, vp=vp:
                    e.tensor_tensor(
                    out=db3[:, :, 2:514],
                    in0=self.v16[vp][:].unsqueeze(1)
                    .broadcast_to([P, SG, 512]),
                    in1=self.ball[:, sg * SG:(sg + 1) * SG, :],
                    op=ALU.mult), reads=[self.b_v[vp], self.b_ball[sg]],
                    writes=[self.b_dBu[sp]], tag=f"dbu{c}_{g}_{sg}")
                cc = g * 16 + sg * SG
                t_inj = op("vector", lambda e, sp=sp, cc=cc, db3=db3:
                    e.tensor_copy(
                    out=db3[:, :, 0:1].squeeze(2),
                    in_=self.carry[:, cc:cc + SG]),
                    reads=[self.b_carry], writes=[self.b_dBu[sp]],
                    tag=f"cin{c}_{g}_{sg}")
                hsl = self.hfull[:].rearrange("p s c -> p (s c)")[
                    :, sg * SCF:(sg + 1) * SCF]
                op("vector", lambda e, sp=sp, hsl=hsl: e.tensor_tensor_scan(
                    out=hsl, data0=self.dA[sp][:], data1=self.dBu[sp][:],
                    initial=0.0, op0=ALU.mult, op1=ALU.add),
                    reads=[self.b_dA[sp], self.b_dBu[sp]],
                    writes=[self.b_hfull], hard=[t_inj],
                    tag=f"scan{c}_{g}_{sg}")
                slot += 1
                self.drive_gens(gens, slot)
            t_ext = op("vector", lambda e, g=g: e.tensor_copy(
                out=self.carry[:, g * 16:(g + 1) * 16],
                in_=self.hfull[:, :, 513:514].squeeze(2)),
                reads=[self.b_hfull], writes=[self.b_carry],
                tag=f"cout{c}_{g}")
            hD = self.hfull[:]
            for span in (8, 4, 2, 1):
                op("vector", lambda e, span=span, hD=hD: e.tensor_tensor(
                    out=hD[:, 0:span, 2:514], in0=hD[:, 0:span, 2:514],
                    in1=hD[:, span:2 * span, 2:514], op=ALU.add),
                    reads=[self.b_hfull], writes=[self.b_hfull],
                    hard=[t_ext] if span == 8 else (),
                    tag=f"tree{c}_{g}_{span}")
            op("vector", lambda e, g=g: e.scalar_tensor_tensor(
                out=self.ytmp[:], in0=u16[:, g, :],
                scalar=self.c_dr[:, g:g + 1],
                in1=self.hfull[:, 0:1, 2:514].squeeze(1),
                op0=ALU.mult, op1=ALU.add),
                reads=[b_u, self.b_hfull], writes=[self.b_ytmp],
                tag=f"yd{c}_{g}")
            op("vector", lambda e, g=g: e.tensor_tensor(
                out=self.yrhs[:, g, :], in0=self.ytmp[:],
                in1=self.sz16[:, g, :], op=ALU.mult),
                reads=[self.b_ytmp, self.b_sz[g]], writes=[self.b_yrhs],
                tag=f"ysz{c}_{g}")
        if c == 0:
            self.tap("yrhs", self.yrhs[:])
        self.drive_gens(gens, 10 ** 9)
        while any(gst["ms"] is not None for gst in gens):
            self.drive_gens(gens, 10 ** 9)

    def emit_all(self):
        self.prologue()
        for _ in self.gen_front(0):
            pass
        for c in range(self.n_chunks):
            self.emit_span(c)
        for _ in self.gen_tail(self.n_chunks - 1):
            pass

    def finish(self):
        v = self.pr.cnt["dmaA"]
        self.pr.emit("scalar", None, [("dmaA", v)], None, 0, "final_wait")
        vs = self.pr.cnt["dmaS"]
        self.pr.emit("sync", None, [("dmaS", vs)], None, 0, "final_waitS")

    def replay(self, block):
        run_ends = {"dmaS": [], "dmaA": []}
        last_dma_idx = {}
        eng_pos = {e: 0 for e in ENGS}
        cur_run = {}
        for (e, fn, ws, inc_sem, inc, tag) in self.pr.ops:
            pos = eng_pos[e]
            eng_pos[e] += 1
            if inc_sem not in ("dmaS", "dmaA") or not inc:
                continue
            adjacent = (last_dma_idx.get((e, inc_sem)) == pos - 1
                        and not ws)
            last_dma_idx[(e, inc_sem)] = pos
            if not adjacent and inc_sem in cur_run:
                run_ends[inc_sem].append(cur_run[inc_sem])
            cur_run[inc_sem] = cur_run.get(inc_sem, 0) + inc
        for s, v in cur_run.items():
            run_ends[s].append(v)

        import bisect

        def roundup(s, v):
            ends = run_ends.get(s)
            if not ends:
                return v
            i = bisect.bisect_left(ends, v)
            return ends[i] if i < len(ends) else v

        def section(engname):
            def go(eng):
                recent = []
                for (e, fn, ws, inc_sem, inc, tag) in self.pr.ops:
                    if e != engname:
                        continue
                    for (s, v) in ws:
                        if s in ("dmaS", "dmaA"):
                            v = roundup(s, v)
                        eng.wait_ge(self.sems[s], v)
                    if fn is None:
                        continue
                    recent.append(tag)
                    try:
                        inst = fn(eng)
                    except Exception:
                        import sys
                        sys.stderr.write(
                            "replay fail at %s; recent=%s\n"
                            % (tag, recent[-30:]))
                        raise
                    if inc:
                        inst.then_inc(self.sems[inc_sem], inc)
            return go

        block.tensor(section("tensor"))
        block.vector(section("vector"))
        block.scalar(section("scalar"))
        block.gpsimd(section("gpsimd"))
        block.sync(section("sync"))


def build(n_chunks=CH, taps=()):
    nc = bass.Bass("TRN2", target_bir_lowering=False)
    with contextlib.ExitStack() as stack:
        kb = KB(nc, stack, taps=taps, n_chunks=n_chunks)
        block = stack.enter_context(nc.Block())
        kb.emit_all()
        kb.finish()
        kb.replay(block)
    return nc


def _host_blocks(ins):
    W_in = np.asarray(ins["L0_attn_W_in"], np.float32)
    conv_w = np.asarray(ins["L0_attn_conv_w"], np.float32)
    W_x = np.asarray(ins["L0_attn_W_x"], np.float32)
    W_dt = np.asarray(ins["L0_attn_W_dt"], np.float32)
    W_out = np.asarray(ins["L0_attn_W_out"], np.float32)
    Wg = np.asarray(ins["L0_ffn_Wg"], np.float32)
    We = np.asarray(ins["L0_ffn_W"], np.float32)
    be = np.asarray(ins["L0_ffn_b"], np.float32)

    wf = np.zeros((P, FT), np.float32)

    def put(name, arr):
        o = BOFF[name]
        wf[:, o:o + arr.shape[1]] = arr

    put("eye", np.eye(P, dtype=np.float32))
    r16 = np.zeros((P, 2048), np.float32)
    r16[0:16] = np.kron(np.eye(16, dtype=np.float32), np.ones((1, P), np.float32))
    put("r16", r16)
    r8 = np.zeros((P, 1024), np.float32)
    r8[0:8] = np.kron(np.eye(8, dtype=np.float32), np.ones((1, P), np.float32))
    put("r8", r8)
    bep = np.zeros((P, 1024), np.float32)
    bep[0:8] = be
    put("be", bep)
    wp = []
    for mg in [0, 4, 1, 5, 2, 6, 3, 7]:
        for k in range(KD):
            wp.append(W_in[k * P:(k + 1) * P, mg * 512:(mg + 1) * 512])
    put("win", np.concatenate(wp, axis=1))
    cp = []
    for g in range(KI):
        for kk in range(4):
            cp.append(np.diag(conv_w[g * P:(g + 1) * P, 0, kk]).astype(np.float32))
    put("conv", np.concatenate(cp, axis=1))
    xp = [W_x[k * P:(k + 1) * P, DS:2 * DS] for k in range(KI)]
    put("wx", np.concatenate(xp, axis=1))
    dp = []
    for g in range(KI):
        for k in range(KI):
            dp.append(W_dt[k * P:(k + 1) * P, g * P:(g + 1) * P])
    put("wdt", np.concatenate(dp, axis=1))
    op_ = []
    for mp in range(4):
        for k in range(KI):
            op_.append(W_out[k * P:(k + 1) * P, mp * 256:(mp + 1) * 256])
    put("wout", np.concatenate(op_, axis=1))
    gp = [np.ascontiguousarray(Wg[:, k * P:(k + 1) * P].T) for k in range(KD)]
    put("wg", np.concatenate(gp, axis=1))
    ep = []
    for e in range(E):
        WeT = We[e].T
        for k in range(KD):
            ep.append(np.ascontiguousarray(WeT[k * P:(k + 1) * P, :]))
    put("we", np.concatenate(ep, axis=1))

    b_in = np.asarray(ins["L0_attn_b_in"], np.float32)
    b_dt = np.asarray(ins["L0_attn_b_dt"], np.float32)
    b_out = np.asarray(ins["L0_attn_b_out"], np.float32)
    Dv = np.asarray(ins["L0_attn_D"], np.float32)
    conv_b = np.asarray(ins["L0_attn_conv_b"], np.float32)
    A_log = np.asarray(ins["L0_attn_A_log"], np.float32)
    b_x = np.asarray(ins["L0_attn_b_x"], np.float32)
    bg = np.asarray(ins["L0_ffn_bg"], np.float32)

    bias = {
        "binr": np.ascontiguousarray(b_in.reshape(32, P).T),
        "bdtr": np.ascontiguousarray(b_dt.reshape(16, P).T),
        "boutr": np.ascontiguousarray(b_out.reshape(8, P).T),
        "dr": np.ascontiguousarray(Dv.reshape(16, P).T),
        "convbr": np.ascontiguousarray(conv_b.reshape(16, P).T),
        "alogr": np.ascontiguousarray(
            A_log.reshape(16, P, 16).transpose(1, 0, 2).reshape(P, 256)),
        "bx": np.zeros((P, 1), np.float32),
        "bgrow": np.ascontiguousarray(bg.reshape(1, 8)),
    }
    bias["bx"][0:16, 0] = b_x[DS:2 * DS]
    return wf, bias


def _host_x(xb):
    return np.ascontiguousarray(
        xb.T.reshape(KD, P, T).transpose(1, 0, 2)).astype(np.float32)


def _host_out(o):
    return np.ascontiguousarray(
        np.asarray(o).transpose(1, 0, 2).reshape(D, T).T)


def kernel(**inputs):
    x = np.asarray(inputs["L0_attn_x"], np.float32)
    wf, bias = _host_blocks(inputs)
    nc = build()
    in_maps = []
    for b in range(8):
        m = {"x_in": _host_x(x[b]), "wf32": wf}
        for k, v in bias.items():
            m["bias_" + k] = v
        in_maps.append(m)
    res = run_bass_kernel_spmd(nc, in_maps, core_ids=list(range(8)))
    out = np.stack([_host_out(r["out"]) for r in res.results], axis=0)
    return out


# version 09
# speedup vs baseline: 1.0070x; 1.0056x over previous
import contextlib
import numpy as np

import concourse.bass as bass
import concourse.mybir as mybir
from concourse.bass_utils import run_bass_kernel_spmd

F32 = mybir.dt.float32
BF16 = mybir.dt.bfloat16
AF = mybir.ActivationFunctionType
ALU = mybir.AluOpType
AXX = mybir.AxisListType.X

P = 128
T = 4096
D = 1024
DI = 2048
DS = 16
E = 8
CH = 8
TT = 512
KD = D // P
KI = DI // P
SBLK = 514
SG = 2
NSG = DS // SG
SCF = SG * SBLK

ENGS = ("tensor", "vector", "scalar", "gpsimd", "sync")
SEMS = ENGS + ("dmaS", "dmaA")

BLK = [
    ("eye", 128),
    ("r16", 2048),
    ("r8", 1024),
    ("be", 1024),
    ("win", 64 * 512),
    ("conv", 64 * 128),
    ("wx", 16 * 16),
    ("wdt", 256 * 128),
    ("wout", 64 * 256),
    ("wg", 8 * 8),
    ("we", 64 * 1024),
]
BOFF = {}
_o = 0
for _n, _c in BLK:
    BOFF[_n] = _o
    _o += _c
FT = _o


class Buf:
    __slots__ = ("w", "rs")

    def __init__(self):
        self.w = None
        self.rs = []

    def r(self):
        return [self.w] if self.w is not None else []

    def wkill(self):
        out = list(self.rs)
        if self.w is not None:
            out.append(self.w)
        return out


class Prog:
    def __init__(self):
        self.ops = []
        self.cnt = {s: 0 for s in SEMS}
        self.maxw = {}

    def emit(self, eng, fn, waits=(), inc_sem=None, inc=1, tag="", hard=()):
        if inc_sem is None:
            inc_sem = eng
        seen = {}
        for (s, v) in waits:
            if s == eng and s in ENGS:
                continue
            if v > seen.get(s, -1):
                seen[s] = v
        for (s, v) in hard:
            if v > seen.get(s, -1):
                seen[s] = v
        ws = []
        for s, v in seen.items():
            k = (eng, s)
            if self.maxw.get(k, -1) >= v:
                continue
            self.maxw[k] = v
            ws.append((s, v))
        tok = None
        if inc:
            self.cnt[inc_sem] += inc
            tok = (inc_sem, self.cnt[inc_sem])
        self.ops.append((eng, fn, ws, inc_sem, inc, tag))
        return tok


class KB:

    def __init__(self, nc, stack, taps=(), n_chunks=CH):
        self.nc = nc
        self.n_chunks = n_chunks
        self.st = stack
        self.pr = Prog()
        self.taps_req = set(taps)
        self.taps = {}
        self.d_x = nc.dram_tensor("x_in", [P, KD, T], F32, kind="ExternalInput").ap()
        self.d_wf = nc.dram_tensor("wf32", [P, FT], F32, kind="ExternalInput").ap()
        self.d_w16 = nc.dram_tensor("w16", [P, FT], BF16, kind="Internal").ap()
        self.d_out = nc.dram_tensor("out", [P, KD, T], F32, kind="ExternalOutput").ap()
        bias_specs = [
            ("binr", [P, 32]), ("bdtr", [P, 16]), ("boutr", [P, 8]),
            ("dr", [P, 16]), ("convbr", [P, 16]), ("alogr", [P, 256]),
            ("bx", [P, 1]), ("bgrow", [1, 8]),
        ]
        self.d_bias = {n: nc.dram_tensor("bias_" + n, s, F32, kind="ExternalInput").ap()
                       for n, s in bias_specs}
        self.sems = {}
        self.alloc()

    def sb(self, name, shape, dt):
        t = self.st.enter_context(self.nc.sbuf_tensor(name, shape, dt))
        return t

    def alloc(self):
        sb = self.sb
        st = self.st
        self.c_binr = sb("c_binr", [P, 32], F32)
        self.c_bdtr = sb("c_bdtr", [P, 16], F32)
        self.c_boutr = sb("c_boutr", [P, 8], F32)
        self.c_dr = sb("c_dr", [P, 16], F32)
        self.c_convbr = sb("c_convbr", [P, 16], F32)
        self.c_A = sb("c_A", [P, 256], F32)
        self.c_bx = sb("c_bx", [P, 1], F32)
        self.c_bgrow = sb("c_bgrow", [1, 8], F32)
        self.c_bgrow16 = sb("c_bgrow16", [1, 8], BF16)
        self.c_bgb = sb("c_bgb", [P, 8], F32)
        self.c_eye16 = sb("c_eye16", [P, P], BF16)
        self.c_r16 = sb("c_r16", [16, 2048], BF16)
        self.c_be16 = sb("c_be16", [8, 1024], BF16)
        self.c_ones1 = sb("c_ones1", [1, P], BF16)
        self.c_onesc = sb("c_onesc", [P, 1], BF16)
        self.xch = sb("xch", [P, KD, TT], F32)
        self.scr16 = sb("scr16", [P, KD, TT], BF16)
        self.xn1 = sb("xn1", [P, KD, TT], BF16)
        self.xn2 = sb("xn2", [P, KD, TT], BF16)
        self.u16 = [sb(f"u16_{i}", [P, KI, TT], BF16) for i in range(2)]
        self.sz16 = sb("sz16", [P, KI, TT], BF16)
        self.ball = sb("ball", [P, DS, TT], BF16)
        self.yrhs = sb("yrhs", [P, KI, TT], BF16)
        self.xcv = [sb(f"xcv{i}", [P, 516], BF16) for i in range(2)]
        self.tails = sb("tails", [P, 16, 4], BF16)
        self.dA = [sb(f"dA{i}", [P, SCF], F32) for i in range(2)]
        self.dBu = [sb(f"dBu{i}", [P, SCF], BF16) for i in range(2)]
        self.hfull = sb("hfull", [P, DS, SBLK], BF16)
        self.carry = sb("carry", [P, 256], BF16)
        self.sp1 = sb("sp1", [P, TT], F32)
        self.sp2 = sb("sp2", [P, TT], F32)
        self.sp3 = sb("sp3", [P, TT], F32)
        self.dful = sb("dful", [P, 8, TT], BF16)
        self.v16 = [sb(f"v16_{i}", [P, TT], BF16) for i in range(2)]
        self.ytmp = sb("ytmp", [P, TT], BF16)
        self.st_a = sb("st_a", [1, TT], F32)[:]
        self.st_b = sb("st_b", [1, TT], F32)[:]
        self.st_c = sb("st_c", [1, TT], F32)[:]
        self.mrow16 = sb("mrow16", [1, TT], BF16)
        self.rrow16 = sb("rrow16", [1, TT], BF16)
        self.mb16 = sb("mb16", [P, TT], BF16)
        self.rb16 = sb("rb16", [P, TT], BF16)
        self.gtm = sb("gtm", [P, 4, 8], F32)
        self.ge = sb("ge", [P, 8], F32)
        self.gm = sb("gm", [P, 8], F32)
        self.gs = sb("gs", [P, 8], F32)
        self.wtm16 = sb("wtm16", [P, 4, 8], BF16)
        self.sbW8 = sb("sbW8", [8, TT], BF16)
        self.sbB16 = sb("sbB16", [16, TT], BF16)
        self.streams = {}
        for name, ppc, pc, bp, rb in [
            ("win", 64, 512, 2, 2),
            ("conv", 64, 128, 4, 2),
            ("wx", 16, 16, 16, 1),
            ("wdt", 256, 128, 8, 2),
            ("wout", 64, 256, 4, 2),
            ("wg", 8, 8, 8, 1),
            ("we", 64, 1024, 1, 2),
        ]:
            arena = sb("ar_" + name, [P, rb * bp * pc], BF16)
            self.streams[name] = dict(
                name=name, ppc=ppc, pc=pc, bp=bp, rb=rb, arena=arena,
                slots=[Buf() for _ in range(rb)], gb=0, base=0,
                cast_tok=None)
        self.pbank = [self.st.enter_context(
            self.nc.psum_tensor(f"pb{i}", [P, 512], F32)) for i in range(8)]
        self.bankbuf = [Buf() for _ in range(8)]
        for s in SEMS:
            self.sems[s] = st.enter_context(self.nc.semaphore("sem_" + s))
        self.b_xch = Buf()
        self.b_scr16 = Buf()
        self.b_xn1 = Buf()
        self.b_xn2 = Buf()
        self.b_u = [Buf(), Buf()]
        self.b_sz = [Buf() for _ in range(KI)]
        self.b_ball = [Buf() for _ in range(NSG)]
        self.b_yrhs = Buf()
        self.b_xcv = [Buf(), Buf()]
        self.b_tails = Buf()
        self.b_dA = [Buf(), Buf()]
        self.b_dBu = [Buf(), Buf()]
        self.b_hfull = Buf()
        self.b_carry = Buf()
        self.b_sp = Buf()
        self.b_dful = [Buf() for _ in range(8)]
        self.b_v = [Buf(), Buf()]
        self.b_ytmp = Buf()
        self.b_ln = Buf()
        self.b_bcast = Buf()
        self.b_gate = Buf()
        self.b_sbB = Buf()
        self.b_sbW8 = Buf()
        self.b_wb = Buf()

    def op(self, eng, fn, reads=(), writes=(), waits=(), inc_sem=None, inc=1,
           tag="", hard=()):
        ws = list(waits)
        bc = getattr(self, "b_consts", None)
        if bc is not None and bc.w is not None:
            ws += bc.r()
        for b in reads:
            ws += b.r()
        for b in writes:
            ws += b.wkill()
        t = self.pr.emit(eng, fn, ws, inc_sem, inc, tag, hard=hard)
        if t is not None:
            for b in reads:
                b.rs.append(t)
            for b in writes:
                b.w = t
                b.rs = []
        return t

    def dma(self, queue_eng, sem, out_ap, in_ap, reads=(), writes=(),
            waits=(), tag=""):
        fn = (lambda e, o=out_ap, i=in_ap: e.dma_start(out=o, in_=i))
        return self.op(queue_eng, fn, reads=reads, writes=writes, waits=waits,
                       inc_sem=sem, inc=16, tag=tag)

    def mm_seq(self, items, bank_bufs, tag=""):
        n = len(items)
        banks = sorted({b for _, _, b in items})
        tok = None
        first_by_bank = {b: True for b in banks}
        bc = getattr(self, "b_consts", None)
        for i, (fn, rbs, bi) in enumerate(items):
            ws = []
            if i == 0 and bc is not None and bc.w is not None:
                ws += bc.r()
            for b in rbs:
                ws += b.r()
            if first_by_bank.get(bi):
                ws += bank_bufs[bi].wkill()
                first_by_bank[bi] = False
            last = i == n - 1
            t = self.pr.emit("tensor", fn, ws, None, 1 if last else 0, tag)
            if last:
                tok = t
        for _, rbs, _bi in items:
            for b in rbs:
                if tok not in b.rs:
                    b.rs.append(tok)
        for b in banks:
            bank_bufs[b].w = tok
            bank_bufs[b].rs = []
        return tok

    def tap(self, name, ap):
        if name not in self.taps_req:
            return
        shape = list(ap.shape)
        d = self.nc.dram_tensor("tap_" + name, shape, ap.dtype,
                                kind="ExternalOutput").ap()
        self.taps[name] = name
        waits = [(s, self.pr.cnt[s]) for s in ("tensor", "vector", "scalar",
                                               "gpsimd")]
        self.dma("scalar", "dmaA", d, ap, waits=waits, tag="tap_" + name)

    def set_stream_chunk(self, name, c):
        st = self.streams[name]
        st["chunk_base"] = c * (st["ppc"] // st["bp"])

    def _ensure(self, name, upto):
        s = self.streams[name]
        nb = s["ppc"] // s["bp"]
        cols = s["bp"] * s["pc"]
        while s["gb"] < upto:
            b = s["gb"]
            slot = b % s["rb"]
            src = BOFF[name] + (b % nb) * cols
            waits = []
            if b < nb and s.get("cast_tok"):
                waits.append(s["cast_tok"])
            self.dma("sync", "dmaS",
                     s["arena"][:, slot * cols:(slot + 1) * cols],
                     self.d_w16[:, src:src + cols],
                     writes=[s["slots"][slot]], waits=waits,
                     tag=f"w_{name}_{b}")
            s["gb"] += 1

    def piece(self, name, j):
        s = self.streams[name]
        gb = s["chunk_base"] + j // s["bp"]
        self._ensure(name, gb + 1)
        slot = gb % s["rb"]
        cols = s["bp"] * s["pc"]
        off = slot * cols + (j % s["bp"]) * s["pc"]
        return s["arena"][:, off:off + s["pc"]], s["slots"][slot]

    def prologue(self):
        nc = self.nc
        self.b_consts = Buf()
        BC = self.b_consts

        for n, dst in [("binr", self.c_binr), ("bdtr", self.c_bdtr),
                       ("boutr", self.c_boutr), ("dr", self.c_dr),
                       ("convbr", self.c_convbr), ("alogr", self.c_A),
                       ("bx", self.c_bx), ("bgrow", self.c_bgrow)]:
            self.dma("scalar", "dmaA", dst[:], self.d_bias[n],
                     writes=[BC], tag="bias_" + n)
        self.op("scalar", lambda e: e.activation(
            out=self.c_A[:], in_=self.c_A[:], func=AF.Exp),
            writes=[BC], tag="A_exp")
        self.op("vector", lambda e: e.tensor_scalar(
            out=self.c_A[:], in0=self.c_A[:], scalar1=-1.0, scalar2=None,
            op0=ALU.mult), writes=[BC], tag="A_neg")
        self.op("vector", lambda e: e.memset(self.c_ones1[:], 1.0),
                writes=[BC])
        self.op("vector", lambda e: e.memset(self.c_onesc[:], 1.0),
                writes=[BC])
        self.op("vector", lambda e: e.tensor_copy(
            out=self.c_bgrow16[:], in_=self.c_bgrow[:]), writes=[BC])
        self.mm_seq([(lambda e: e.matmul(
            out=self.pbank[7][:, 0:8], lhsT=self.c_ones1[:],
            rhs=self.c_bgrow16[:], start=True, stop=True), [BC], 7)],
            self.bankbuf, tag="bgb_mm")
        self.op("vector", lambda e: e.tensor_copy(
            out=self.c_bgb[:], in_=self.pbank[7][:, 0:8]),
            reads=[self.bankbuf[7]], writes=[BC], tag="bgb_ev")
        for par in range(2):
            da3 = self.dA[par][:].rearrange("p (s c) -> p s c", s=SG)
            db3 = self.dBu[par][:].rearrange("p (s c) -> p s c", s=SG)
            self.op("vector", lambda e, a=da3: e.memset(a[:, :, 0:1], 0.0),
                    writes=[self.b_dA[par]])
            self.op("vector", lambda e, a=da3: e.memset(a[:, :, 1:2], 1.0),
                    writes=[self.b_dA[par]])
            self.op("vector", lambda e, a=db3: e.memset(a[:, :, 1:2], 0.0),
                    writes=[self.b_dBu[par]])
        self.op("vector", lambda e: e.memset(self.carry[:], 0.0),
                writes=[self.b_carry])
        self.op("gpsimd", lambda e: e.memset(self.tails[:], 0.0),
                writes=[self.b_tails])
        xin = self.xch[:].rearrange("p k t -> p (k t)")
        fout = self.dful[:].rearrange("p k t -> p (k t)")
        b_in = [Buf() for _ in range(4)]
        b_out = [Buf() for _ in range(4)]
        pcw = 1024
        n_pieces = (FT + pcw - 1) // pcw
        blk_tok = {}
        blk_end = {n: BOFF[n] + c for n, c in BLK}
        for i in range(n_pieces):
            lo = i * pcw
            hi = min(FT, lo + pcw)
            w = hi - lo
            sl = i % 4
            s_in = xin[:, sl * pcw:sl * pcw + w]
            s_out = fout[:, sl * pcw:sl * pcw + w]
            qi = ("sync", "dmaS") if i % 2 == 0 else ("scalar", "dmaA")
            qo = ("scalar", "dmaA") if i % 2 == 0 else ("sync", "dmaS")
            self.dma(qi[0], qi[1], s_in, self.d_wf[:, lo:hi],
                     writes=[b_in[sl]], tag=f"cast_in{i}")
            self.op("vector", lambda e, o=s_out, s=s_in: e.tensor_copy(
                out=o, in_=s), reads=[b_in[sl]], writes=[b_out[sl]],
                tag=f"cast{i}")
            t = self.dma(qo[0], qo[1], self.d_w16[:, lo:hi], s_out,
                         reads=[b_out[sl]], tag=f"cast_out{i}")
            for n, endc in blk_end.items():
                if lo < endc <= hi:
                    blk_tok[n] = t
        self.blk_tok = blk_tok
        for name in self.streams:
            self.streams[name]["cast_tok"] = blk_tok[name]
        last_in = []
        for b in b_in:
            last_in += b.wkill()
        self.b_xch.rs = list(last_in)
        for ds in range(8):
            self.b_dful[ds].rs = list(b_out[ds // 2].wkill())
        self.dma("sync", "dmaS", self.c_eye16[:],
                 self.d_w16[:, BOFF["eye"]:BOFF["eye"] + 128],
                 waits=[blk_tok["eye"]], writes=[self.b_consts],
                 tag="ld_eye")
        self.dma("sync", "dmaS", self.c_r16[:],
                 self.d_w16[0:16, BOFF["r16"]:BOFF["r16"] + 2048],
                 waits=[blk_tok["r16"]], writes=[self.b_consts],
                 tag="ld_r16")
        self.dma("sync", "dmaS", self.c_be16[:],
                 self.d_w16[0:8, BOFF["be"]:BOFF["be"] + 1024],
                 waits=[blk_tok["be"]], writes=[self.b_consts],
                 tag="ld_be")

    def gen_ln(self, src_f32_3d, src_buf, dst16, b_dst, xsq, b_xsq,
               bk0, bk1, tag):
        op = self.op
        yield 0
        op("scalar", lambda e: e.activation(
            out=self.scr16[:], in_=src_f32_3d, func=AF.Identity),
            reads=[src_buf], writes=[self.b_scr16], tag=tag + "_cast")
        yield 0
        op("vector", lambda e, o=xsq: e.tensor_tensor(
            out=o, in0=self.scr16[:], in1=self.scr16[:], op=ALU.mult),
            reads=[self.b_scr16], writes=[b_xsq], tag=tag + "_sq")
        yield 0
        items = []
        for k in range(KD):
            items.append((lambda e, k=k: e.matmul(
                out=self.pbank[bk0][0:1, :], lhsT=self.c_onesc[:],
                rhs=self.scr16[:, k, :], start=(k == 0), stop=(k == KD - 1)),
                [self.b_scr16], bk0))
        for k in range(KD):
            items.append((lambda e, k=k, x=xsq: e.matmul(
                out=self.pbank[bk1][0:1, :], lhsT=self.c_onesc[:],
                rhs=x[:, k, :], start=(k == 0), stop=(k == KD - 1)),
                [b_xsq], bk1))
        self.mm_seq(items, self.bankbuf, tag=tag + "_stats")
        yield 0
        op("vector", lambda e: e.tensor_scalar(
            out=self.st_a, in0=self.pbank[bk0][0:1, :], scalar1=1.0 / D,
            scalar2=None, op0=ALU.mult), reads=[self.bankbuf[bk0]],
            writes=[self.b_ln], tag=tag + "_mean")
        op("vector", lambda e: e.tensor_scalar(
            out=self.st_b, in0=self.pbank[bk1][0:1, :], scalar1=1.0 / D,
            scalar2=None, op0=ALU.mult), reads=[self.bankbuf[bk1]],
            writes=[self.b_ln], tag=tag + "_msq")
        op("vector", lambda e: e.tensor_tensor(
            out=self.st_c, in0=self.st_a, in1=self.st_a,
            op=ALU.mult), writes=[self.b_ln], tag=tag + "_m2")
        op("vector", lambda e: e.tensor_tensor(
            out=self.st_b, in0=self.st_b, in1=self.st_c,
            op=ALU.subtract), writes=[self.b_ln], tag=tag + "_var")
        yield 0
        op("scalar", lambda e: e.activation(
            out=self.st_c, in_=self.st_b, func=AF.Ln, bias=np.float32(1e-5)),
            writes=[self.b_ln], tag=tag + "_lnv")
        op("scalar", lambda e: e.activation(
            out=self.st_b, in_=self.st_c, func=AF.Exp, scale=np.float32(-0.5)),
            writes=[self.b_ln], tag=tag + "_rstd")
        op("vector", lambda e: e.tensor_copy(out=self.mrow16[:],
                                             in_=self.st_a),
           writes=[self.b_ln], tag=tag + "_m16")
        op("vector", lambda e: e.tensor_copy(out=self.rrow16[:],
                                             in_=self.st_b),
           writes=[self.b_ln], tag=tag + "_r16c")
        yield 0
        items = [
            (lambda e: e.matmul(out=self.pbank[bk0][:, :],
                                lhsT=self.c_ones1[:],
                                rhs=self.mrow16[:], start=True, stop=True),
             [self.b_ln], bk0),
            (lambda e: e.matmul(out=self.pbank[bk1][:, :],
                                lhsT=self.c_ones1[:],
                                rhs=self.rrow16[:], start=True, stop=True),
             [self.b_ln], bk1),
        ]
        self.mm_seq(items, self.bankbuf, tag=tag + "_bc")
        op("scalar", lambda e: e.activation(
            out=self.mb16[:], in_=self.pbank[bk0][:, :], func=AF.Identity),
            reads=[self.bankbuf[bk0]], writes=[self.b_bcast],
            tag=tag + "_mbev")
        op("scalar", lambda e: e.activation(
            out=self.rb16[:], in_=self.pbank[bk1][:, :], func=AF.Identity),
            reads=[self.bankbuf[bk1]], writes=[self.b_bcast],
            tag=tag + "_rbev")
        yield 0
        for k in range(KD):
            op("vector", lambda e, k=k: e.tensor_tensor(
                out=dst16[:, k, :],
                in0=self.scr16[:, k, :], in1=self.mb16[:], op=ALU.subtract),
                reads=[self.b_scr16, self.b_bcast], writes=[b_dst],
                tag=tag + f"_sub{k}")
            op("vector", lambda e, k=k: e.tensor_tensor(
                out=dst16[:, k, :], in0=dst16[:, k, :], in1=self.rb16[:],
                op=ALU.mult), reads=[self.b_bcast], writes=[b_dst],
                tag=tag + f"_mul{k}")
            if k % 3 == 2:
                yield 0

    def gen_front(self, c):
        if c >= self.n_chunks:
            return
        op = self.op
        for nm in ("win", "conv", "wx"):
            self.set_stream_chunk(nm, c)
        upar = c % 2
        u16 = self.u16[upar]
        b_u = self.b_u[upar]
        yield 40
        self.dma("scalar", "dmaA", self.xch[:],
                 self.d_x[:, :, c * TT:(c + 1) * TT],
                 writes=[self.b_xch], tag=f"x{c}")
        for ms in self.gen_ln(self.xch[:], self.b_xch, self.xn1, self.b_xn1,
                              self.xn2[:], self.b_xn2, 6, 7, f"ln1_{c}"):
            yield max(ms, 42)
        if c == 0:
            self.tap("xn1", self.xn1[:])
        WSEQ = [0, 4, 1, 5, 2, 6, 3, 7]
        for oi, mg in enumerate(WSEQ):
            is_z = mg >= 4
            yield 46
            for kb in range(4):
                items = []
                for k in (2 * kb, 2 * kb + 1):
                    pap, pslot = self.piece("win", oi * 8 + k)
                    for m in range(4):
                        items.append((lambda e, pap=pap, m=m, k=k: e.matmul(
                            out=self.pbank[m][:, :],
                            lhsT=pap[:, m * P:(m + 1) * P],
                            rhs=self.xn1[:, k, :],
                            start=(k == 0), stop=(k == KD - 1)),
                            [pslot, self.b_xn1], m))
                self.mm_seq(items, self.bankbuf, tag=f"win{c}_{mg}_{kb}")
            for m in range(4):
                j = mg * 4 + m
                if j < KI:
                    g = j
                    par = g % 2
                    yield 46
                    op("vector", lambda e, g=g, par=par: e.tensor_copy(
                        out=self.xcv[par][:, 0:3], in_=self.tails[:, g, 0:3]),
                        reads=[self.b_tails], writes=[self.b_xcv[par]],
                        tag=f"tin{c}_{g}")
                    op("scalar", lambda e, m=m, g=g, par=par: e.activation(
                        out=self.xcv[par][:, 3:515], in_=self.pbank[m][:, :],
                        func=AF.Identity, bias=self.c_binr[:, g:g + 1]),
                        reads=[self.bankbuf[m]], writes=[self.b_xcv[par]],
                        tag=f"xhev{c}_{g}")
                    citems = []
                    for kk in range(4):
                        pap, pslot = self.piece("conv", g * 4 + kk)
                        citems.append((lambda e, pap=pap, kk=kk, par=par, m=m:
                                       e.matmul(
                                           out=self.pbank[m][:, :], lhsT=pap,
                                           rhs=self.xcv[par][:, kk:kk + 512],
                                           start=(kk == 0), stop=(kk == 3)),
                                       [pslot, self.b_xcv[par]], m))
                    self.mm_seq(citems, self.bankbuf, tag=f"conv{c}_{g}")
                    op("vector", lambda e, g=g, par=par: e.tensor_copy(
                        out=self.tails[:, g, 0:3],
                        in_=self.xcv[par][:, 512:515]),
                        reads=[self.b_xcv[par]], writes=[self.b_tails],
                        tag=f"tout{c}_{g}")
                    op("scalar", lambda e, g=g, m=m: e.activation(
                        out=u16[:, g, :], in_=self.pbank[m][:, :],
                        func=AF.Silu, bias=self.c_convbr[:, g:g + 1]),
                        reads=[self.bankbuf[m]], writes=[b_u],
                        tag=f"uev{c}_{g}")
                else:
                    zi = j - KI
                    yield 8 * zi + 10
                    op("scalar", lambda e, m=m, zi=zi, j=j: e.activation(
                        out=self.sz16[:, zi, :], in_=self.pbank[m][:, :],
                        func=AF.Silu, bias=self.c_binr[:, j:j + 1]),
                        reads=[self.bankbuf[m]], writes=[self.b_sz[zi]],
                        tag=f"szev{c}_{zi}")
        if c == 0:
            self.tap("u", u16[:])
            self.tap("sz", self.sz16[:])
        yield 80
        items = []
        for k in range(KI):
            pap, pslot = self.piece("wx", k)
            items.append((lambda e, pap=pap, k=k: e.matmul(
                out=self.pbank[7][0:16, :], lhsT=pap, rhs=u16[:, k, :],
                start=(k == 0), stop=(k == KI - 1)), [pslot, b_u], 7))
        self.mm_seq(items, self.bankbuf, tag=f"wxB{c}")
        op("vector", lambda e: e.tensor_scalar(
            out=self.sbB16[:], in0=self.pbank[7][0:16, :],
            scalar1=self.c_bx[0:16, :], scalar2=None, op0=ALU.add),
            reads=[self.bankbuf[7]], writes=[self.b_sbB], tag=f"Bev{c}")
        if c == 0:
            self.tap("B", self.sbB16[:])

    def gen_ball(self, c):
        if c >= self.n_chunks:
            return
        op = self.op
        for sb in range(NSG):
            yield max(0, sb - 2)
            for si in (2 * sb, 2 * sb + 1):
                bb = 6 + (si % 2)
                self.mm_seq([(lambda e, si=si, bb=bb: e.matmul(
                    out=self.pbank[bb][:, :],
                    lhsT=self.c_r16[:, si * P:(si + 1) * P],
                    rhs=self.sbB16[:],
                    start=True, stop=True),
                    [self.b_sbB], bb)],
                    self.bankbuf, tag=f"bb{c}_{si}")
                op("scalar", lambda e, si=si, bb=bb, sb=sb: e.activation(
                    out=self.ball[:, si, :], in_=self.pbank[bb][:, :],
                    func=AF.Identity), reads=[self.bankbuf[bb]],
                    writes=[self.b_ball[sb]], tag=f"bbev{c}_{si}")

    def gen_wdt(self, c):
        op = self.op
        self.set_stream_chunk("wdt", c)
        upar = c % 2
        u16 = self.u16[upar]
        b_u = self.b_u[upar]
        wsl = [0, 0, 1, 1, 2, 2, 41, 44, 47, 50, 53, 56, 59, 62, 65, 68]
        for g in range(KI):
            yield wsl[g]
            bank = 6 if g == 0 else (7 if g == 1 else 4 + (g % 2))
            items = []
            for k in range(KI):
                pap, pslot = self.piece("wdt", g * 16 + k)
                items.append((lambda e, pap=pap, k=k, bank=bank: e.matmul(
                    out=self.pbank[bank][:, :], lhsT=pap,
                    rhs=u16[:, k, :], start=(k == 0),
                    stop=(k == KI - 1)), [pslot, b_u], bank))
            self.mm_seq(items, self.bankbuf, tag=f"wdt{c}_{g}")
            op("scalar", lambda e, g=g, bank=bank: e.activation(
                out=self.sp1[:], in_=self.pbank[bank][:, :], func=AF.Abs,
                bias=self.c_bdtr[:, g:g + 1]), reads=[self.bankbuf[bank]],
                writes=[self.b_sp], tag=f"spabs{c}_{g}")
            op("scalar", lambda e, g=g, bank=bank: e.activation(
                out=self.sp2[:], in_=self.pbank[bank][:, :], func=AF.Relu,
                bias=self.c_bdtr[:, g:g + 1]), reads=[self.bankbuf[bank]],
                writes=[self.b_sp], tag=f"sprelu{c}_{g}")
            op("scalar", lambda e: e.activation(
                out=self.sp3[:], in_=self.sp1[:], func=AF.Exp,
                scale=np.float32(-1.0)),
                writes=[self.b_sp], tag=f"spexp{c}_{g}")
            op("scalar", lambda e: e.activation(
                out=self.sp1[:], in_=self.sp3[:], func=AF.Ln,
                bias=np.float32(1.0)),
                writes=[self.b_sp], tag=f"spln{c}_{g}")
            ds = g % 8
            op("vector", lambda e, ds=ds: e.tensor_tensor(
                out=self.dful[:, ds, :], in0=self.sp2[:], in1=self.sp1[:],
                op=ALU.add), reads=[self.b_sp], writes=[self.b_dful[ds]],
                tag=f"delta{c}_{g}")
            if c == 0 and g == 0:
                self.tap("delta_g0", self.dful[:, 0, :])

    def gen_tail(self, c):
        if c < 0:
            return
        op = self.op
        for nm in ("wout", "wg", "we"):
            self.set_stream_chunk(nm, c)
        wbu = self.xn1
        b_wb = self.b_xn1
        yield 0
        self.dma("scalar", "dmaA", self.xch[:],
                 self.d_x[:, :, c * TT:(c + 1) * TT],
                 writes=[self.b_xch], tag=f"xr{c}")
        for mp in range(4):
            b0 = (mp % 2) * 2
            yield 0
            for kq in range(4):
                items = []
                for k in range(4 * kq, 4 * kq + 4):
                    pap, pslot = self.piece("wout", mp * 16 + k)
                    for mm_i in range(2):
                        items.append((lambda e, pap=pap, mm_i=mm_i, k=k,
                                      b0=b0: e.matmul(
                            out=self.pbank[b0 + mm_i][:, :],
                            lhsT=pap[:, mm_i * P:(mm_i + 1) * P],
                            rhs=self.yrhs[:, k, :], start=(k == 0),
                            stop=(k == KI - 1)), [pslot, self.b_yrhs],
                            b0 + mm_i))
                self.mm_seq(items, self.bankbuf, tag=f"wout{c}_{mp}_{kq}")
            for mm_i in range(2):
                m = mp * 2 + mm_i
                bb_ = b0 + mm_i
                op("vector", lambda e, m=m, bb_=bb_:
                   e.scalar_tensor_tensor(
                    out=self.xch[:, m, :], in0=self.pbank[bb_][:, :],
                    scalar=self.c_boutr[:, m:m + 1], in1=self.xch[:, m, :],
                    op0=ALU.add, op1=ALU.add),
                    reads=[self.bankbuf[bb_]], writes=[self.b_xch],
                    tag=f"res1{c}_{m}")
        if c == 0:
            self.tap("resid1", self.xch[:])
        for ms in self.gen_ln(self.xch[:], self.b_xch, self.xn2, self.b_xn2,
                              self.xn1[:], self.b_xn1, 0, 1, f"ln2_{c}"):
            yield ms
        if c == 0:
            self.tap("xn2", self.xn2[:])
        yield 0
        items = []
        for j in range(4):
            for k in range(KD):
                pap, pslot = self.piece("wg", k)
                items.append((lambda e, pap=pap, j=j, k=k: e.matmul(
                    out=self.pbank[2][:, j * 8:(j + 1) * 8],
                    lhsT=self.xn2[:, k, j * P:(j + 1) * P], rhs=pap,
                    start=(k == 0), stop=(k == KD - 1)),
                    [pslot, self.b_xn2], 2))
        self.mm_seq(items, self.bankbuf, tag=f"glog{c}")
        for j in range(4):
            op("vector", lambda e, j=j: e.tensor_tensor(
                out=self.gtm[:, j, :],
                in0=self.pbank[2][:, j * 8:(j + 1) * 8],
                in1=self.c_bgb[:], op=ALU.add),
                reads=[self.bankbuf[2]], writes=[self.b_gate],
                tag=f"gev{c}_{j}")
        if c == 0:
            self.tap("glog", self.gtm[:])
        yield 0
        for j in range(4):
            gj = self.gtm[:, j, :]
            pv = op("vector", lambda e, gj=gj: e.reduce_max(
                out=self.gs[:, 0:1], in_=gj, axis=AXX),
                reads=[self.b_gate], writes=[self.b_gate], tag=f"m1_{c}{j}")
            pv = op("vector", lambda e: e.tensor_scalar(
                out=self.gs[:, 1:2], in0=self.gs[:, 0:1], scalar1=-1.0,
                scalar2=None, op0=ALU.mult), writes=[self.b_gate], hard=[pv])
            pv = op("scalar", lambda e, gj=gj: e.activation(
                out=self.ge[:], in_=gj, func=AF.Exp,
                bias=self.gs[:, 1:2]),
                writes=[self.b_gate], hard=[pv], tag=f"gex{c}{j}")
            pv = op("vector", lambda e: e.reduce_sum(
                out=self.gs[:, 2:3], in_=self.ge[:], axis=AXX),
                reads=[self.b_gate], writes=[self.b_gate], hard=[pv])
            t_rd = pv = op("vector", lambda e: e.reciprocal(
                out=self.gs[:, 3:4], in_=self.gs[:, 2:3]),
                writes=[self.b_gate], hard=[pv])
            pv = op("vector", lambda e, gj=gj: e.tensor_scalar(
                out=self.gm[:], in0=gj, scalar1=self.gs[:, 0:1], scalar2=None,
                op0=ALU.is_ge), writes=[self.b_gate], hard=[pv])
            pv = op("vector", lambda e, gj=gj: e.scalar_tensor_tensor(
                out=self.gm[:], in0=self.gm[:], scalar=-1e30, in1=gj,
                op0=ALU.mult, op1=ALU.add), writes=[self.b_gate], hard=[pv])
            pv = op("vector", lambda e: e.reduce_max(
                out=self.gs[:, 4:5], in_=self.gm[:], axis=AXX),
                writes=[self.b_gate], hard=[pv])
            pv = op("vector", lambda e, gj=gj: e.tensor_scalar(
                out=self.gm[:], in0=gj, scalar1=self.gs[:, 4:5], scalar2=None,
                op0=ALU.is_ge), writes=[self.b_gate], hard=[pv])
            pv = op("vector", lambda e: e.tensor_tensor(
                out=self.ge[:], in0=self.ge[:], in1=self.gm[:], op=ALU.mult),
                writes=[self.b_gate], hard=[pv])
            op("vector", lambda e, j=j: e.tensor_scalar(
                out=self.wtm16[:, j, :], in0=self.ge[:],
                scalar1=self.gs[:, 3:4], scalar2=None, op0=ALU.mult),
                reads=[self.b_gate], writes=[self.b_gate], hard=[pv, t_rd],
                tag=f"wtm{c}{j}")
            yield 0
        if c == 0:
            self.tap("wtm", self.wtm16[:])
        items = []
        for j in range(4):
            items.append((lambda e, j=j: e.transpose(
                out=self.pbank[3][:].bitcast(BF16)[0:8, j * P:(j + 1) * P],
                in_=self.wtm16[:, j, :], identity=self.c_eye16[:]),
                [self.b_gate], 3))
        self.mm_seq(items, self.bankbuf, tag=f"wtr{c}")
        op("vector", lambda e: e.tensor_copy(
            out=self.sbW8[:], in_=self.pbank[3][:].bitcast(BF16)[0:8, 0:512]),
            reads=[self.bankbuf[3]], writes=[self.b_sbW8], tag=f"w8{c}")
        yield 0
        for e_i in range(E):
            bb = (e_i % 2)
            self.mm_seq([(lambda e, e_i=e_i, bb=bb: e.matmul(
                out=self.pbank[bb][:, :],
                lhsT=self.c_r16[0:8, e_i * P:(e_i + 1) * P],
                rhs=self.sbW8[:], start=True, stop=True),
                [self.b_sbW8], bb)], self.bankbuf, tag=f"wb{c}_{e_i}")
            op("scalar", lambda e, e_i=e_i, bb=bb: e.activation(
                out=wbu[:, e_i, :], in_=self.pbank[bb][:, :],
                func=AF.Identity), reads=[self.bankbuf[bb]],
                writes=[b_wb], tag=f"wbev{c}_{e_i}")
        for e_i in range(E):
            yield 4
            op("vector", lambda e, e_i=e_i: e.tensor_tensor(
                out=self.scr16[:],
                in0=self.xn2[:],
                in1=wbu[:, e_i, :].unsqueeze(1)
                .broadcast_to([P, KD, TT]),
                op=ALU.mult), reads=[self.b_xn2, b_wb],
                writes=[self.b_scr16], tag=f"xe{c}_{e_i}")
            for k in range(KD):
                pap, pslot = self.piece("we", e_i * 8 + k)
                items = []
                for o in range(KD):
                    items.append((lambda e, pap=pap, o=o, k=k, e_i=e_i:
                                  e.matmul(
                                      out=self.pbank[o][:, :],
                                      lhsT=pap[:, o * P:(o + 1) * P],
                                      rhs=self.scr16[:, k, :],
                                      start=(e_i == 0 and k == 0),
                                      stop=False),
                                  [pslot, self.b_scr16], o))
                self.mm_seq(items, self.bankbuf, tag=f"we{c}_{e_i}_{k}")
        items = []
        for o in range(KD):
            items.append((lambda e, o=o: e.matmul(
                out=self.pbank[o][:, :],
                lhsT=self.c_be16[:, o * P:(o + 1) * P], rhs=self.sbW8[:],
                start=False, stop=True),
                [self.b_sbW8], o))
        self.mm_seq(items, self.bankbuf, tag=f"webias{c}")
        yield 0
        for o in range(KD):
            op("vector", lambda e, o=o: e.tensor_tensor(
                out=self.xch[:, o, :], in0=self.pbank[o][:, :],
                in1=self.xch[:, o, :], op=ALU.add),
                reads=[self.bankbuf[o]], writes=[self.b_xch],
                tag=f"res2{c}_{o}")
        if c == 0:
            self.tap("resid2", self.xch[:])
        self.dma("scalar", "dmaA", self.d_out[:, :, c * TT:(c + 1) * TT],
                 self.xch[:], reads=[self.b_xch], tag=f"out{c}")

    def drive_gens(self, gens, slot):
        for gst in gens:
            n = 0
            while (gst["ms"] is not None and gst["ms"] <= slot
                   and n < gst["rate"]):
                try:
                    gst["ms"] = next(gst["it"])
                except StopIteration:
                    gst["ms"] = None
                n += 1

    def mkgen(self, it, rate):
        try:
            ms = next(it)
        except StopIteration:
            ms = None
        return {"it": it, "ms": ms, "rate": rate}

    def emit_span(self, c):
        op = self.op
        upar = c % 2
        u16 = self.u16[upar]
        b_u = self.b_u[upar]
        gens = [self.mkgen(self.gen_wdt(c), 4),
                self.mkgen(self.gen_ball(c), 2),
                self.mkgen(self.gen_tail(c - 1), 4),
                self.mkgen(self.gen_front(c + 1), 2)]
        slot = 0
        for g in range(KI):
            self.drive_gens(gens, slot)
            ds = g % 8
            vp = g % 2
            op("vector", lambda e, g=g, ds=ds, vp=vp: e.tensor_tensor(
                out=self.v16[vp][:], in0=self.dful[:, ds, :],
                in1=u16[:, g, :],
                op=ALU.mult), reads=[self.b_dful[ds], b_u],
                writes=[self.b_v[vp]], tag=f"v{c}_{g}")
            for sg in range(NSG):
                sp = sg % 2
                da3 = self.dA[sp][:].rearrange("p (s c) -> p s c", s=SG)
                db3 = self.dBu[sp][:].rearrange("p (s c) -> p s c", s=SG)
                for jj in range(SG):
                    s_idx = sg * SG + jj
                    col = g * 16 + s_idx
                    op("scalar", lambda e, jj=jj, col=col, sp=sp, da3=da3,
                       ds=ds:
                       e.activation(
                           out=da3[:, jj, 2:514], in_=self.dful[:, ds, :],
                           func=AF.Exp, scale=self.c_A[:, col:col + 1]),
                       reads=[self.b_dful[ds]],
                       writes=[self.b_dA[sp]],
                       tag=f"dA{c}_{g}_{sg}_{jj}")
                op("vector", lambda e, sp=sp, sg=sg, db3=db3, vp=vp:
                    e.tensor_tensor(
                    out=db3[:, :, 2:514],
                    in0=self.v16[vp][:].unsqueeze(1)
                    .broadcast_to([P, SG, 512]),
                    in1=self.ball[:, sg * SG:(sg + 1) * SG, :],
                    op=ALU.mult), reads=[self.b_v[vp], self.b_ball[sg]],
                    writes=[self.b_dBu[sp]], tag=f"dbu{c}_{g}_{sg}")
                cc = g * 16 + sg * SG
                t_inj = op("vector", lambda e, sp=sp, cc=cc, db3=db3:
                    e.tensor_copy(
                    out=db3[:, :, 0:1].squeeze(2),
                    in_=self.carry[:, cc:cc + SG]),
                    reads=[self.b_carry], writes=[self.b_dBu[sp]],
                    tag=f"cin{c}_{g}_{sg}")
                hsl = self.hfull[:].rearrange("p s c -> p (s c)")[
                    :, sg * SCF:(sg + 1) * SCF]
                op("vector", lambda e, sp=sp, hsl=hsl: e.tensor_tensor_scan(
                    out=hsl, data0=self.dA[sp][:], data1=self.dBu[sp][:],
                    initial=0.0, op0=ALU.mult, op1=ALU.add),
                    reads=[self.b_dA[sp], self.b_dBu[sp]],
                    writes=[self.b_hfull], hard=[t_inj],
                    tag=f"scan{c}_{g}_{sg}")
                slot += 1
                self.drive_gens(gens, slot)
            t_ext = op("vector", lambda e, g=g: e.tensor_copy(
                out=self.carry[:, g * 16:(g + 1) * 16],
                in_=self.hfull[:, :, 513:514].squeeze(2)),
                reads=[self.b_hfull], writes=[self.b_carry],
                tag=f"cout{c}_{g}")
            hD = self.hfull[:]
            for span in (8, 4, 2, 1):
                op("vector", lambda e, span=span, hD=hD: e.tensor_tensor(
                    out=hD[:, 0:span, 2:514], in0=hD[:, 0:span, 2:514],
                    in1=hD[:, span:2 * span, 2:514], op=ALU.add),
                    reads=[self.b_hfull], writes=[self.b_hfull],
                    hard=[t_ext] if span == 8 else (),
                    tag=f"tree{c}_{g}_{span}")
            op("vector", lambda e, g=g: e.scalar_tensor_tensor(
                out=self.ytmp[:], in0=u16[:, g, :],
                scalar=self.c_dr[:, g:g + 1],
                in1=self.hfull[:, 0:1, 2:514].squeeze(1),
                op0=ALU.mult, op1=ALU.add),
                reads=[b_u, self.b_hfull], writes=[self.b_ytmp],
                tag=f"yd{c}_{g}")
            op("vector", lambda e, g=g: e.tensor_tensor(
                out=self.yrhs[:, g, :], in0=self.ytmp[:],
                in1=self.sz16[:, g, :], op=ALU.mult),
                reads=[self.b_ytmp, self.b_sz[g]], writes=[self.b_yrhs],
                tag=f"ysz{c}_{g}")
        if c == 0:
            self.tap("yrhs", self.yrhs[:])
        self.drive_gens(gens, 10 ** 9)
        while any(gst["ms"] is not None for gst in gens):
            self.drive_gens(gens, 10 ** 9)

    def emit_all(self):
        self.prologue()
        for _ in self.gen_front(0):
            pass
        for c in range(self.n_chunks):
            self.emit_span(c)
        for _ in self.gen_tail(self.n_chunks - 1):
            pass

    def finish(self):
        v = self.pr.cnt["dmaA"]
        self.pr.emit("scalar", None, [("dmaA", v)], None, 0, "final_wait")
        vs = self.pr.cnt["dmaS"]
        self.pr.emit("sync", None, [("dmaS", vs)], None, 0, "final_waitS")

    def replay(self, block):
        run_ends = {"dmaS": [], "dmaA": []}
        last_dma_idx = {}
        eng_pos = {e: 0 for e in ENGS}
        cur_run = {}
        for (e, fn, ws, inc_sem, inc, tag) in self.pr.ops:
            pos = eng_pos[e]
            eng_pos[e] += 1
            if inc_sem not in ("dmaS", "dmaA") or not inc:
                continue
            adjacent = (last_dma_idx.get((e, inc_sem)) == pos - 1
                        and not ws)
            last_dma_idx[(e, inc_sem)] = pos
            if not adjacent and inc_sem in cur_run:
                run_ends[inc_sem].append(cur_run[inc_sem])
            cur_run[inc_sem] = cur_run.get(inc_sem, 0) + inc
        for s, v in cur_run.items():
            run_ends[s].append(v)

        import bisect

        def roundup(s, v):
            ends = run_ends.get(s)
            if not ends:
                return v
            i = bisect.bisect_left(ends, v)
            return ends[i] if i < len(ends) else v

        def section(engname):
            def go(eng):
                recent = []
                for (e, fn, ws, inc_sem, inc, tag) in self.pr.ops:
                    if e != engname:
                        continue
                    for (s, v) in ws:
                        if s in ("dmaS", "dmaA"):
                            v = roundup(s, v)
                        eng.wait_ge(self.sems[s], v)
                    if fn is None:
                        continue
                    recent.append(tag)
                    try:
                        inst = fn(eng)
                    except Exception:
                        import sys
                        sys.stderr.write(
                            "replay fail at %s; recent=%s\n"
                            % (tag, recent[-30:]))
                        raise
                    if inc:
                        inst.then_inc(self.sems[inc_sem], inc)
            return go

        block.tensor(section("tensor"))
        block.vector(section("vector"))
        block.scalar(section("scalar"))
        block.gpsimd(section("gpsimd"))
        block.sync(section("sync"))


def build(n_chunks=CH, taps=()):
    nc = bass.Bass("TRN2", target_bir_lowering=False)
    with contextlib.ExitStack() as stack:
        kb = KB(nc, stack, taps=taps, n_chunks=n_chunks)
        block = stack.enter_context(nc.Block())
        kb.emit_all()
        kb.finish()
        kb.replay(block)
    return nc


def _host_blocks(ins):
    W_in = np.asarray(ins["L0_attn_W_in"], np.float32)
    conv_w = np.asarray(ins["L0_attn_conv_w"], np.float32)
    W_x = np.asarray(ins["L0_attn_W_x"], np.float32)
    W_dt = np.asarray(ins["L0_attn_W_dt"], np.float32)
    W_out = np.asarray(ins["L0_attn_W_out"], np.float32)
    Wg = np.asarray(ins["L0_ffn_Wg"], np.float32)
    We = np.asarray(ins["L0_ffn_W"], np.float32)
    be = np.asarray(ins["L0_ffn_b"], np.float32)

    wf = np.zeros((P, FT), np.float32)

    def put(name, arr):
        o = BOFF[name]
        wf[:, o:o + arr.shape[1]] = arr

    put("eye", np.eye(P, dtype=np.float32))
    r16 = np.zeros((P, 2048), np.float32)
    r16[0:16] = np.kron(np.eye(16, dtype=np.float32), np.ones((1, P), np.float32))
    put("r16", r16)
    r8 = np.zeros((P, 1024), np.float32)
    r8[0:8] = np.kron(np.eye(8, dtype=np.float32), np.ones((1, P), np.float32))
    put("r8", r8)
    bep = np.zeros((P, 1024), np.float32)
    bep[0:8] = be
    put("be", bep)
    wp = []
    for mg in [0, 4, 1, 5, 2, 6, 3, 7]:
        for k in range(KD):
            wp.append(W_in[k * P:(k + 1) * P, mg * 512:(mg + 1) * 512])
    put("win", np.concatenate(wp, axis=1))
    cp = []
    for g in range(KI):
        for kk in range(4):
            cp.append(np.diag(conv_w[g * P:(g + 1) * P, 0, kk]).astype(np.float32))
    put("conv", np.concatenate(cp, axis=1))
    xp = [W_x[k * P:(k + 1) * P, DS:2 * DS] for k in range(KI)]
    put("wx", np.concatenate(xp, axis=1))
    dp = []
    for g in range(KI):
        for k in range(KI):
            dp.append(W_dt[k * P:(k + 1) * P, g * P:(g + 1) * P])
    put("wdt", np.concatenate(dp, axis=1))
    op_ = []
    for mp in range(4):
        for k in range(KI):
            op_.append(W_out[k * P:(k + 1) * P, mp * 256:(mp + 1) * 256])
    put("wout", np.concatenate(op_, axis=1))
    gp = [np.ascontiguousarray(Wg[:, k * P:(k + 1) * P].T) for k in range(KD)]
    put("wg", np.concatenate(gp, axis=1))
    ep = []
    for e in range(E):
        WeT = We[e].T
        for k in range(KD):
            ep.append(np.ascontiguousarray(WeT[k * P:(k + 1) * P, :]))
    put("we", np.concatenate(ep, axis=1))

    b_in = np.asarray(ins["L0_attn_b_in"], np.float32)
    b_dt = np.asarray(ins["L0_attn_b_dt"], np.float32)
    b_out = np.asarray(ins["L0_attn_b_out"], np.float32)
    Dv = np.asarray(ins["L0_attn_D"], np.float32)
    conv_b = np.asarray(ins["L0_attn_conv_b"], np.float32)
    A_log = np.asarray(ins["L0_attn_A_log"], np.float32)
    b_x = np.asarray(ins["L0_attn_b_x"], np.float32)
    bg = np.asarray(ins["L0_ffn_bg"], np.float32)

    bias = {
        "binr": np.ascontiguousarray(b_in.reshape(32, P).T),
        "bdtr": np.ascontiguousarray(b_dt.reshape(16, P).T),
        "boutr": np.ascontiguousarray(b_out.reshape(8, P).T),
        "dr": np.ascontiguousarray(Dv.reshape(16, P).T),
        "convbr": np.ascontiguousarray(conv_b.reshape(16, P).T),
        "alogr": np.ascontiguousarray(
            A_log.reshape(16, P, 16).transpose(1, 0, 2).reshape(P, 256)),
        "bx": np.zeros((P, 1), np.float32),
        "bgrow": np.ascontiguousarray(bg.reshape(1, 8)),
    }
    bias["bx"][0:16, 0] = b_x[DS:2 * DS]
    return wf, bias


def _host_x(xb):
    return np.ascontiguousarray(
        xb.T.reshape(KD, P, T).transpose(1, 0, 2)).astype(np.float32)


def _host_out(o):
    return np.ascontiguousarray(
        np.asarray(o).transpose(1, 0, 2).reshape(D, T).T)


def kernel(**inputs):
    x = np.asarray(inputs["L0_attn_x"], np.float32)
    wf, bias = _host_blocks(inputs)
    nc = build()
    in_maps = []
    for b in range(8):
        m = {"x_in": _host_x(x[b]), "wf32": wf}
        for k, v in bias.items():
            m["bias_" + k] = v
        in_maps.append(m)
    res = run_bass_kernel_spmd(nc, in_maps, core_ids=list(range(8)))
    out = np.stack([_host_out(r["out"]) for r in res.results], axis=0)
    return out


# version 11
# speedup vs baseline: 1.0604x; 1.0530x over previous
import contextlib
import numpy as np

import concourse.bass as bass
import concourse.mybir as mybir
from concourse.bass_utils import run_bass_kernel_spmd

F32 = mybir.dt.float32
BF16 = mybir.dt.bfloat16
AF = mybir.ActivationFunctionType
ALU = mybir.AluOpType
AXX = mybir.AxisListType.X

P = 128
T = 4096
D = 1024
DI = 2048
DS = 16
E = 8
CH = 8
TT = 512
KD = D // P
KI = DI // P
SBLK = 514
SG = 2
NSG = DS // SG
SCF = SG * SBLK

ENGS = ("tensor", "vector", "scalar", "gpsimd", "sync")
SEMS = ENGS + ("dmaS", "dmaA")

BLK = [
    ("eye", 128),
    ("r16", 2048),
    ("r8", 1024),
    ("be", 1024),
    ("win", 64 * 512),
    ("conv", 64 * 128),
    ("wx", 16 * 16),
    ("wdt", 256 * 128),
    ("wout", 64 * 256),
    ("wg", 8 * 8),
    ("we", 64 * 1024),
]
BOFF = {}
_o = 0
for _n, _c in BLK:
    BOFF[_n] = _o
    _o += _c
FT = _o


class Buf:
    __slots__ = ("w", "rs")

    def __init__(self):
        self.w = None
        self.rs = []

    def r(self):
        return [self.w] if self.w is not None else []

    def wkill(self):
        out = list(self.rs)
        if self.w is not None:
            out.append(self.w)
        return out


class Prog:
    def __init__(self):
        self.ops = []
        self.cnt = {s: 0 for s in SEMS}
        self.maxw = {}

    def emit(self, eng, fn, waits=(), inc_sem=None, inc=1, tag="", hard=()):
        if inc_sem is None:
            inc_sem = eng
        seen = {}
        for (s, v) in waits:
            if s == eng and s in ENGS:
                continue
            if v > seen.get(s, -1):
                seen[s] = v
        for (s, v) in hard:
            if v > seen.get(s, -1):
                seen[s] = v
        ws = []
        for s, v in seen.items():
            k = (eng, s)
            if self.maxw.get(k, -1) >= v:
                continue
            self.maxw[k] = v
            ws.append((s, v))
        tok = None
        if inc:
            self.cnt[inc_sem] += inc
            tok = (inc_sem, self.cnt[inc_sem])
        self.ops.append((eng, fn, ws, inc_sem, inc, tag))
        return tok


class KB:

    def __init__(self, nc, stack, taps=(), n_chunks=CH):
        self.nc = nc
        self.n_chunks = n_chunks
        self.st = stack
        self.pr = Prog()
        self.taps_req = set(taps)
        self.taps = {}
        self.d_x = nc.dram_tensor("x_in", [P, KD, T], F32, kind="ExternalInput").ap()
        self.d_wf = nc.dram_tensor("wf32", [P, FT], F32, kind="ExternalInput").ap()
        self.d_w16 = nc.dram_tensor("w16", [P, FT], BF16, kind="Internal").ap()
        self.d_out = nc.dram_tensor("out", [P, KD, T], F32, kind="ExternalOutput").ap()
        bias_specs = [
            ("binr", [P, 32]), ("bdtr", [P, 16]), ("boutr", [P, 8]),
            ("dr", [P, 16]), ("convbr", [P, 16]), ("alogr", [P, 256]),
            ("bx", [P, 1]), ("bgrow", [1, 8]),
        ]
        self.d_bias = {n: nc.dram_tensor("bias_" + n, s, F32, kind="ExternalInput").ap()
                       for n, s in bias_specs}
        self.sems = {}
        self.alloc()

    def sb(self, name, shape, dt):
        t = self.st.enter_context(self.nc.sbuf_tensor(name, shape, dt))
        return t

    def alloc(self):
        sb = self.sb
        st = self.st
        self.c_binr = sb("c_binr", [P, 32], F32)
        self.c_bdtr = sb("c_bdtr", [P, 16], F32)
        self.c_boutr = sb("c_boutr", [P, 8], F32)
        self.c_dr = sb("c_dr", [P, 16], F32)
        self.c_convbr = sb("c_convbr", [P, 16], F32)
        self.c_A = sb("c_A", [P, 256], F32)
        self.c_bx = sb("c_bx", [P, 1], F32)
        self.c_bgrow = sb("c_bgrow", [1, 8], F32)
        self.c_bgrow16 = sb("c_bgrow16", [1, 8], BF16)
        self.c_bgb = sb("c_bgb", [P, 8], F32)
        self.c_eye16 = sb("c_eye16", [P, P], BF16)
        self.c_r16 = sb("c_r16", [16, 2048], BF16)
        self.c_be16 = sb("c_be16", [8, 1024], BF16)
        self.c_ones1 = sb("c_ones1", [1, P], BF16)
        self.c_onesc = sb("c_onesc", [P, 1], BF16)
        self.xch = sb("xch", [P, KD, TT], F32)
        self.scr16 = sb("scr16", [P, KD, TT], BF16)
        self.xn1 = sb("xn1", [P, KD, TT], BF16)
        self.xn2 = sb("xn2", [P, KD, TT], BF16)
        self.u16 = [sb(f"u16_{i}", [P, KI, TT], BF16) for i in range(2)]
        self.sz16 = sb("sz16", [P, KI, TT], BF16)
        self.ball = sb("ball", [P, DS, TT], BF16)
        self.yrhs = sb("yrhs", [P, KI, TT], BF16)
        self.xcv = [sb(f"xcv{i}", [P, 516], BF16) for i in range(2)]
        self.tails = sb("tails", [P, 16, 4], BF16)
        self.dA = [sb(f"dA{i}", [P, SCF], F32) for i in range(2)]
        self.dBu = [sb(f"dBu{i}", [P, SCF], BF16) for i in range(2)]
        self.hfull = sb("hfull", [P, DS, SBLK], BF16)
        self.carry = sb("carry", [P, 256], BF16)
        self.sp1 = sb("sp1", [P, TT], F32)
        self.sp2 = sb("sp2", [P, TT], F32)
        self.sp3 = sb("sp3", [P, TT], F32)
        self.dful = sb("dful", [P, 8, TT], BF16)
        self.v16 = [sb(f"v16_{i}", [P, TT], BF16) for i in range(2)]
        self.ytmp = sb("ytmp", [P, TT], BF16)
        self.st_a = sb("st_a", [1, TT], F32)[:]
        self.st_b = sb("st_b", [1, TT], F32)[:]
        self.st_c = sb("st_c", [1, TT], F32)[:]
        self.mrow16 = sb("mrow16", [1, TT], BF16)
        self.rrow16 = sb("rrow16", [1, TT], BF16)
        self.mb16 = sb("mb16", [P, TT], BF16)
        self.rb16 = sb("rb16", [P, TT], BF16)
        self.gtm = sb("gtm", [P, 4, 8], F32)
        self.ge = sb("ge", [P, 8], F32)
        self.gm = sb("gm", [P, 8], F32)
        self.gs = sb("gs", [P, 8], F32)
        self.wtm16 = sb("wtm16", [P, 4, 8], BF16)
        self.sbW8 = sb("sbW8", [8, TT], BF16)
        self.sbB16 = sb("sbB16", [16, TT], BF16)
        self.streams = {}
        for name, ppc, pc, bp, rb in [
            ("win", 64, 512, 2, 2),
            ("conv", 64, 128, 4, 2),
            ("wx", 16, 16, 16, 1),
            ("wdt", 256, 128, 8, 2),
            ("wout", 64, 256, 4, 2),
            ("wg", 8, 8, 8, 1),
            ("we", 64, 1024, 1, 2),
        ]:
            arena = sb("ar_" + name, [P, rb * bp * pc], BF16)
            self.streams[name] = dict(
                name=name, ppc=ppc, pc=pc, bp=bp, rb=rb, arena=arena,
                slots=[Buf() for _ in range(rb)], gb=0, base=0,
                cast_tok=None)
        self.pbank = [self.st.enter_context(
            self.nc.psum_tensor(f"pb{i}", [P, 512], F32)) for i in range(8)]
        self.bankbuf = [Buf() for _ in range(8)]
        for s in SEMS:
            self.sems[s] = st.enter_context(self.nc.semaphore("sem_" + s))
        self.b_xch = Buf()
        self.b_scr16 = Buf()
        self.b_xn1 = Buf()
        self.b_xn2 = Buf()
        self.b_u = [Buf(), Buf()]
        self.b_sz = [Buf() for _ in range(KI)]
        self.b_ball = [Buf() for _ in range(NSG)]
        self.b_yrhs = Buf()
        self.b_xcv = [Buf(), Buf()]
        self.b_tails = Buf()
        self.b_dA = [Buf(), Buf()]
        self.b_dBu = [Buf(), Buf()]
        self.b_hfull = Buf()
        self.b_carry = Buf()
        self.b_sp = Buf()
        self.b_dful = [Buf() for _ in range(8)]
        self.b_v = [Buf(), Buf()]
        self.b_ytmp = Buf()
        self.b_ln = Buf()
        self.b_bcast = Buf()
        self.b_gate = Buf()
        self.b_sbB = Buf()
        self.b_sbW8 = Buf()
        self.b_wb = Buf()

    def op(self, eng, fn, reads=(), writes=(), waits=(), inc_sem=None, inc=1,
           tag="", hard=()):
        ws = list(waits)
        bc = getattr(self, "b_consts", None)
        if bc is not None and bc.w is not None:
            ws += bc.r()
        for b in reads:
            ws += b.r()
        for b in writes:
            ws += b.wkill()
        t = self.pr.emit(eng, fn, ws, inc_sem, inc, tag, hard=hard)
        if t is not None:
            for b in reads:
                b.rs.append(t)
            for b in writes:
                b.w = t
                b.rs = []
        return t

    def dma(self, queue_eng, sem, out_ap, in_ap, reads=(), writes=(),
            waits=(), tag=""):
        fn = (lambda e, o=out_ap, i=in_ap: e.dma_start(out=o, in_=i))
        return self.op(queue_eng, fn, reads=reads, writes=writes, waits=waits,
                       inc_sem=sem, inc=16, tag=tag)

    def mm_seq(self, items, bank_bufs, tag=""):
        n = len(items)
        banks = sorted({b for _, _, b in items})
        tok = None
        first_by_bank = {b: True for b in banks}
        bc = getattr(self, "b_consts", None)
        for i, (fn, rbs, bi) in enumerate(items):
            ws = []
            if i == 0 and bc is not None and bc.w is not None:
                ws += bc.r()
            for b in rbs:
                ws += b.r()
            if first_by_bank.get(bi):
                ws += bank_bufs[bi].wkill()
                first_by_bank[bi] = False
            last = i == n - 1
            t = self.pr.emit("tensor", fn, ws, None, 1 if last else 0, tag)
            if last:
                tok = t
        for _, rbs, _bi in items:
            for b in rbs:
                if tok not in b.rs:
                    b.rs.append(tok)
        for b in banks:
            bank_bufs[b].w = tok
            bank_bufs[b].rs = []
        return tok

    def tap(self, name, ap):
        if name not in self.taps_req:
            return
        shape = list(ap.shape)
        d = self.nc.dram_tensor("tap_" + name, shape, ap.dtype,
                                kind="ExternalOutput").ap()
        self.taps[name] = name
        waits = [(s, self.pr.cnt[s]) for s in ("tensor", "vector", "scalar",
                                               "gpsimd")]
        self.dma("scalar", "dmaA", d, ap, waits=waits, tag="tap_" + name)

    def set_stream_chunk(self, name, c):
        st = self.streams[name]
        st["chunk_base"] = c * (st["ppc"] // st["bp"])

    def _ensure(self, name, upto):
        s = self.streams[name]
        nb = s["ppc"] // s["bp"]
        cols = s["bp"] * s["pc"]
        while s["gb"] < upto:
            b = s["gb"]
            slot = b % s["rb"]
            src = BOFF[name] + (b % nb) * cols
            waits = []
            if b < nb and s.get("cast_tok"):
                waits.append(s["cast_tok"])
            self.dma("sync", "dmaS",
                     s["arena"][:, slot * cols:(slot + 1) * cols],
                     self.d_w16[:, src:src + cols],
                     writes=[s["slots"][slot]], waits=waits,
                     tag=f"w_{name}_{b}")
            s["gb"] += 1

    def piece(self, name, j):
        s = self.streams[name]
        gb = s["chunk_base"] + j // s["bp"]
        self._ensure(name, gb + 1)
        slot = gb % s["rb"]
        cols = s["bp"] * s["pc"]
        off = slot * cols + (j % s["bp"]) * s["pc"]
        return s["arena"][:, off:off + s["pc"]], s["slots"][slot]

    def prologue(self):
        nc = self.nc
        self.b_consts = Buf()
        BC = self.b_consts

        for n, dst in [("binr", self.c_binr), ("bdtr", self.c_bdtr),
                       ("boutr", self.c_boutr), ("dr", self.c_dr),
                       ("convbr", self.c_convbr), ("alogr", self.c_A),
                       ("bx", self.c_bx), ("bgrow", self.c_bgrow)]:
            self.dma("scalar", "dmaA", dst[:], self.d_bias[n],
                     writes=[BC], tag="bias_" + n)
        self.op("scalar", lambda e: e.activation(
            out=self.c_A[:], in_=self.c_A[:], func=AF.Exp),
            writes=[BC], tag="A_exp")
        self.op("vector", lambda e: e.tensor_scalar(
            out=self.c_A[:], in0=self.c_A[:], scalar1=-1.0, scalar2=None,
            op0=ALU.mult), writes=[BC], tag="A_neg")
        self.op("vector", lambda e: e.memset(self.c_ones1[:], 1.0),
                writes=[BC])
        self.op("vector", lambda e: e.memset(self.c_onesc[:], 1.0),
                writes=[BC])
        self.op("vector", lambda e: e.tensor_copy(
            out=self.c_bgrow16[:], in_=self.c_bgrow[:]), writes=[BC])
        self.mm_seq([(lambda e: e.matmul(
            out=self.pbank[7][:, 0:8], lhsT=self.c_ones1[:],
            rhs=self.c_bgrow16[:], start=True, stop=True), [BC], 7)],
            self.bankbuf, tag="bgb_mm")
        self.op("vector", lambda e: e.tensor_copy(
            out=self.c_bgb[:], in_=self.pbank[7][:, 0:8]),
            reads=[self.bankbuf[7]], writes=[BC], tag="bgb_ev")
        for par in range(2):
            da3 = self.dA[par][:].rearrange("p (s c) -> p s c", s=SG)
            db3 = self.dBu[par][:].rearrange("p (s c) -> p s c", s=SG)
            self.op("vector", lambda e, a=da3: e.memset(a[:, :, 0:1], 0.0),
                    writes=[self.b_dA[par]])
            self.op("vector", lambda e, a=da3: e.memset(a[:, :, 1:2], 1.0),
                    writes=[self.b_dA[par]])
            self.op("vector", lambda e, a=db3: e.memset(a[:, :, 1:2], 0.0),
                    writes=[self.b_dBu[par]])
        self.op("vector", lambda e: e.memset(self.carry[:], 0.0),
                writes=[self.b_carry])
        self.op("gpsimd", lambda e: e.memset(self.tails[:], 0.0),
                writes=[self.b_tails])
        xin = self.xch[:].rearrange("p k t -> p (k t)")
        uin = self.u16[0][:].rearrange("p k t -> p (k t)").bitcast(F32)
        fo0 = self.dful[:].rearrange("p k t -> p (k t)")
        fo1 = self.scr16[:].rearrange("p k t -> p (k t)")
        NSL = 2
        b_in = [Buf() for _ in range(8)]
        b_out = [Buf() for _ in range(8)]
        pcw = 1024
        n_pieces = (FT + pcw - 1) // pcw
        blk_tok = {}
        blk_end = {n: BOFF[n] + c for n, c in BLK}
        for i in range(n_pieces):
            lo = i * pcw
            hi = min(FT, lo + pcw)
            w = hi - lo
            sl = i % NSL
            if sl < 4:
                s_in = xin[:, sl * pcw:sl * pcw + w]
            else:
                s_in = uin[:, (sl - 4) * pcw:(sl - 4) * pcw + w]
            if sl < 4:
                s_out = fo0[:, sl * pcw:sl * pcw + w]
            else:
                s_out = fo1[:, (sl - 4) * pcw:(sl - 4) * pcw + w]
            self.dma("scalar", "dmaA", s_in, self.d_wf[:, lo:hi],
                     writes=[b_in[sl]], tag=f"cast_in{i}")
            self.op("vector", lambda e, o=s_out, s=s_in: e.tensor_copy(
                out=o, in_=s), reads=[b_in[sl]], writes=[b_out[sl]],
                tag=f"cast{i}")
            t = self.dma("sync", "dmaS", self.d_w16[:, lo:hi], s_out,
                         reads=[b_out[sl]], tag=f"cast_out{i}")
            for n, endc in blk_end.items():
                if lo < endc <= hi:
                    blk_tok[n] = t
        self.blk_tok = blk_tok
        for name in self.streams:
            self.streams[name]["cast_tok"] = blk_tok[name]
        self.b_xch.rs = sum((b_in[i].wkill() for i in range(4)), [])
        self.b_u[0].rs = sum((b_in[i].wkill() for i in range(4, 8)), [])
        for ds in range(8):
            self.b_dful[ds].rs = list(b_out[ds // 2].wkill())
        self.b_scr16.rs = sum((b_out[i].wkill() for i in range(4, 8)), [])
        self.dma("sync", "dmaS", self.c_eye16[:],
                 self.d_w16[:, BOFF["eye"]:BOFF["eye"] + 128],
                 waits=[blk_tok["eye"]], writes=[self.b_consts],
                 tag="ld_eye")
        self.dma("sync", "dmaS", self.c_r16[:],
                 self.d_w16[0:16, BOFF["r16"]:BOFF["r16"] + 2048],
                 waits=[blk_tok["r16"]], writes=[self.b_consts],
                 tag="ld_r16")
        self.dma("sync", "dmaS", self.c_be16[:],
                 self.d_w16[0:8, BOFF["be"]:BOFF["be"] + 1024],
                 waits=[blk_tok["be"]], writes=[self.b_consts],
                 tag="ld_be")

    def gen_ln(self, src_f32_3d, src_buf, dst16, b_dst, xsq, b_xsq,
               bk0, bk1, tag):
        op = self.op
        yield 0
        op("scalar", lambda e: e.activation(
            out=self.scr16[:], in_=src_f32_3d, func=AF.Identity),
            reads=[src_buf], writes=[self.b_scr16], tag=tag + "_cast")
        yield 0
        op("vector", lambda e, o=xsq: e.tensor_tensor(
            out=o, in0=self.scr16[:], in1=self.scr16[:], op=ALU.mult),
            reads=[self.b_scr16], writes=[b_xsq], tag=tag + "_sq")
        yield 0
        items = []
        for k in range(KD):
            items.append((lambda e, k=k: e.matmul(
                out=self.pbank[bk0][0:1, :], lhsT=self.c_onesc[:],
                rhs=self.scr16[:, k, :], start=(k == 0), stop=(k == KD - 1)),
                [self.b_scr16], bk0))
        for k in range(KD):
            items.append((lambda e, k=k, x=xsq: e.matmul(
                out=self.pbank[bk1][0:1, :], lhsT=self.c_onesc[:],
                rhs=x[:, k, :], start=(k == 0), stop=(k == KD - 1)),
                [b_xsq], bk1))
        self.mm_seq(items, self.bankbuf, tag=tag + "_stats")
        yield 0
        op("vector", lambda e: e.tensor_scalar(
            out=self.st_a, in0=self.pbank[bk0][0:1, :], scalar1=1.0 / D,
            scalar2=None, op0=ALU.mult), reads=[self.bankbuf[bk0]],
            writes=[self.b_ln], tag=tag + "_mean")
        op("vector", lambda e: e.tensor_scalar(
            out=self.st_b, in0=self.pbank[bk1][0:1, :], scalar1=1.0 / D,
            scalar2=None, op0=ALU.mult), reads=[self.bankbuf[bk1]],
            writes=[self.b_ln], tag=tag + "_msq")
        op("vector", lambda e: e.tensor_tensor(
            out=self.st_c, in0=self.st_a, in1=self.st_a,
            op=ALU.mult), writes=[self.b_ln], tag=tag + "_m2")
        op("vector", lambda e: e.tensor_tensor(
            out=self.st_b, in0=self.st_b, in1=self.st_c,
            op=ALU.subtract), writes=[self.b_ln], tag=tag + "_var")
        yield 0
        op("scalar", lambda e: e.activation(
            out=self.st_c, in_=self.st_b, func=AF.Ln, bias=np.float32(1e-5)),
            writes=[self.b_ln], tag=tag + "_lnv")
        op("scalar", lambda e: e.activation(
            out=self.st_b, in_=self.st_c, func=AF.Exp, scale=np.float32(-0.5)),
            writes=[self.b_ln], tag=tag + "_rstd")
        op("vector", lambda e: e.tensor_copy(out=self.mrow16[:],
                                             in_=self.st_a),
           writes=[self.b_ln], tag=tag + "_m16")
        op("vector", lambda e: e.tensor_copy(out=self.rrow16[:],
                                             in_=self.st_b),
           writes=[self.b_ln], tag=tag + "_r16c")
        yield 0
        items = [
            (lambda e: e.matmul(out=self.pbank[bk0][:, :],
                                lhsT=self.c_ones1[:],
                                rhs=self.mrow16[:], start=True, stop=True),
             [self.b_ln], bk0),
            (lambda e: e.matmul(out=self.pbank[bk1][:, :],
                                lhsT=self.c_ones1[:],
                                rhs=self.rrow16[:], start=True, stop=True),
             [self.b_ln], bk1),
        ]
        self.mm_seq(items, self.bankbuf, tag=tag + "_bc")
        op("scalar", lambda e: e.activation(
            out=self.mb16[:], in_=self.pbank[bk0][:, :], func=AF.Identity),
            reads=[self.bankbuf[bk0]], writes=[self.b_bcast],
            tag=tag + "_mbev")
        op("scalar", lambda e: e.activation(
            out=self.rb16[:], in_=self.pbank[bk1][:, :], func=AF.Identity),
            reads=[self.bankbuf[bk1]], writes=[self.b_bcast],
            tag=tag + "_rbev")
        yield 0
        for k in range(KD):
            op("vector", lambda e, k=k: e.tensor_tensor(
                out=dst16[:, k, :],
                in0=self.scr16[:, k, :], in1=self.mb16[:], op=ALU.subtract),
                reads=[self.b_scr16, self.b_bcast], writes=[b_dst],
                tag=tag + f"_sub{k}")
            op("vector", lambda e, k=k: e.tensor_tensor(
                out=dst16[:, k, :], in0=dst16[:, k, :], in1=self.rb16[:],
                op=ALU.mult), reads=[self.b_bcast], writes=[b_dst],
                tag=tag + f"_mul{k}")
            if k % 3 == 2:
                yield 0

    def gen_front(self, c):
        if c >= self.n_chunks:
            return
        op = self.op
        for nm in ("win", "conv", "wx"):
            self.set_stream_chunk(nm, c)
        upar = c % 2
        u16 = self.u16[upar]
        b_u = self.b_u[upar]
        yield 40
        self.dma("scalar", "dmaA", self.xch[:],
                 self.d_x[:, :, c * TT:(c + 1) * TT],
                 writes=[self.b_xch], tag=f"x{c}")
        for ms in self.gen_ln(self.xch[:], self.b_xch, self.xn1, self.b_xn1,
                              self.xn2[:], self.b_xn2, 6, 7, f"ln1_{c}"):
            yield max(ms, 42)
        if c == 0:
            self.tap("xn1", self.xn1[:])
        WSEQ = [0, 4, 1, 5, 2, 6, 3, 7]
        for oi, mg in enumerate(WSEQ):
            is_z = mg >= 4
            yield 46
            for kb in range(4):
                items = []
                for k in (2 * kb, 2 * kb + 1):
                    pap, pslot = self.piece("win", oi * 8 + k)
                    for m in range(4):
                        items.append((lambda e, pap=pap, m=m, k=k: e.matmul(
                            out=self.pbank[m][:, :],
                            lhsT=pap[:, m * P:(m + 1) * P],
                            rhs=self.xn1[:, k, :],
                            start=(k == 0), stop=(k == KD - 1)),
                            [pslot, self.b_xn1], m))
                self.mm_seq(items, self.bankbuf, tag=f"win{c}_{mg}_{kb}")
            for m in range(4):
                j = mg * 4 + m
                if j < KI:
                    g = j
                    par = g % 2
                    yield 46
                    op("vector", lambda e, g=g, par=par: e.tensor_copy(
                        out=self.xcv[par][:, 0:3], in_=self.tails[:, g, 0:3]),
                        reads=[self.b_tails], writes=[self.b_xcv[par]],
                        tag=f"tin{c}_{g}")
                    op("scalar", lambda e, m=m, g=g, par=par: e.activation(
                        out=self.xcv[par][:, 3:515], in_=self.pbank[m][:, :],
                        func=AF.Identity, bias=self.c_binr[:, g:g + 1]),
                        reads=[self.bankbuf[m]], writes=[self.b_xcv[par]],
                        tag=f"xhev{c}_{g}")
                    citems = []
                    for kk in range(4):
                        pap, pslot = self.piece("conv", g * 4 + kk)
                        citems.append((lambda e, pap=pap, kk=kk, par=par, m=m:
                                       e.matmul(
                                           out=self.pbank[m][:, :], lhsT=pap,
                                           rhs=self.xcv[par][:, kk:kk + 512],
                                           start=(kk == 0), stop=(kk == 3)),
                                       [pslot, self.b_xcv[par]], m))
                    self.mm_seq(citems, self.bankbuf, tag=f"conv{c}_{g}")
                    op("vector", lambda e, g=g, par=par: e.tensor_copy(
                        out=self.tails[:, g, 0:3],
                        in_=self.xcv[par][:, 512:515]),
                        reads=[self.b_xcv[par]], writes=[self.b_tails],
                        tag=f"tout{c}_{g}")
                    op("scalar", lambda e, g=g, m=m: e.activation(
                        out=u16[:, g, :], in_=self.pbank[m][:, :],
                        func=AF.Silu, bias=self.c_convbr[:, g:g + 1]),
                        reads=[self.bankbuf[m]], writes=[b_u],
                        tag=f"uev{c}_{g}")
                else:
                    zi = j - KI
                    yield 8 * zi + 10
                    op("scalar", lambda e, m=m, zi=zi, j=j: e.activation(
                        out=self.sz16[:, zi, :], in_=self.pbank[m][:, :],
                        func=AF.Silu, bias=self.c_binr[:, j:j + 1]),
                        reads=[self.bankbuf[m]], writes=[self.b_sz[zi]],
                        tag=f"szev{c}_{zi}")
        if c == 0:
            self.tap("u", u16[:])
            self.tap("sz", self.sz16[:])
        yield 80
        items = []
        for k in range(KI):
            pap, pslot = self.piece("wx", k)
            items.append((lambda e, pap=pap, k=k: e.matmul(
                out=self.pbank[7][0:16, :], lhsT=pap, rhs=u16[:, k, :],
                start=(k == 0), stop=(k == KI - 1)), [pslot, b_u], 7))
        self.mm_seq(items, self.bankbuf, tag=f"wxB{c}")
        op("vector", lambda e: e.tensor_scalar(
            out=self.sbB16[:], in0=self.pbank[7][0:16, :],
            scalar1=self.c_bx[0:16, :], scalar2=None, op0=ALU.add),
            reads=[self.bankbuf[7]], writes=[self.b_sbB], tag=f"Bev{c}")
        if c == 0:
            self.tap("B", self.sbB16[:])

    def gen_ball(self, c):
        if c >= self.n_chunks:
            return
        op = self.op
        for sb in range(NSG):
            yield max(0, sb - 2)
            for si in (2 * sb, 2 * sb + 1):
                bb = 6 + (si % 2)
                self.mm_seq([(lambda e, si=si, bb=bb: e.matmul(
                    out=self.pbank[bb][:, :],
                    lhsT=self.c_r16[:, si * P:(si + 1) * P],
                    rhs=self.sbB16[:],
                    start=True, stop=True),
                    [self.b_sbB], bb)],
                    self.bankbuf, tag=f"bb{c}_{si}")
                op("scalar", lambda e, si=si, bb=bb, sb=sb: e.activation(
                    out=self.ball[:, si, :], in_=self.pbank[bb][:, :],
                    func=AF.Identity), reads=[self.bankbuf[bb]],
                    writes=[self.b_ball[sb]], tag=f"bbev{c}_{si}")

    def gen_wdt(self, c):
        op = self.op
        self.set_stream_chunk("wdt", c)
        upar = c % 2
        u16 = self.u16[upar]
        b_u = self.b_u[upar]
        wsl = [0, 0, 1, 1, 2, 2, 41, 44, 47, 50, 53, 56, 59, 62, 65, 68]
        for g in range(KI):
            yield wsl[g]
            bank = 6 if g == 0 else (7 if g == 1 else 4 + (g % 2))
            items = []
            for k in range(KI):
                pap, pslot = self.piece("wdt", g * 16 + k)
                items.append((lambda e, pap=pap, k=k, bank=bank: e.matmul(
                    out=self.pbank[bank][:, :], lhsT=pap,
                    rhs=u16[:, k, :], start=(k == 0),
                    stop=(k == KI - 1)), [pslot, b_u], bank))
            self.mm_seq(items, self.bankbuf, tag=f"wdt{c}_{g}")
            op("scalar", lambda e, g=g, bank=bank: e.activation(
                out=self.sp1[:], in_=self.pbank[bank][:, :], func=AF.Abs,
                bias=self.c_bdtr[:, g:g + 1]), reads=[self.bankbuf[bank]],
                writes=[self.b_sp], tag=f"spabs{c}_{g}")
            op("scalar", lambda e, g=g, bank=bank: e.activation(
                out=self.sp2[:], in_=self.pbank[bank][:, :], func=AF.Relu,
                bias=self.c_bdtr[:, g:g + 1]), reads=[self.bankbuf[bank]],
                writes=[self.b_sp], tag=f"sprelu{c}_{g}")
            op("scalar", lambda e: e.activation(
                out=self.sp3[:], in_=self.sp1[:], func=AF.Exp,
                scale=np.float32(-1.0)),
                writes=[self.b_sp], tag=f"spexp{c}_{g}")
            op("scalar", lambda e: e.activation(
                out=self.sp1[:], in_=self.sp3[:], func=AF.Ln,
                bias=np.float32(1.0)),
                writes=[self.b_sp], tag=f"spln{c}_{g}")
            ds = g % 8
            op("vector", lambda e, ds=ds: e.tensor_tensor(
                out=self.dful[:, ds, :], in0=self.sp2[:], in1=self.sp1[:],
                op=ALU.add), reads=[self.b_sp], writes=[self.b_dful[ds]],
                tag=f"delta{c}_{g}")
            if c == 0 and g == 0:
                self.tap("delta_g0", self.dful[:, 0, :])

    def gen_tail(self, c):
        if c < 0:
            return
        op = self.op
        for nm in ("wout", "wg", "we"):
            self.set_stream_chunk(nm, c)
        wbu = self.xn1
        b_wb = self.b_xn1
        yield 0
        self.dma("scalar", "dmaA", self.xch[:],
                 self.d_x[:, :, c * TT:(c + 1) * TT],
                 writes=[self.b_xch], tag=f"xr{c}")
        for mp in range(4):
            b0 = (mp % 2) * 2
            yield 0
            for kq in range(4):
                items = []
                for k in range(4 * kq, 4 * kq + 4):
                    pap, pslot = self.piece("wout", mp * 16 + k)
                    for mm_i in range(2):
                        items.append((lambda e, pap=pap, mm_i=mm_i, k=k,
                                      b0=b0: e.matmul(
                            out=self.pbank[b0 + mm_i][:, :],
                            lhsT=pap[:, mm_i * P:(mm_i + 1) * P],
                            rhs=self.yrhs[:, k, :], start=(k == 0),
                            stop=(k == KI - 1)), [pslot, self.b_yrhs],
                            b0 + mm_i))
                self.mm_seq(items, self.bankbuf, tag=f"wout{c}_{mp}_{kq}")
            for mm_i in range(2):
                m = mp * 2 + mm_i
                bb_ = b0 + mm_i
                op("vector", lambda e, m=m, bb_=bb_:
                   e.scalar_tensor_tensor(
                    out=self.xch[:, m, :], in0=self.pbank[bb_][:, :],
                    scalar=self.c_boutr[:, m:m + 1], in1=self.xch[:, m, :],
                    op0=ALU.add, op1=ALU.add),
                    reads=[self.bankbuf[bb_]], writes=[self.b_xch],
                    tag=f"res1{c}_{m}")
        if c == 0:
            self.tap("resid1", self.xch[:])
        for ms in self.gen_ln(self.xch[:], self.b_xch, self.xn2, self.b_xn2,
                              self.xn1[:], self.b_xn1, 0, 1, f"ln2_{c}"):
            yield ms
        if c == 0:
            self.tap("xn2", self.xn2[:])
        yield 0
        items = []
        for j in range(4):
            for k in range(KD):
                pap, pslot = self.piece("wg", k)
                items.append((lambda e, pap=pap, j=j, k=k: e.matmul(
                    out=self.pbank[2][:, j * 8:(j + 1) * 8],
                    lhsT=self.xn2[:, k, j * P:(j + 1) * P], rhs=pap,
                    start=(k == 0), stop=(k == KD - 1)),
                    [pslot, self.b_xn2], 2))
        self.mm_seq(items, self.bankbuf, tag=f"glog{c}")
        for j in range(4):
            op("vector", lambda e, j=j: e.tensor_tensor(
                out=self.gtm[:, j, :],
                in0=self.pbank[2][:, j * 8:(j + 1) * 8],
                in1=self.c_bgb[:], op=ALU.add),
                reads=[self.bankbuf[2]], writes=[self.b_gate],
                tag=f"gev{c}_{j}")
        if c == 0:
            self.tap("glog", self.gtm[:])
        yield 0
        for j in range(4):
            gj = self.gtm[:, j, :]
            pv = op("vector", lambda e, gj=gj: e.reduce_max(
                out=self.gs[:, 0:1], in_=gj, axis=AXX),
                reads=[self.b_gate], writes=[self.b_gate], tag=f"m1_{c}{j}")
            pv = op("vector", lambda e: e.tensor_scalar(
                out=self.gs[:, 1:2], in0=self.gs[:, 0:1], scalar1=-1.0,
                scalar2=None, op0=ALU.mult), writes=[self.b_gate], hard=[pv])
            pv = op("scalar", lambda e, gj=gj: e.activation(
                out=self.ge[:], in_=gj, func=AF.Exp,
                bias=self.gs[:, 1:2]),
                writes=[self.b_gate], hard=[pv], tag=f"gex{c}{j}")
            pv = op("vector", lambda e: e.reduce_sum(
                out=self.gs[:, 2:3], in_=self.ge[:], axis=AXX),
                reads=[self.b_gate], writes=[self.b_gate], hard=[pv])
            t_rd = pv = op("vector", lambda e: e.reciprocal(
                out=self.gs[:, 3:4], in_=self.gs[:, 2:3]),
                writes=[self.b_gate], hard=[pv])
            pv = op("vector", lambda e, gj=gj: e.tensor_scalar(
                out=self.gm[:], in0=gj, scalar1=self.gs[:, 0:1], scalar2=None,
                op0=ALU.is_ge), writes=[self.b_gate], hard=[pv])
            pv = op("vector", lambda e, gj=gj: e.scalar_tensor_tensor(
                out=self.gm[:], in0=self.gm[:], scalar=-1e30, in1=gj,
                op0=ALU.mult, op1=ALU.add), writes=[self.b_gate], hard=[pv])
            pv = op("vector", lambda e: e.reduce_max(
                out=self.gs[:, 4:5], in_=self.gm[:], axis=AXX),
                writes=[self.b_gate], hard=[pv])
            pv = op("vector", lambda e, gj=gj: e.tensor_scalar(
                out=self.gm[:], in0=gj, scalar1=self.gs[:, 4:5], scalar2=None,
                op0=ALU.is_ge), writes=[self.b_gate], hard=[pv])
            pv = op("vector", lambda e: e.tensor_tensor(
                out=self.ge[:], in0=self.ge[:], in1=self.gm[:], op=ALU.mult),
                writes=[self.b_gate], hard=[pv])
            op("vector", lambda e, j=j: e.tensor_scalar(
                out=self.wtm16[:, j, :], in0=self.ge[:],
                scalar1=self.gs[:, 3:4], scalar2=None, op0=ALU.mult),
                reads=[self.b_gate], writes=[self.b_gate], hard=[pv, t_rd],
                tag=f"wtm{c}{j}")
            yield 0
        if c == 0:
            self.tap("wtm", self.wtm16[:])
        items = []
        for j in range(4):
            items.append((lambda e, j=j: e.transpose(
                out=self.pbank[3][:].bitcast(BF16)[0:8, j * P:(j + 1) * P],
                in_=self.wtm16[:, j, :], identity=self.c_eye16[:]),
                [self.b_gate], 3))
        self.mm_seq(items, self.bankbuf, tag=f"wtr{c}")
        op("vector", lambda e: e.tensor_copy(
            out=self.sbW8[:], in_=self.pbank[3][:].bitcast(BF16)[0:8, 0:512]),
            reads=[self.bankbuf[3]], writes=[self.b_sbW8], tag=f"w8{c}")
        yield 0
        for e_i in range(E):
            bb = (e_i % 2)
            self.mm_seq([(lambda e, e_i=e_i, bb=bb: e.matmul(
                out=self.pbank[bb][:, :],
                lhsT=self.c_r16[0:8, e_i * P:(e_i + 1) * P],
                rhs=self.sbW8[:], start=True, stop=True),
                [self.b_sbW8], bb)], self.bankbuf, tag=f"wb{c}_{e_i}")
            op("scalar", lambda e, e_i=e_i, bb=bb: e.activation(
                out=wbu[:, e_i, :], in_=self.pbank[bb][:, :],
                func=AF.Identity), reads=[self.bankbuf[bb]],
                writes=[b_wb], tag=f"wbev{c}_{e_i}")
        for e_i in range(E):
            yield 4
            op("vector", lambda e, e_i=e_i: e.tensor_tensor(
                out=self.scr16[:],
                in0=self.xn2[:],
                in1=wbu[:, e_i, :].unsqueeze(1)
                .broadcast_to([P, KD, TT]),
                op=ALU.mult), reads=[self.b_xn2, b_wb],
                writes=[self.b_scr16], tag=f"xe{c}_{e_i}")
            for k in range(KD):
                pap, pslot = self.piece("we", e_i * 8 + k)
                items = []
                for o in range(KD):
                    items.append((lambda e, pap=pap, o=o, k=k, e_i=e_i:
                                  e.matmul(
                                      out=self.pbank[o][:, :],
                                      lhsT=pap[:, o * P:(o + 1) * P],
                                      rhs=self.scr16[:, k, :],
                                      start=(e_i == 0 and k == 0),
                                      stop=False),
                                  [pslot, self.b_scr16], o))
                self.mm_seq(items, self.bankbuf, tag=f"we{c}_{e_i}_{k}")
        items = []
        for o in range(KD):
            items.append((lambda e, o=o: e.matmul(
                out=self.pbank[o][:, :],
                lhsT=self.c_be16[:, o * P:(o + 1) * P], rhs=self.sbW8[:],
                start=False, stop=True),
                [self.b_sbW8], o))
        self.mm_seq(items, self.bankbuf, tag=f"webias{c}")
        yield 0
        for o in range(KD):
            op("vector", lambda e, o=o: e.tensor_tensor(
                out=self.xch[:, o, :], in0=self.pbank[o][:, :],
                in1=self.xch[:, o, :], op=ALU.add),
                reads=[self.bankbuf[o]], writes=[self.b_xch],
                tag=f"res2{c}_{o}")
        if c == 0:
            self.tap("resid2", self.xch[:])
        self.dma("scalar", "dmaA", self.d_out[:, :, c * TT:(c + 1) * TT],
                 self.xch[:], reads=[self.b_xch], tag=f"out{c}")

    def drive_gens(self, gens, slot):
        for gst in gens:
            n = 0
            while (gst["ms"] is not None and gst["ms"] <= slot
                   and n < gst["rate"]):
                try:
                    gst["ms"] = next(gst["it"])
                except StopIteration:
                    gst["ms"] = None
                n += 1

    def mkgen(self, it, rate):
        try:
            ms = next(it)
        except StopIteration:
            ms = None
        return {"it": it, "ms": ms, "rate": rate}

    def emit_span(self, c):
        op = self.op
        upar = c % 2
        u16 = self.u16[upar]
        b_u = self.b_u[upar]
        gens = [self.mkgen(self.gen_wdt(c), 4),
                self.mkgen(self.gen_ball(c), 2),
                self.mkgen(self.gen_tail(c - 1), 4),
                self.mkgen(self.gen_front(c + 1), 2)]
        slot = 0
        for g in range(KI):
            self.drive_gens(gens, slot)
            ds = g % 8
            vp = g % 2
            op("vector", lambda e, g=g, ds=ds, vp=vp: e.tensor_tensor(
                out=self.v16[vp][:], in0=self.dful[:, ds, :],
                in1=u16[:, g, :],
                op=ALU.mult), reads=[self.b_dful[ds], b_u],
                writes=[self.b_v[vp]], tag=f"v{c}_{g}")
            for sg in range(NSG):
                sp = sg % 2
                da3 = self.dA[sp][:].rearrange("p (s c) -> p s c", s=SG)
                db3 = self.dBu[sp][:].rearrange("p (s c) -> p s c", s=SG)
                for jj in range(SG):
                    s_idx = sg * SG + jj
                    col = g * 16 + s_idx
                    op("scalar", lambda e, jj=jj, col=col, sp=sp, da3=da3,
                       ds=ds:
                       e.activation(
                           out=da3[:, jj, 2:514], in_=self.dful[:, ds, :],
                           func=AF.Exp, scale=self.c_A[:, col:col + 1]),
                       reads=[self.b_dful[ds]],
                       writes=[self.b_dA[sp]],
                       tag=f"dA{c}_{g}_{sg}_{jj}")
                op("vector", lambda e, sp=sp, sg=sg, db3=db3, vp=vp:
                    e.tensor_tensor(
                    out=db3[:, :, 2:514],
                    in0=self.v16[vp][:].unsqueeze(1)
                    .broadcast_to([P, SG, 512]),
                    in1=self.ball[:, sg * SG:(sg + 1) * SG, :],
                    op=ALU.mult), reads=[self.b_v[vp], self.b_ball[sg]],
                    writes=[self.b_dBu[sp]], tag=f"dbu{c}_{g}_{sg}")
                cc = g * 16 + sg * SG
                t_inj = op("vector", lambda e, sp=sp, cc=cc, db3=db3:
                    e.tensor_copy(
                    out=db3[:, :, 0:1].squeeze(2),
                    in_=self.carry[:, cc:cc + SG]),
                    reads=[self.b_carry], writes=[self.b_dBu[sp]],
                    tag=f"cin{c}_{g}_{sg}")
                hsl = self.hfull[:].rearrange("p s c -> p (s c)")[
                    :, sg * SCF:(sg + 1) * SCF]
                op("vector", lambda e, sp=sp, hsl=hsl: e.tensor_tensor_scan(
                    out=hsl, data0=self.dA[sp][:], data1=self.dBu[sp][:],
                    initial=0.0, op0=ALU.mult, op1=ALU.add),
                    reads=[self.b_dA[sp], self.b_dBu[sp]],
                    writes=[self.b_hfull], hard=[t_inj],
                    tag=f"scan{c}_{g}_{sg}")
                slot += 1
                self.drive_gens(gens, slot)
            t_ext = op("vector", lambda e, g=g: e.tensor_copy(
                out=self.carry[:, g * 16:(g + 1) * 16],
                in_=self.hfull[:, :, 513:514].squeeze(2)),
                reads=[self.b_hfull], writes=[self.b_carry],
                tag=f"cout{c}_{g}")
            hD = self.hfull[:]
            for span in (8, 4, 2, 1):
                op("vector", lambda e, span=span, hD=hD: e.tensor_tensor(
                    out=hD[:, 0:span, 2:514], in0=hD[:, 0:span, 2:514],
                    in1=hD[:, span:2 * span, 2:514], op=ALU.add),
                    reads=[self.b_hfull], writes=[self.b_hfull],
                    hard=[t_ext] if span == 8 else (),
                    tag=f"tree{c}_{g}_{span}")
            op("vector", lambda e, g=g: e.scalar_tensor_tensor(
                out=self.ytmp[:], in0=u16[:, g, :],
                scalar=self.c_dr[:, g:g + 1],
                in1=self.hfull[:, 0:1, 2:514].squeeze(1),
                op0=ALU.mult, op1=ALU.add),
                reads=[b_u, self.b_hfull], writes=[self.b_ytmp],
                tag=f"yd{c}_{g}")
            op("vector", lambda e, g=g: e.tensor_tensor(
                out=self.yrhs[:, g, :], in0=self.ytmp[:],
                in1=self.sz16[:, g, :], op=ALU.mult),
                reads=[self.b_ytmp, self.b_sz[g]], writes=[self.b_yrhs],
                tag=f"ysz{c}_{g}")
        if c == 0:
            self.tap("yrhs", self.yrhs[:])
        self.drive_gens(gens, 10 ** 9)
        while any(gst["ms"] is not None for gst in gens):
            self.drive_gens(gens, 10 ** 9)

    def emit_all(self):
        self.prologue()
        for _ in self.gen_front(0):
            pass
        for c in range(self.n_chunks):
            self.emit_span(c)
        for _ in self.gen_tail(self.n_chunks - 1):
            pass

    def finish(self):
        v = self.pr.cnt["dmaA"]
        self.pr.emit("scalar", None, [("dmaA", v)], None, 0, "final_wait")
        vs = self.pr.cnt["dmaS"]
        self.pr.emit("sync", None, [("dmaS", vs)], None, 0, "final_waitS")

    def replay(self, block):
        run_ends = {"dmaS": [], "dmaA": []}
        last_dma_idx = {}
        eng_pos = {e: 0 for e in ENGS}
        cur_run = {}
        for (e, fn, ws, inc_sem, inc, tag) in self.pr.ops:
            pos = eng_pos[e]
            eng_pos[e] += 1
            if inc_sem not in ("dmaS", "dmaA") or not inc:
                continue
            adjacent = (last_dma_idx.get((e, inc_sem)) == pos - 1
                        and not ws)
            last_dma_idx[(e, inc_sem)] = pos
            if not adjacent and inc_sem in cur_run:
                run_ends[inc_sem].append(cur_run[inc_sem])
            cur_run[inc_sem] = cur_run.get(inc_sem, 0) + inc
        for s, v in cur_run.items():
            run_ends[s].append(v)

        import bisect

        def roundup(s, v):
            ends = run_ends.get(s)
            if not ends:
                return v
            i = bisect.bisect_left(ends, v)
            return ends[i] if i < len(ends) else v

        def section(engname):
            def go(eng):
                recent = []
                for (e, fn, ws, inc_sem, inc, tag) in self.pr.ops:
                    if e != engname:
                        continue
                    for (s, v) in ws:
                        if s in ("dmaS", "dmaA"):
                            v = roundup(s, v)
                        eng.wait_ge(self.sems[s], v)
                    if fn is None:
                        continue
                    recent.append(tag)
                    try:
                        inst = fn(eng)
                    except Exception:
                        import sys
                        sys.stderr.write(
                            "replay fail at %s; recent=%s\n"
                            % (tag, recent[-30:]))
                        raise
                    if inc:
                        inst.then_inc(self.sems[inc_sem], inc)
            return go

        block.tensor(section("tensor"))
        block.vector(section("vector"))
        block.scalar(section("scalar"))
        block.gpsimd(section("gpsimd"))
        block.sync(section("sync"))


def build(n_chunks=CH, taps=()):
    nc = bass.Bass("TRN2", target_bir_lowering=False)
    with contextlib.ExitStack() as stack:
        kb = KB(nc, stack, taps=taps, n_chunks=n_chunks)
        block = stack.enter_context(nc.Block())
        kb.emit_all()
        kb.finish()
        kb.replay(block)
    return nc


def _host_blocks(ins):
    W_in = np.asarray(ins["L0_attn_W_in"], np.float32)
    conv_w = np.asarray(ins["L0_attn_conv_w"], np.float32)
    W_x = np.asarray(ins["L0_attn_W_x"], np.float32)
    W_dt = np.asarray(ins["L0_attn_W_dt"], np.float32)
    W_out = np.asarray(ins["L0_attn_W_out"], np.float32)
    Wg = np.asarray(ins["L0_ffn_Wg"], np.float32)
    We = np.asarray(ins["L0_ffn_W"], np.float32)
    be = np.asarray(ins["L0_ffn_b"], np.float32)

    wf = np.zeros((P, FT), np.float32)

    def put(name, arr):
        o = BOFF[name]
        wf[:, o:o + arr.shape[1]] = arr

    put("eye", np.eye(P, dtype=np.float32))
    r16 = np.zeros((P, 2048), np.float32)
    r16[0:16] = np.kron(np.eye(16, dtype=np.float32), np.ones((1, P), np.float32))
    put("r16", r16)
    r8 = np.zeros((P, 1024), np.float32)
    r8[0:8] = np.kron(np.eye(8, dtype=np.float32), np.ones((1, P), np.float32))
    put("r8", r8)
    bep = np.zeros((P, 1024), np.float32)
    bep[0:8] = be
    put("be", bep)
    wp = []
    for mg in [0, 4, 1, 5, 2, 6, 3, 7]:
        for k in range(KD):
            wp.append(W_in[k * P:(k + 1) * P, mg * 512:(mg + 1) * 512])
    put("win", np.concatenate(wp, axis=1))
    cp = []
    for g in range(KI):
        for kk in range(4):
            cp.append(np.diag(conv_w[g * P:(g + 1) * P, 0, kk]).astype(np.float32))
    put("conv", np.concatenate(cp, axis=1))
    xp = [W_x[k * P:(k + 1) * P, DS:2 * DS] for k in range(KI)]
    put("wx", np.concatenate(xp, axis=1))
    dp = []
    for g in range(KI):
        for k in range(KI):
            dp.append(W_dt[k * P:(k + 1) * P, g * P:(g + 1) * P])
    put("wdt", np.concatenate(dp, axis=1))
    op_ = []
    for mp in range(4):
        for k in range(KI):
            op_.append(W_out[k * P:(k + 1) * P, mp * 256:(mp + 1) * 256])
    put("wout", np.concatenate(op_, axis=1))
    gp = [np.ascontiguousarray(Wg[:, k * P:(k + 1) * P].T) for k in range(KD)]
    put("wg", np.concatenate(gp, axis=1))
    ep = []
    for e in range(E):
        WeT = We[e].T
        for k in range(KD):
            ep.append(np.ascontiguousarray(WeT[k * P:(k + 1) * P, :]))
    put("we", np.concatenate(ep, axis=1))

    b_in = np.asarray(ins["L0_attn_b_in"], np.float32)
    b_dt = np.asarray(ins["L0_attn_b_dt"], np.float32)
    b_out = np.asarray(ins["L0_attn_b_out"], np.float32)
    Dv = np.asarray(ins["L0_attn_D"], np.float32)
    conv_b = np.asarray(ins["L0_attn_conv_b"], np.float32)
    A_log = np.asarray(ins["L0_attn_A_log"], np.float32)
    b_x = np.asarray(ins["L0_attn_b_x"], np.float32)
    bg = np.asarray(ins["L0_ffn_bg"], np.float32)

    bias = {
        "binr": np.ascontiguousarray(b_in.reshape(32, P).T),
        "bdtr": np.ascontiguousarray(b_dt.reshape(16, P).T),
        "boutr": np.ascontiguousarray(b_out.reshape(8, P).T),
        "dr": np.ascontiguousarray(Dv.reshape(16, P).T),
        "convbr": np.ascontiguousarray(conv_b.reshape(16, P).T),
        "alogr": np.ascontiguousarray(
            A_log.reshape(16, P, 16).transpose(1, 0, 2).reshape(P, 256)),
        "bx": np.zeros((P, 1), np.float32),
        "bgrow": np.ascontiguousarray(bg.reshape(1, 8)),
    }
    bias["bx"][0:16, 0] = b_x[DS:2 * DS]
    return wf, bias


def _host_x(xb):
    return np.ascontiguousarray(
        xb.T.reshape(KD, P, T).transpose(1, 0, 2)).astype(np.float32)


def _host_out(o):
    return np.ascontiguousarray(
        np.asarray(o).transpose(1, 0, 2).reshape(D, T).T)


def kernel(**inputs):
    x = np.asarray(inputs["L0_attn_x"], np.float32)
    wf, bias = _host_blocks(inputs)
    nc = build()
    in_maps = []
    for b in range(8):
        m = {"x_in": _host_x(x[b]), "wf32": wf}
        for k, v in bias.items():
            m["bias_" + k] = v
        in_maps.append(m)
    res = run_bass_kernel_spmd(nc, in_maps, core_ids=list(range(8)))
    out = np.stack([_host_out(r["out"]) for r in res.results], axis=0)
    return out
